# version 25
# baseline (speedup 1.0000x reference)
import math
from contextlib import ExitStack

import numpy as np
import ml_dtypes

import concourse.bass as bass
import concourse.mybir as mybir
from concourse.bass_utils import run_bass_kernel_spmd

F32 = mybir.dt.float32
BF16 = mybir.dt.bfloat16
AF = mybir.ActivationFunctionType
ALU = mybir.AluOpType
AX = mybir.AxisListType
BFT = ml_dtypes.bfloat16

FULL_CFG = dict(D=4096, T=1024, H=32, DH=128, QL=12, KL=4, E=28, R=8)


def build(cfg):
    D, T, H, DH = cfg["D"], cfg["T"], cfg["H"], cfg["DH"]
    QL, KL, E, R = cfg["QL"], cfg["KL"], cfg["E"], cfg["R"]
    KT = D // 128
    SKT = (2 * T) // 128
    NT = T // 128
    ER = E * R
    N1 = min(128, ER)
    N2 = ER - N1
    CH = [(c0, min(512, T - c0)) for c0 in range(0, T, 512)]
    NCH = len(CH)
    QSC = float(1.0 / math.sqrt(KL))
    EPS = 1e-5

    AR = 4 * T + (512 if 4 * T + 512 <= 4096 else 0)
    CA, CB, CC, CD = 0, T, 2 * T, 3 * T
    KG = 2 * T
    A2C = 2 * T
    FC = 3 * T
    PB = [0, T]
    NPB = 4
    if SKT >= 16:
        QKT, BKT, UKT, DKT = 6, 8, 10, 5
    else:
        QKT, BKT, UKT, DKT = 3, 4, 5, 2

    nc = bass.Bass("TRN2", target_bir_lowering=False)

    dx = nc.dram_tensor("xT_own", [D, T], F32, kind="ExternalInput")
    dxo = nc.dram_tensor("xT_oth", [D, T], F32, kind="ExternalInput")
    dwq = nc.dram_tensor("wq", [D, QL], F32, kind="ExternalInput")
    dwkv = nc.dram_tensor("wkv", [D, KL], F32, kind="ExternalInput")
    dwqk = nc.dram_tensor("wqk", [QL, H * KL], F32, kind="ExternalInput")
    dwvu = nc.dram_tensor("wvu", [KL, D], F32, kind="ExternalInput")
    dbqd = nc.dram_tensor("bqd", [QL, 1], F32, kind="ExternalInput")
    dbkvd = nc.dram_tensor("bkvd", [KL, 1], F32, kind="ExternalInput")
    dbqk = nc.dram_tensor("bqk", [KL, H], F32, kind="ExternalInput")
    dbvu = nc.dram_tensor("bvuf", [1, D], F32, kind="ExternalInput")
    dbo = nc.dram_tensor("bo", [128, KT], F32, kind="ExternalInput")
    dwo = nc.dram_tensor("wo", [D, D], F32, kind="ExternalInput")
    dbwt = nc.dram_tensor("bwt", [D, D], F32, kind="ExternalInput")
    dgwt = nc.dram_tensor("gwt", [D, E], F32, kind="ExternalInput")
    dlat = nc.dram_tensor("lat", [D, ER], F32, kind="ExternalInput")
    dlbt = nc.dram_tensor("lbt", [ER, D], F32, kind="ExternalInput")
    de1 = nc.dram_tensor("e1", [E, N1], F32, kind="ExternalInput")
    de2 = nc.dram_tensor("e2", [E, N2], F32, kind="ExternalInput")
    did = nc.dram_tensor("idm", [128, 128], F32, kind="ExternalInput")
    dout = nc.dram_tensor("outT", [D, T], F32, kind="ExternalOutput")
    dgs = nc.dram_tensor("gscr", [D, T], mybir.dt.float16)

    es = ExitStack()
    with es:
        sbw = lambda st_, name, shape, dt: st_.enter_context(nc.sbuf_tensor(name, shape, dt))
        sbuf = lambda name, shape, dt: sbw(es, name, shape, dt)
        xnT = sbuf("xnT", [128, KT * T], BF16)
        bt1 = sbuf("bt1", [128, D], BF16)
        bt2 = sbuf("bt2", [N2, D], BF16)
        hg1 = sbuf("hg1", [128, T], BF16)
        hg2 = sbuf("hg2", [N2, T], BF16)
        bqk_f = sbuf("bqk_f", [KL, H], F32)

        bo_f = sbuf("bo_f", [128, KT], F32)
        bqd_f = sbuf("bqd_f", [QL, 1], F32)
        bkvd_f = sbuf("bkvd_f", [KL, 1], F32)
        ones_b = sbuf("ones_b", [128, 1], BF16)
        ones8 = sbuf("ones8", [128, 8], BF16)
        ones_f = sbuf("ones_f", [1, 128], F32)
        es_mid = ExitStack()
        cq_b = sbw(es_mid, "cq_b", [QL, T], BF16)
        ckv_b = sbw(es_mid, "ckv_b", [KL, 2 * T], BF16)
        kaug = sbw(es_mid, "kaug", [128, 8 * SKT], BF16)
        wvu8 = sbw(es_mid, "wvu8", [8, D], BF16)
        wqk_b = sbw(es_mid, "wqk_b", [QL, H * KL], BF16)
        rs_bf = sbw(es_mid, "rs_bf", [128, T], BF16)
        es1 = ExitStack()
        id_f = sbw(es1, "id_f", [128, 128], F32)
        ckv_f = sbw(es1, "ckv_f", [KL, 2 * T], F32)
        rs_f = sbw(es1, "rs_f", [128, T], F32)
        mrs_f = sbw(es1, "mrs_f", [128, T], F32)
        rso_f = sbw(es1, "rso_f", [128, T], F32)
        srow = sbw(es1, "srow", [1, 3 * T], F32)
        xst = sbw(es1, "xst", [128, 8 * T], BF16)
        xsq = sbw(es1, "xsq", [128, 2 * T], BF16)
        tmpa = sbw(es1, "tmpa", [128, T], F32)
        mrsb = sbw(es1, "mrsb", [1, T], BF16)
        cs_kv = sbw(es1, "cs_kv", [1, KL], BF16)
        wq_b = sbw(es1, "wq_b", [128, KT * QL], BF16)
        wkv_b = sbw(es1, "wkv_b", [128, KT * KL], BF16)
        gwt_b = sbw(es1, "gwt_b", [128, KT * E], BF16)
        at1 = sbw(es1, "at1", [128, KT * N1], BF16)
        at2 = sbw(es1, "at2", [128, KT * N2], BF16)
        e1_b = sbw(es1, "e1_b", [E, N1], BF16)
        e2_b = sbw(es1, "e2_b", [E, N2], BF16)
        logits_f = sbw(es1, "logits_f", [E, T], F32)
        gt_b = sbw(es1, "gt_b", [E, T], BF16)
        lt_f = sbw(es1, "lt_f", [128, NT * E], F32)
        mk1 = sbw(es1, "mk1", [128, NT * E], F32)
        mk2 = sbw(es1, "mk2", [128, NT * E], F32)
        msk = sbw(es1, "msk", [128, NT * E], F32)
        gtk = sbw(es1, "gtk", [128, NT * E], F32)
        mrow = sbw(es1, "mrow", [128, 6 * NT], F32)
        gex1 = sbw(es1, "gex1", [128, T], BF16)
        gex2 = sbw(es1, "gex2", [N2, T], BF16)

        arena = es.enter_context(nc.psum_tensor("arena", [128, AR], F32))

        sems = {s: es.enter_context(nc.semaphore(s)) for s in
                ("dW", "dY", "dZ", "sT", "sV", "sS", "dF", "dG", "dH",
                 "dX0", "dX1", "dX2", "dX3", "dX4", "dX5", "dX6", "dX7",
                 "dL0", "dL1", "dO0", "dO1", "dP0", "dP1")}
        cnt = {s: 0 for s in sems}
        EV = {}
        block = es.enter_context(nc.Block())

        def reg(key, sem):
            EV[key] = (sem, cnt[sem])

        def I(instr, sem, key=None, n=1):
            cnt[sem] += n
            instr.then_inc(sems[sem], n)
            if key is not None:
                reg(key, sem)
            return instr

        def Ip(instr, sem, key):
            cnt[sem] += 1
            instr.then_inc(sems[sem], 1)
            assert EV[key] == (sem, cnt[sem]), (key, EV[key], (sem, cnt[sem]))

        def W(eng, *keys):
            best = {}
            for k in keys:
                if k is None or k not in EV:
                    continue
                s, v = EV[k]
                if v > 0:
                    best[s] = max(best.get(s, 0), v)
            for s, v in best.items():
                eng.wait_ge(sems[s], v)

        def vself(v):
            if cnt["sV"] > 0:
                v.wait_ge(sems["sV"], cnt["sV"])

        def sself(s):
            if cnt["sS"] > 0:
                s.wait_ge(sems["sS"], cnt["sS"])

        @block.vector
        def _(v):
            I(v.memset(ones_b[:, :], 1.0), "sV")
            I(v.memset(ones8[:, :], 1.0), "sV")
            I(v.memset(ones_f[:, :], 1.0), "sV")
            I(v.memset(wvu8[:, :], 0.0), "sV")
            I(v.memset(kaug[:, :], 0.0), "sV", ("init",))

        @block.sync
        def _(sy):
            for dsrc, ddst in ((did, id_f), (dbqd, bqd_f), (dbkvd, bkvd_f),
                               (dbqk, bqk_f), (dbo, bo_f)):
                I(sy.dma_start(out=ddst[:, :], in_=dsrc.ap()), "dY", n=16)
            reg(("smallY",), "dY")

        gsz = min(8, KT)
        ng = KT // gsz

        @block.gpsimd
        def _(g):
            for kt in range(min(8, KT)):
                I(g.dma_start(out=xnT[:, kt * T:(kt + 1) * T],
                              in_=dx.ap()[kt * 128:(kt + 1) * 128, :]),
                  "dX%d" % kt, ("so", "x", kt), n=16)

        for kt in range(KT):
            EV[("so", "sq", kt)] = ("sS", kt + 1)
        EV[("so", "rstd")] = ("sS", KT + 2)
        for kt in range(KT):
            EV[("st", "sq", kt)] = ("sS", KT + 2 + kt + 1)
        EV[("st", "rstd")] = ("sS", 2 * KT + 4)
        EV[("gexp",)] = ("sS", 2 * KT + 5)

        slot_ev = {}

        def stats_pass(bk, dsrc, ca, cbk, r2, first, extra=None, inject=None):
            own = bk == "so"
            for kt in range(KT):
                sl = kt % 8
                buf = xnT if own else xst
                off = (kt if own else sl) * T
                if not (own and kt < 8):
                    @block.gpsimd
                    def _(g, kt=kt, sl=sl, buf=buf, off=off):
                        if own:
                            W(g, (bk, "x", kt - 8))
                        elif kt < 8:
                            W(g, ("so", "x", KT - 8 + kt),
                              ("so", "x", kt) if kt < KT else None)
                        else:
                            W(g, (bk, "x", kt - 8), (bk, "sT", kt - 8),
                              (bk, "sq", kt - 8))
                        I(g.dma_start(out=buf[:, off:off + T],
                                      in_=dsrc.ap()[kt * 128:(kt + 1) * 128, :]),
                          "dX%d" % (kt % 8), (bk, "x", kt), n=16)
                        if inject and kt in inject:
                            inject[kt](g)

                @block.tensor
                def _(t, kt=kt, buf=buf, off=off):
                    W(t, (bk, "x", kt), (bk, "sq", kt))
                    if kt == 0:
                        W(t, *first)
                    if extra is not None:
                        extra(t, kt, off)
                    for c0, cw in CH:
                        t.matmul(arena[0:8, ca + c0:ca + c0 + cw], ones8[:, :],
                                 buf[:, off + c0:off + c0 + cw],
                                 start=(kt == 0), stop=(kt == KT - 1),
                                 skip_group_check=True)
                    mm = None
                    for c0, cw in CH:
                        mm = t.matmul(arena[r2:r2 + 8, cbk + c0:cbk + c0 + cw],
                                      ones8[:, :],
                                      xsq[:, (kt % 2) * T + c0:(kt % 2) * T + c0 + cw],
                                      start=(kt == 0), stop=(kt == KT - 1),
                                      skip_group_check=True)
                    I(mm, "sT", (bk, "sT", kt))

        def _w_wq(g):
            I(g.dma_start(out=wq_b.ap().rearrange("p (kt q) -> p kt q", kt=KT),
                          in_=dwq.ap().rearrange("(kt p) q -> p kt q", p=128)), "dW", n=16)

        def _w_wkv(g):
            I(g.dma_start(out=wkv_b.ap().rearrange("p (kt q) -> p kt q", kt=KT),
                          in_=dwkv.ap().rearrange("(kt p) q -> p kt q", p=128)), "dW", n=16)

        def _w_gwt(i):
            def f(g):
                k0, k1 = i * gsz, (i + 1) * gsz
                I(g.dma_start(
                    out=gwt_b[:, k0 * E:k1 * E].rearrange("p (kt e) -> p kt e", kt=gsz),
                    in_=dgwt.ap()[k0 * 128:k1 * 128, :].rearrange("(kt p) e -> p kt e", p=128)),
                  "dW", n=16)
            return f

        def _w_at1(i):
            def f(g):
                k0, k1 = i * gsz, (i + 1) * gsz
                I(g.dma_start(
                    out=at1[:, k0 * N1:k1 * N1].rearrange("p (kt e) -> p kt e", kt=gsz),
                    in_=dlat.ap()[k0 * 128:k1 * 128, 0:N1].rearrange("(kt p) e -> p kt e", p=128)),
                  "dW", n=16)
            return f

        def _w_at2(i, last=False):
            def f(g):
                k0, k1 = i * gsz, (i + 1) * gsz
                I(g.dma_start(
                    out=at2[:, k0 * N2:k1 * N2].rearrange("p (kt e) -> p kt e", kt=gsz),
                    in_=dlat.ap()[k0 * 128:k1 * 128, N1:ER].rearrange("(kt p) e -> p kt e", p=128)),
                  "dW", n=16)
                if last:
                    reg(("wA",), "dW")
            return f

        wparts = [_w_wq, _w_wkv] + [_w_gwt(i) for i in range(ng)] \
            + [_w_at1(i) for i in range(ng)] \
            + [_w_at2(i, last=(i == ng - 1)) for i in range(ng)]
        winj = {}
        if KT >= 16:
            step = max(1, (KT - 10) // len(wparts))
            for n_, fn in enumerate(wparts):
                winj[8 + n_ * step] = fn
            assert 8 + (len(wparts) - 1) * step < KT
        else:
            @block.gpsimd
            def _(g):
                for fn in wparts:
                    fn(g)

        stats_pass("so", dx, CA, CB, 0, (("init",),), inject=winj)


        @block.tensor
        def _(t):
            W(t, ("wA",))
            mm = None
            for kt in range(KT):
                mm = t.matmul(arena[0:8, CD:CD + KL], ones8[:, :],
                              wkv_b[:, kt * KL:(kt + 1) * KL],
                              start=(kt == 0), stop=(kt == KT - 1),
                              skip_group_check=True)
            I(mm, "sT", ("cskvT",))

        @block.vector
        def _(v):
            vself(v)
            W(v, ("cskvT",))
            I(v.tensor_copy(cs_kv[0:1, 0:KL], arena[0:1, CD:CD + KL]), "sV", ("cskv",))


        @block.vector
        def _(v):
            vself(v)
            W(v, ("so", "sT", KT - 1))
            I(v.tensor_scalar(srow[0:1, 0:T], arena[0:1, CA:CA + T],
                              1.0 / D, None, ALU.mult), "sV")
            vself(v)
            I(v.tensor_scalar(srow[0:1, T:2 * T], arena[0:1, CB:CB + T],
                              1.0 / D, EPS, ALU.mult, ALU.add), "sV")
            vself(v)
            I(v.tensor_tensor(srow[0:1, 2 * T:3 * T], srow[0:1, 0:T],
                              srow[0:1, 0:T], ALU.mult), "sV")
            vself(v)
            I(v.tensor_tensor(srow[0:1, 2 * T:3 * T], srow[0:1, T:2 * T],
                              srow[0:1, 2 * T:3 * T], ALU.subtract), "sV", ("so", "var"))

        @block.tensor
        def _(t):
            W(t, ("so", "rstd"))
            mm = None
            for c0, cw in CH:
                mm = t.matmul(arena[0:128, CA + c0:CA + c0 + cw], ones_f[:, :],
                              srow[0:1, 2 * T + c0:2 * T + c0 + cw],
                              start=True, stop=True, skip_group_check=True)
            I(mm, "sT", ("so", "bc1"))

        @block.vector
        def _(v):
            vself(v)
            W(v, ("so", "bc1"))
            I(v.tensor_copy(rs_f[:, :], arena[0:128, CA:CA + T]), "sV", ("so", "rsc"))
            vself(v)
            I(v.tensor_copy(rs_bf[:, :], arena[0:128, CA:CA + T]), "sV", ("rsbf",))

        @block.tensor
        def _(t):
            W(t, ("so", "rsc"), ("rsbf",))
            mm = None
            for c0, cw in CH:
                mm = t.matmul(arena[0:128, CA + c0:CA + c0 + cw], ones_f[:, :],
                              srow[0:1, c0:c0 + cw],
                              start=True, stop=True, skip_group_check=True)
            I(mm, "sT", ("so", "bc2"))

        @block.vector
        def _(v):
            vself(v)
            W(v, ("so", "bc2"))
            I(v.tensor_copy(mrs_f[:, :], arena[0:128, CA:CA + T]), "sV", ("so", "meanc"))

        for kt in range(KT):
            @block.vector
            def _(v, kt=kt):
                vself(v)
                W(v, ("so", "meanc"), ("so", "sT", kt), ("so", "sq", kt))
                I(v.scalar_tensor_tensor(xnT[:, kt * T:(kt + 1) * T],
                                         xnT[:, kt * T:(kt + 1) * T], 1.0,
                                         mrs_f[:, :], ALU.mult, ALU.subtract),
                  "sV", ("b", "xn", kt))

            @block.tensor
            def _(t, kt=kt):
                W(t, ("b", "xn", kt))
                if kt == 0:
                    W(t, ("cskv",))
                xn = xnT[:, kt * T:(kt + 1) * T]
                for c0, cw in CH:
                    t.matmul(arena[0:QL, CC + c0:CC + c0 + cw],
                             wq_b[:, kt * QL:(kt + 1) * QL], xn[:, c0:c0 + cw],
                             start=(kt == 0), stop=(kt == KT - 1), skip_group_check=True)
                for c0, cw in CH:
                    t.matmul(arena[32:32 + KL, CC + c0:CC + c0 + cw],
                             wkv_b[:, kt * KL:(kt + 1) * KL], xn[:, c0:c0 + cw],
                             start=(kt == 0), stop=(kt == KT - 1), skip_group_check=True)
                for c0, cw in CH:
                    t.matmul(arena[0:E, CA + c0:CA + c0 + cw],
                             gwt_b[:, kt * E:(kt + 1) * E], xn[:, c0:c0 + cw],
                             start=(kt == 0), stop=(kt == KT - 1), skip_group_check=True)
                for c0, cw in CH:
                    t.matmul(arena[0:N1, CB + c0:CB + c0 + cw],
                             at1[:, kt * N1:(kt + 1) * N1], xn[:, c0:c0 + cw],
                             start=(kt == 0), stop=(kt == KT - 1), skip_group_check=True)
                mm = None
                for c0, cw in CH:
                    mm = t.matmul(arena[0:N2, CD + c0:CD + c0 + cw],
                                  at2[:, kt * N2:(kt + 1) * N2], xn[:, c0:c0 + cw],
                                  start=(kt == 0), stop=(kt == KT - 1), skip_group_check=True)
                I(mm, "sT", ("b", "sT", kt))


        @block.gpsimd
        def _(g):
            I(g.dma_start(out=wqk_b[:, :], in_=dwqk.ap()), "dZ", n=16)
            W(g, ("init",))
            I(g.dma_start(out=wvu8[2:6, :], in_=dwvu.ap()), "dZ", n=16)
            I(g.dma_start(out=wvu8[0:1, :], in_=dbvu.ap()), "dZ", n=16)
            I(g.dma_start(out=e1_b[:, :], in_=de1.ap()), "dZ", n=16)
            I(g.dma_start(out=e2_b[:, :], in_=de2.ap()), "dZ", n=16)
            I(g.dma_start(out=bt1[:, :], in_=dlbt.ap()[0:128, :]), "dZ", n=16)
            I(g.dma_start(out=bt2[:, :], in_=dlbt.ap()[128:ER, :]), "dZ", n=16)
            reg(("lbtW",), "dZ")

        @block.vector
        def _(v):
            vself(v)
            W(v, ("b", "sT", KT - 1), ("smallY",))
            I(v.tensor_tensor(tmpa[0:QL, 0:T], arena[0:QL, CC:CC + T], rs_f[0:QL, :],
                              ALU.mult), "sV")
            vself(v)
            I(v.tensor_scalar(cq_b[:, :], tmpa[0:QL, 0:T], bqd_f[:, :], None,
                              ALU.add), "sV", ("cqev",))
            vself(v)
            I(v.tensor_tensor(ckv_f[0:KL, 0:T], arena[32:32 + KL, CC:CC + T],
                              rs_f[0:KL, :], ALU.mult), "sV")
            vself(v)
            I(v.tensor_scalar(ckv_f[0:KL, 0:T], ckv_f[0:KL, 0:T], bkvd_f[:, :], None,
                              ALU.add), "sV")
            vself(v)
            I(v.tensor_copy(ckv_b[:, 0:T], ckv_f[0:KL, 0:T]), "sV", ("ckev0",))
            vself(v)
            I(v.tensor_tensor(logits_f[:, :], arena[0:E, CA:CA + T], rs_f[0:E, :],
                              ALU.mult), "sV", ("lgev",))

        @block.tensor
        def _(t):
            W(t, ("lgev",))
            mm = None
            for tt in range(NT):
                mm = t.transpose(arena[0:128, CA + tt * E:CA + (tt + 1) * E],
                                 logits_f[0:E, tt * 128:(tt + 1) * 128], id_f[0:E, 0:E])
            I(mm, "sT", ("ltT",))

        @block.vector
        def _(v):
            vself(v)
            W(v, ("ltT",))
            I(v.tensor_copy(lt_f[:, :], arena[0:128, CA:CA + NT * E]), "sV", ("ltc",))
            for tt in range(NT):
                sl = slice(tt * E, (tt + 1) * E)
                vself(v)
                I(v.reduce_max(mrow[:, tt:tt + 1], lt_f[:, sl], axis=AX.X), "sV")
                vself(v)
                I(v.tensor_scalar(mk1[:, sl], lt_f[:, sl], mrow[:, tt:tt + 1], None,
                                  ALU.is_equal), "sV")
                vself(v)
                I(v.scalar_tensor_tensor(msk[:, sl], mk1[:, sl], -1e30, lt_f[:, sl],
                                         ALU.mult, ALU.add), "sV")
                vself(v)
                I(v.reduce_max(mrow[:, NT + tt:NT + tt + 1], msk[:, sl], axis=AX.X), "sV")
                vself(v)
                I(v.tensor_scalar(mk2[:, sl], msk[:, sl], mrow[:, NT + tt:NT + tt + 1], None,
                                  ALU.is_equal), "sV")
            vself(v)
            I(v.tensor_tensor(mrow[:, 4 * NT:5 * NT], mrow[:, 0:NT], mrow[:, NT:2 * NT],
                              ALU.subtract), "sV", ("gdel",))

        def ckv_oth_extra(t, kt, off):
            for c0, cw in CH:
                t.matmul(arena[32:32 + KL, CC + c0:CC + c0 + cw],
                         wkv_b[:, kt * KL:(kt + 1) * KL],
                         xst[:, off + c0:off + c0 + cw],
                         start=(kt == 0), stop=False, skip_group_check=True)

        stats_pass("st", dxo, CC, CC, 64, (("cqev",), ("ckev0",)), extra=ckv_oth_extra)

        @block.vector
        def _(v):
            vself(v)
            W(v, ("st", "sT", KT - 1), ("so", "bc2"))
            I(v.tensor_scalar(srow[0:1, 0:T], arena[0:1, CC:CC + T],
                              1.0 / D, None, ALU.mult), "sV")
            vself(v)
            I(v.tensor_scalar(srow[0:1, T:2 * T], arena[64:65, CC:CC + T],
                              1.0 / D, EPS, ALU.mult, ALU.add), "sV")
            vself(v)
            I(v.tensor_tensor(srow[0:1, 2 * T:3 * T], srow[0:1, 0:T],
                              srow[0:1, 0:T], ALU.mult), "sV")
            vself(v)
            I(v.tensor_tensor(srow[0:1, 2 * T:3 * T], srow[0:1, T:2 * T],
                              srow[0:1, 2 * T:3 * T], ALU.subtract), "sV", ("st", "var"))

        @block.vector
        def _(v):
            vself(v)
            W(v, ("st", "rstd"))
            I(v.scalar_tensor_tensor(srow[0:1, T:2 * T], srow[0:1, 0:T],
                                     -1.0, srow[0:1, 2 * T:3 * T], ALU.mult, ALU.mult),
              "sV")
            vself(v)
            I(v.tensor_copy(mrsb[0:1, 0:T], srow[0:1, T:2 * T]), "sV", ("mrsb",))

        @block.tensor
        def _(t):
            W(t, ("mrsb",))
            mm = None
            for ci, (c0, cw) in enumerate(CH):
                mm = t.matmul(arena[32:32 + KL, CC + c0:CC + c0 + cw],
                              cs_kv[0:1, 0:KL], mrsb[0:1, c0:c0 + cw],
                              start=False, stop=(ci == NCH - 1), skip_group_check=True)
            I(mm, "sT", ("ckoT",))

        @block.vector
        def _(v):
            vself(v)
            W(v, ("gexp",))
            I(v.tensor_scalar(mrow[:, 4 * NT:5 * NT], mrow[:, 5 * NT:6 * NT], 1.0, None,
                              ALU.add), "sV")
            vself(v)
            I(v.reciprocal(mrow[:, 2 * NT:3 * NT], mrow[:, 4 * NT:5 * NT]), "sV")
            vself(v)
            I(v.tensor_scalar(mrow[:, 3 * NT:4 * NT], mrow[:, 2 * NT:3 * NT], -2.0, 2.0,
                              ALU.mult, ALU.add), "sV")
            for tt in range(NT):
                sl = slice(tt * E, (tt + 1) * E)
                vself(v)
                I(v.tensor_scalar(msk[:, sl], mk1[:, sl], mrow[:, 2 * NT + tt:2 * NT + tt + 1],
                                  2.0, ALU.mult, ALU.mult), "sV")
                vself(v)
                I(v.scalar_tensor_tensor(gtk[:, sl], mk2[:, sl],
                                         mrow[:, 3 * NT + tt:3 * NT + tt + 1], msk[:, sl],
                                         ALU.mult, ALU.add), "sV")
            reg(("gtk",), "sV")

        @block.tensor
        def _(t):
            W(t, ("gtk",), ("ltc",))
            mm = None
            for tt in range(NT):
                mm = t.transpose(arena[0:E, CA + tt * 128:CA + (tt + 1) * 128],
                                 gtk[0:128, tt * E:(tt + 1) * E], id_f[:, :])
            I(mm, "sT", ("gtT",))

        @block.vector
        def _(v):
            vself(v)
            W(v, ("gtT",))
            I(v.tensor_copy(gt_b[:, :], arena[0:E, CA:CA + T]), "sV", ("gtc",))

        @block.tensor
        def _(t):
            W(t, ("gtc",), ("lbtW",))
            mm = None
            for c0, cw in CH:
                mm = t.matmul(arena[0:N1, CA + c0:CA + c0 + cw], e1_b[:, :],
                              gt_b[0:E, c0:c0 + cw], start=True, stop=True,
                              skip_group_check=True)
            I(mm, "sT", ("gx1T",))

        @block.vector
        def _(v):
            vself(v)
            W(v, ("gx1T",))
            I(v.tensor_copy(gex1[:, :], arena[0:N1, CA:CA + T]), "sV", ("gx1c",))
            vself(v)
            I(v.tensor_tensor(hg1[:, :], arena[0:N1, CB:CB + T], gex1[:, :], ALU.mult),
              "sV", ("hg1",))

        @block.tensor
        def _(t):
            W(t, ("hg1",))
            mm = None
            for c0, cw in CH:
                mm = t.matmul(arena[0:N2, CB + c0:CB + c0 + cw], e2_b[:, :],
                              gt_b[0:E, c0:c0 + cw], start=True, stop=True,
                              skip_group_check=True)
            I(mm, "sT", ("gx2T",))

        @block.vector
        def _(v):
            vself(v)
            W(v, ("gx2T",))
            I(v.tensor_copy(gex2[:, :], arena[0:N2, CB:CB + T]), "sV", ("gx2c",))
            vself(v)
            I(v.tensor_tensor(hg2[:, :], arena[0:N2, CD:CD + T], gex2[:, :], ALU.mult),
              "sV", ("hg2",))

        @block.tensor
        def _(t):
            W(t, ("st", "rstd"))
            mm = None
            for c0, cw in CH:
                mm = t.matmul(arena[64:64 + KL, CC + c0:CC + c0 + cw],
                              ones_f[0:1, 0:KL],
                              srow[0:1, 2 * T + c0:2 * T + c0 + cw],
                              start=True, stop=True, skip_group_check=True)
            I(mm, "sT", ("st", "bc1"))

        @block.vector
        def _(v):
            vself(v)
            W(v, ("st", "bc1"))
            I(v.tensor_copy(rso_f[0:KL, :], arena[64:64 + KL, CC:CC + T]),
              "sV", ("st", "rsc"))
            vself(v)
            W(v, ("ckoT",))
            I(v.tensor_tensor(ckv_f[0:KL, T:2 * T], arena[32:32 + KL, CC:CC + T],
                              rso_f[0:KL, :], ALU.mult), "sV")
            vself(v)
            I(v.tensor_scalar(ckv_f[0:KL, T:2 * T], ckv_f[0:KL, T:2 * T],
                              bkvd_f[:, :], None, ALU.add), "sV")
            vself(v)
            I(v.tensor_copy(ckv_b[:, T:2 * T], ckv_f[0:KL, T:2 * T]), "sV", ("ckev1",))

        @block.scalar
        def _(s):
            for bk in ("so", "st"):
                for kt in range(KT):
                    src_ = (xnT[:, kt * T:(kt + 1) * T] if bk == "so"
                            else xst[:, (kt % 8) * T:(kt % 8 + 1) * T])
                    war = (bk, "sT", kt - 2)
                    if kt < 2 and bk == "st":
                        war = ("so", "sT", KT - 2 + kt)
                    sself(s)
                    W(s, (bk, "x", kt), war)
                    Ip(s.activation(xsq[:, (kt % 2) * T:(kt % 2 + 1) * T],
                                    src_, AF.Square),
                       "sS", (bk, "sq", kt))
                sself(s)
                W(s, (bk, "var"))
                I(s.activation(srow[0:1, T:2 * T], srow[0:1, 2 * T:3 * T], AF.Ln), "sS")
                sself(s)
                Ip(s.activation(srow[0:1, 2 * T:3 * T], srow[0:1, T:2 * T], AF.Exp,
                                scale=-0.5), "sS", (bk, "rstd"))
            sself(s)
            W(s, ("gdel",))
            Ip(s.activation(mrow[:, 5 * NT:6 * NT], mrow[:, 4 * NT:5 * NT], AF.Exp,
                            scale=-1.0), "sS", ("gexp",))

        @block.tensor
        def _(t):
            W(t, ("hg2",), ("ckev1",))
            mm = None
            for kt in range(SKT):
                mm = t.transpose(arena[0:128, KG + 4 * kt:KG + 4 * kt + 4],
                                 ckv_f[0:KL, kt * 128:(kt + 1) * 128], id_f[0:KL, 0:KL])
            I(mm, "sT", ("kgT",))

        @block.vector
        def _(v):
            vself(v)
            W(v, ("kgT",))
            I(v.tensor_copy(
                kaug.ap().rearrange("p (g c) -> p g c", g=SKT)[:, :, 2:6],
                arena[0:128, KG:KG + 4 * SKT].rearrange("p (g c) -> p g c", g=SKT)),
              "sV", ("kgc",))
            vself(v)
            I(v.memset(kaug.ap().rearrange("p (g c) -> p g c", g=SKT)[:, :, 0:1], 1.0),
              "sV", ("kgo",))

        es1.close()
        es_ao = ExitStack()
        aoT = sbw(es_ao, "aoT", [128, H * T], BF16)
        es_attn = ExitStack()
        pbuf = sbw(es_attn, "pbuf", [128, NPB * T], BF16)
        qb0 = sbw(es_attn, "qb0", [KL, T], BF16)
        qb1 = sbw(es_attn, "qb1", [KL, T], BF16)
        a2b0 = sbw(es_attn, "a2b0", [8, T], BF16)
        a2b1 = sbw(es_attn, "a2b1", [8, T], BF16)
        dinvb = sbw(es_attn, "dinvb", [128, T], F32)
        dtmp = dinvb
        dinv_bf = sbw(es_attn, "dinv_bf", [1, T], BF16)
        onesb = sbw(es_attn, "onesb", [1, 128], BF16)
        fsb = sbw(es_attn, "fsb", [128, T], mybir.dt.float16)
        bw_bf = sbw(es_attn, "bw_bf", [128, 2 * D], BF16)
        qbs = [qb0, qb1]
        a2bs = [a2b0, a2b1]

        assert H == KT
        pair_free = {0: [("gx1c",)] * NCH, 1: [("gx2c",)] * NCH}

        spair = {}
        qloc = {}
        bcdloc = {}
        uploc = {}

        def a2block(t, hh, j, explicit_exp=False):
            if explicit_exp:
                W(t, ("exp", hh, j))
            u = hh * SKT + j
            mm = None
            for c, (c0, cw) in enumerate(CH):
                if j == 0:
                    if hh > 0:
                        W(t, ("a2s", hh - 1, c))
                    elif c == 0:
                        W(t, ("ckev1",), ("kgo",))
                mm = t.matmul(arena[0:8, A2C + c0:A2C + c0 + cw],
                              kaug[:, 8 * j:8 * j + 8],
                              pbuf[:, (u % NPB) * T + c0:(u % NPB) * T + c0 + cw],
                              start=(j == 0), stop=(j == SKT - 1),
                              skip_group_check=True)
            I(mm, "sT", ("a2", hh, j))

        def emit_bwload(mo):
            @block.gpsimd
            def _(g, mo=mo):
                if mo >= 2:
                    W(g, ("fT", mo - 2))
                else:
                    W(g, ("kgo",))
                for i in range(ng):
                    k0, k1 = i * gsz, (i + 1) * gsz
                    I(g.dma_start(
                        out=bw_bf[:, (mo % 2) * D + k0 * 128:(mo % 2) * D + k1 * 128]
                            .rearrange("p (ki c) -> p ki c", ki=gsz),
                        in_=dbwt.ap()[k0 * 128:k1 * 128, mo * 128:(mo + 1) * 128]
                            .rearrange("(ki p) c -> p ki c", p=128)),
                      "dP%d" % (mo % 2), n=16)
                reg(("wLb", mo), "dP%d" % (mo % 2))

        def ffn_mms(mo):
            n = (KT + 2) * NCH
            i = 0
            for ki in range(KT):
                for c0, cw in CH:
                    yield (bw_bf[:, (mo % 2) * D + ki * 128:(mo % 2) * D + (ki + 1) * 128],
                           xnT[:, ki * T + c0:ki * T + c0 + cw], c0, cw,
                           ki == 0, False)
            for c0, cw in CH:
                yield (bt1[:, mo * 128:(mo + 1) * 128], hg1[:, c0:c0 + cw], c0, cw,
                       False, False)
            for ci, (c0, cw) in enumerate(CH):
                yield (bt2[0:N2, mo * 128:(mo + 1) * 128], hg2[0:N2, c0:c0 + cw], c0, cw,
                       False, ci == NCH - 1)

        def emit_bcd(t, hh, pb_):
            W(t, ("dinv", hh))
            for qc, (c0, cw) in enumerate(CH):
                W(t, pair_free[pb_][qc])
                I(t.matmul(arena[0:128, PB[pb_] + c0:PB[pb_] + c0 + cw], onesb[:, :],
                           dinv_bf[0:1, c0:c0 + cw], start=True, stop=True,
                           skip_group_check=True), "sT", ("bcd", hh, qc))
            bcdloc[hh] = PB[pb_]
            pair_free[pb_] = [("dcp", hh, qc) for qc in range(NCH)]

        def emit_up(t, hh, pu):
            W(t, ("a2s", hh, NCH - 1))
            for qc, (c0, cw) in enumerate(CH):
                W(t, pair_free[pu][qc])
                I(t.matmul(arena[0:128, PB[pu] + c0:PB[pu] + c0 + cw],
                           wvu8[0:8, hh * DH:(hh + 1) * DH],
                           a2bs[hh % 2][0:8, c0:c0 + cw], start=True, stop=True,
                           skip_group_check=True), "sT", ("up", hh, qc))
            uploc[hh] = PB[pu]
            pair_free[pu] = [("uptt", hh, qc) for qc in range(NCH)]

        def emit_vtail(v, hh):
            for qc, (c0, cw) in enumerate(CH):
                vself(v)
                W(v, ("bcd", hh, qc))
                Ip(v.tensor_copy(dinvb[:, c0:c0 + cw],
                                 arena[0:128, bcdloc[hh] + c0:bcdloc[hh] + c0 + cw]),
                   "sV", ("dcp", hh, qc))
            for qc, (c0, cw) in enumerate(CH):
                vself(v)
                W(v, ("up", hh, qc))
                Ip(v.tensor_tensor(aoT[:, hh * T + c0:hh * T + c0 + cw],
                                   arena[0:128, uploc[hh] + c0:uploc[hh] + c0 + cw],
                                   dinvb[:, c0:c0 + cw], ALU.mult), "sV", ("uptt", hh, qc))

        emit_bwload(0)
        emit_bwload(1)

        p0 = 1

        @block.tensor
        def _(t, p0=p0):
            W(t, ("hg2",), ("cqev",), ("lbtW",), *pair_free[p0])
            for ci, (c0, cw) in enumerate(CH):
                I(t.matmul(arena[0:KL, PB[p0] + c0:PB[p0] + c0 + cw], wqk_b[0:QL, 0:KL],
                           cq_b[0:QL, c0:c0 + cw], start=True, stop=True,
                           skip_group_check=True), "sT", ("q", 0, ci))
        qloc[0] = PB[p0]
        pair_free[p0] = [("qev", 0, ci) for ci in range(NCH)]

        @block.vector
        def _(v, p0=p0):
            for ci, (c0, cw) in enumerate(CH):
                vself(v)
                W(v, ("q", 0, ci))
                I(v.tensor_scalar(qbs[0][0:KL, c0:c0 + cw],
                                  arena[0:KL, PB[p0] + c0:PB[p0] + c0 + cw],
                                  bqk_f[:, 0:1], QSC, ALU.add, ALU.mult),
                  "sV", ("qev", 0, ci))
            vself(v)
            I(v.memset(onesb[:, :], 1.0), "sV", ("onesb",))

        for h in range(H):
            base_s = cnt["sS"]
            for kt in range(SKT):
                extra = 2 if (kt > DKT and h > 0) else 0
                EV[("exp", h, kt)] = ("sS", base_s + kt + 1 + extra)
            base_s += SKT + (2 if h > 0 else 0)
            cnt["sS"] = base_s
            if h == H - 1:
                EV[("dinv", h)] = ("sS", base_s + 2)
            else:
                EV[("dinv", h)] = ("sS", base_s + DKT + 3)
            off = cnt["sV"]
            if h > 0:
                for c in range(NCH):
                    EV[("a2s", h - 1, c)] = ("sV", off + c + 1)
                off += NCH
            if h + 1 < H:
                for ci in range(NCH):
                    EV[("qev", h + 1, ci)] = ("sV", off + ci + 1)
                off += NCH
            if h > 0:
                for qc in range(NCH):
                    EV[("dcp", h - 1, qc)] = ("sV", off + qc + 1)
                off += NCH
                for qc in range(NCH):
                    EV[("uptt", h - 1, qc)] = ("sV", off + qc + 1)
                off += NCH
            EV[("fcp", h)] = ("sV", off + 1)

            @block.tensor
            def _(t, h=h):
                fq = list(ffn_mms(h))
                fpos = [0]

                def emit_ffn(n):
                    while n > 0 and fpos[0] < len(fq):
                        lhsT, rhs, c0, cw, st_, sp_ = fq[fpos[0]]
                        if fpos[0] == 0:
                            W(t, ("wLb", h), ("fcp", h - 1))
                        mm = t.matmul(arena[0:128, FC + c0:FC + c0 + cw], lhsT, rhs,
                                      start=st_, stop=sp_, skip_group_check=True)
                        if fpos[0] == len(fq) - 1:
                            I(mm, "sT", ("fT", h))
                        fpos[0] += 1
                        n -= 1

                for kt in range(SKT):
                    if kt < SKT - 2:
                        emit_ffn(-(-(len(fq) - fpos[0]) // (SKT - 2 - kt)))
                    if kt < 3:
                        if h > 0:
                            a2block(t, h - 1, SKT - 3 + kt)
                    else:
                        a2block(t, h, kt - 3)
                    p = kt % 2
                    if kt == 0:
                        W(t, ("qev", h, NCH - 1))
                    spair[(h, kt)] = PB[p]
                    mm = None
                    for c, (c0, cw) in enumerate(CH):
                        W(t, pair_free[p][c])
                        mm = t.matmul(arena[0:128, PB[p] + c0:PB[p] + c0 + cw],
                                      ckv_b[0:KL, kt * 128:(kt + 1) * 128],
                                      qbs[h % 2][0:KL, c0:c0 + cw],
                                      start=True, stop=True, skip_group_check=True)
                    I(mm, "sT", ("sc", h, kt))
                    pair_free[p] = [("exp", h, kt)] * NCH
                    if kt == QKT and h + 1 < H:
                        pq = (kt + 1) % 2
                        for ci, (c0, cw) in enumerate(CH):
                            W(t, pair_free[pq][ci])
                            I(t.matmul(arena[0:KL, PB[pq] + c0:PB[pq] + c0 + cw],
                                       wqk_b[0:QL, (h + 1) * KL:(h + 2) * KL],
                                       cq_b[0:QL, c0:c0 + cw], start=True, stop=True,
                                       skip_group_check=True), "sT", ("q", h + 1, ci))
                        qloc[h + 1] = PB[pq]
                        pair_free[pq] = [("qev", h + 1, ci) for ci in range(NCH)]
                    if kt == BKT and h > 0:
                        emit_bcd(t, h - 1, (kt + 1) % 2)
                    if kt == UKT and h > 0:
                        emit_up(t, h - 1, (kt + 1) % 2)
                emit_ffn(len(fq))

            if h > 0:
                @block.gpsimd
                def _(g, h=h):
                    W(g, ("fcp", h - 1))
                    I(g.dma_start(out=dgs.ap()[(h - 1) * 128:h * 128, :], in_=fsb[:, :]),
                      "dF" if (h - 1) % 2 == 0 else "dH", ("fst", h - 1), n=16)

            @block.vector
            def _(v, h=h):
                if h > 0:
                    for c, (c0, cw) in enumerate(CH):
                        vself(v)
                        W(v, ("a2", h - 1, SKT - 1), ("up", h - 3, NCH - 1),
                          ("dinv", h - 3) if h >= 3 else None)
                        Ip(v.tensor_copy(a2bs[(h - 1) % 2][0:8, c0:c0 + cw],
                                         arena[0:8, A2C + c0:A2C + c0 + cw]),
                           "sV", ("a2s", h - 1, c))
                if h + 1 < H:
                    for ci, (c0, cw) in enumerate(CH):
                        vself(v)
                        W(v, ("q", h + 1, ci))
                        inst = v.tensor_scalar(qbs[(h + 1) % 2][0:KL, c0:c0 + cw],
                                               arena[0:KL, qloc[h + 1] + c0:qloc[h + 1] + c0 + cw],
                                               bqk_f[:, h + 1:h + 2], QSC, ALU.add, ALU.mult)
                        Ip(inst, "sV", ("qev", h + 1, ci))
                if h > 0:
                    emit_vtail(v, h - 1)
                vself(v)
                W(v, ("fT", h), ("fst", h - 1))
                Ip(v.tensor_tensor(fsb[:, :], arena[0:128, FC:FC + T], rs_bf[:, :],
                                   ALU.mult), "sV", ("fcp", h))

            if h + 2 < H:
                emit_bwload(h + 2)

        @block.tensor
        def _(t):
            for j in (SKT - 3, SKT - 2, SKT - 1):
                a2block(t, H - 1, j, explicit_exp=True)
            emit_bcd(t, H - 1, 0)
            emit_up(t, H - 1, 1)

        off = cnt["sV"]
        for c in range(NCH):
            EV[("a2s", H - 1, c)] = ("sV", off + c + 1)
        off += NCH
        for qc in range(NCH):
            EV[("dcp", H - 1, qc)] = ("sV", off + qc + 1)
        off += NCH
        for qc in range(NCH):
            EV[("uptt", H - 1, qc)] = ("sV", off + qc + 1)

        @block.vector
        def _(v):
            for c, (c0, cw) in enumerate(CH):
                vself(v)
                W(v, ("a2", H - 1, SKT - 1), ("up", H - 3, NCH - 1), ("dinv", H - 3))
                Ip(v.tensor_copy(a2bs[(H - 1) % 2][0:8, c0:c0 + cw],
                                 arena[0:8, A2C + c0:A2C + c0 + cw]),
                   "sV", ("a2s", H - 1, c))
            emit_vtail(v, H - 1)

        @block.gpsimd
        def _(g):
            W(g, ("fcp", H - 1))
            I(g.dma_start(out=dgs.ap()[(H - 1) * 128:H * 128, :], in_=fsb[:, :]),
              "dF" if (H - 1) % 2 == 0 else "dH", ("fst", H - 1), n=16)

        cnt["sS"] -= H * SKT + 2 * (H - 1)

        @block.scalar
        def _(s):
            for h in range(H):
                for kt in range(SKT):
                    u = h * SKT + kt
                    war = None
                    if u >= NPB:
                        up_, uk = divmod(u - NPB, SKT)
                        war = ("a2", up_, uk)
                    W(s, ("sc", h, kt), war)
                    sl = u % NPB
                    inst = s.activation(pbuf[:, sl * T:(sl + 1) * T],
                                        arena[0:128, spair[(h, kt)]:spair[(h, kt)] + T],
                                        AF.Exp)
                    Ip(inst, "sS", ("exp", h, kt))
                    if kt == DKT and h > 0:
                        W(s, ("a2s", h - 1, NCH - 1),
                          ("bcd", h - 2, NCH - 1) if h >= 2 else None)
                        I(s.activation(dtmp[0:1, 0:T], a2bs[(h - 1) % 2][0:1, 0:T],
                                       AF.Ln), "sS")
                        sself(s)
                        Ip(s.activation(dinv_bf[0:1, 0:T], dtmp[0:1, 0:T], AF.Exp,
                                        scale=-1.0), "sS", ("dinv", h - 1))
            W(s, ("a2s", H - 1, NCH - 1), ("bcd", H - 2, NCH - 1))
            I(s.activation(dtmp[0:1, 0:T], a2bs[(H - 1) % 2][0:1, 0:T], AF.Ln), "sS")
            sself(s)
            Ip(s.activation(dinv_bf[0:1, 0:T], dtmp[0:1, 0:T], AF.Exp, scale=-1.0),
               "sS", ("dinv", H - 1))

        es_attn.close()
        es3 = ExitStack()
        wo_bf = sbw(es3, "wo_bf", [128, 2 * D], BF16)
        gsl = sbw(es3, "gsl", [128, 2 * T], mybir.dt.float16)
        ost = sbw(es3, "ost", [128, 2 * T], F32)
        g_f = sbw(es3, "g_f", [128, T], F32)

        tb = cnt["sT"]
        vb = cnt["sV"]
        sb_ = cnt["sS"]
        lb = cnt["dL0"]
        gb = cnt["dL1"]
        gB = cnt["dG"]
        ob = {0: cnt["dO0"], 1: cnt["dO1"]}
        for mo in range(KT):
            EV[("woT", mo)] = ("sT", tb + mo + 1)
            EV[("silu", mo)] = ("sS", sb_ + mo + 1)
            EV[("stt", mo)] = ("sV", vb + 2 * mo + 1)
            EV[("res", mo)] = ("sV", vb + 2 * mo + 2)
            EV[("xq", mo)] = ("dL0", lb + 16 * (mo + 1))
            EV[("gld", mo)] = (("dL1", gb + 16 * (mo // 2 + 1)) if mo % 2 == 0
                               else ("dG", gB + 16 * (mo // 2 + 1)))
            EV[("out", mo)] = ("dO%d" % (mo % 2), ob[mo % 2] + 16 * (mo // 2 + 1))

        @block.gpsimd
        def _(g):
            for mo in range(KT):
                if mo >= 2:
                    W(g, ("woT", mo - 2))
                else:
                    W(g, ("uptt", H - 1, NCH - 1), ("fcp", H - 1),
                      ("fst", H - 1), ("fst", H - 2), ("dinv", H - 1))
                for i in range(ng):
                    k0, k1 = i * gsz, (i + 1) * gsz
                    I(g.dma_start(
                        out=wo_bf[:, (mo % 2) * D + k0 * 128:(mo % 2) * D + k1 * 128]
                            .rearrange("p (ki c) -> p ki c", ki=gsz),
                        in_=dwo.ap()[k0 * 128:k1 * 128, mo * 128:(mo + 1) * 128]
                            .rearrange("(ki p) c -> p ki c", p=128)),
                      "dP%d" % (mo % 2), n=16)
                reg(("wL", mo), "dP%d" % (mo % 2))

        for mo in range(KT):
            @block.tensor
            def _(t, mo=mo):
                PW = PB[mo % 2]
                W(t, ("wL", mo), ("stt", mo - 2))
                if mo < 2:
                    W(t, *pair_free[mo % 2])
                mm = None
                for ki in range(KT):
                    for c0, cw in CH:
                        mm = t.matmul(arena[0:128, PW + c0:PW + c0 + cw],
                                      wo_bf[:, (mo % 2) * D + ki * 128:(mo % 2) * D + (ki + 1) * 128],
                                      aoT[:, ki * T + c0:ki * T + c0 + cw],
                                      start=(ki == 0), stop=(ki == KT - 1),
                                      skip_group_check=True)
                Ip(mm, "sT", ("woT", mo))

        @block.scalar
        def _(s):
            for mo in range(KT):
                sself(s)
                W(s, ("gld", mo), ("res", mo - 1))
                if mo == 0:
                    W(s, ("uptt", H - 1, NCH - 1))
                Ip(s.activation(g_f[:, :], gsl[:, (mo % 2) * T:(mo % 2 + 1) * T],
                                AF.Silu), "sS", ("silu", mo))

        @block.vector
        def _(v):
            for mo in range(KT):
                PW = PB[mo % 2]
                vself(v)
                W(v, ("woT", mo), ("silu", mo), ("out", mo - 2))
                Ip(v.scalar_tensor_tensor(ost[:, (mo % 2) * T:(mo % 2 + 1) * T],
                                          arena[0:128, PW:PW + T], bo_f[:, mo:mo + 1],
                                          g_f[:, :], ALU.add, ALU.mult), "sV", ("stt", mo))
                vself(v)
                W(v, ("xq", mo))
                Ip(v.tensor_tensor(ost[:, (mo % 2) * T:(mo % 2 + 1) * T],
                                   ost[:, (mo % 2) * T:(mo % 2 + 1) * T],
                                   g_f[:, :], ALU.add), "sV", ("res", mo))

        @block.sync
        def _(sy):
            def gld(mo):
                if mo >= KT:
                    return
                if mo >= 2:
                    W(sy, ("silu", mo - 2), ("fst", mo))
                else:
                    W(sy, ("fst", mo), ("dinv", H - 1), ("a2", H - 1, SKT - 1),
                      ("fst", H - 1), ("fst", H - 2), ("uptt", H - 1, NCH - 1))
                sgl = "dL1" if mo % 2 == 0 else "dG"
                inst = sy.dma_start(out=gsl[:, (mo % 2) * T:(mo % 2 + 1) * T],
                                    in_=dgs.ap()[mo * 128:(mo + 1) * 128, :])
                cnt[sgl] += 16
                inst.then_inc(sems[sgl], 16)
                assert EV[("gld", mo)] == (sgl, cnt[sgl]), (mo, EV[("gld", mo)], cnt[sgl])

            gld(0)
            gld(1)
            for mo in range(KT):
                W(sy, ("stt", mo))
                inst = sy.dma_start(out=g_f[:, :], in_=dx.ap()[mo * 128:(mo + 1) * 128, :])
                cnt["dL0"] += 16
                inst.then_inc(sems["dL0"], 16)
                assert EV[("xq", mo)] == ("dL0", cnt["dL0"])
                W(sy, ("res", mo))
                inst = sy.dma_start(out=dout.ap()[mo * 128:(mo + 1) * 128, :],
                                    in_=ost[:, (mo % 2) * T:(mo % 2 + 1) * T])
                so = "dO%d" % (mo % 2)
                cnt[so] += 16
                inst.then_inc(sems[so], 16)
                assert EV[("out", mo)] == (so, cnt[so])
                gld(mo + 2)

        @block.sync
        def _(sy):
            for s in ("dY", "dW", "dF", "dG", "dH", "dO0", "dO1", "dL0", "dL1", "dP0", "dP1",
                      "dX0", "dX1", "dX2", "dX3"):
                if cnt[s] > 0:
                    sy.wait_ge(sems[s], cnt[s])

        es3.close()
        es_ao.close()
        es_mid.close()

    return nc



def _host_layouts(cfg, inputs):
    D, T, H, DH = cfg["D"], cfg["T"], cfg["H"], cfg["DH"]
    QL, KL, E, R = cfg["QL"], cfg["KL"], cfg["E"], cfg["R"]
    KT = D // 128
    ER = E * R
    N1 = min(128, ER)
    N2 = ER - N1
    f32 = np.float32
    C = lambda a: np.ascontiguousarray(a, dtype=f32)

    w = {}
    w["wq"] = C(inputs["L0_attn_Wq_d"])
    w["wkv"] = C(inputs["L0_attn_Wkv_d"])
    w["wqk"] = C(inputs["L0_attn_W_qk"])
    w["wvu"] = C(inputs["L0_attn_Wv_u"])
    w["bqd"] = C(inputs["L0_attn_bq_d"].reshape(QL, 1))
    w["bkvd"] = C(inputs["L0_attn_bkv_d"].reshape(KL, 1))
    w["bqk"] = C(np.asarray(inputs["L0_attn_b_qk"]).reshape(H, KL).T)
    w["bvuf"] = C(np.asarray(inputs["L0_attn_bv_u"]).reshape(1, D))
    w["bo"] = C(np.asarray(inputs["L0_attn_bo"]).reshape(KT, 128).T)
    w["wo"] = C(inputs["L0_attn_Wo"])
    w["bwt"] = C(np.asarray(inputs["L0_ffn_base_W"]).T)
    w["gwt"] = C(np.asarray(inputs["L0_ffn_gate_W"]).T)
    w["lat"] = C(np.asarray(inputs["L0_ffn_lora_A"]).reshape(ER, D).T)
    w["lbt"] = C(np.asarray(inputs["L0_ffn_lora_B"]).transpose(0, 2, 1).reshape(ER, D))
    e_full = np.zeros((E, ER), dtype=f32)
    for e in range(E):
        e_full[e, e * R:(e + 1) * R] = 1.0
    w["e1"] = C(e_full[:, 0:N1])
    w["e2"] = C(e_full[:, N1:ER])
    w["idm"] = np.eye(128, dtype=f32)
    return w


def kernel(**inputs):
    cfg = FULL_CFG
    D, T = cfg["D"], cfg["T"]
    x = np.asarray(inputs["L0_attn_x"], dtype=np.float32)
    B, S, _ = x.shape
    n_cores = 8
    w = _host_layouts(cfg, inputs)

    nc = build(cfg)
    in_maps = []
    for c in range(n_cores):
        b, q = c // 2, c % 2
        own = np.ascontiguousarray(x[b, q * T:(q + 1) * T, :].T)
        oth = np.ascontiguousarray(x[b, (1 - q) * T:(2 - q) * T, :].T)
        m = dict(w)
        m["xT_own"] = own
        m["xT_oth"] = oth
        in_maps.append(m)

    res = run_bass_kernel_spmd(nc, in_maps, core_ids=list(range(n_cores)))
    out = np.empty((B, S, D), dtype=np.float32)
    for c in range(n_cores):
        b, q = c // 2, c % 2
        out[b, q * T:(q + 1) * T, :] = res.results[c]["outT"].T
    return out


# version 27
# speedup vs baseline: 1.0152x; 1.0152x over previous
import math
from contextlib import ExitStack

import numpy as np
import ml_dtypes

import concourse.bass as bass
import concourse.mybir as mybir
from concourse.bass_utils import run_bass_kernel_spmd

F32 = mybir.dt.float32
BF16 = mybir.dt.bfloat16
AF = mybir.ActivationFunctionType
ALU = mybir.AluOpType
AX = mybir.AxisListType
BFT = ml_dtypes.bfloat16

FULL_CFG = dict(D=4096, T=1024, H=32, DH=128, QL=12, KL=4, E=28, R=8)


def build(cfg):
    D, T, H, DH = cfg["D"], cfg["T"], cfg["H"], cfg["DH"]
    QL, KL, E, R = cfg["QL"], cfg["KL"], cfg["E"], cfg["R"]
    KT = D // 128
    SKT = (2 * T) // 128
    NT = T // 128
    ER = E * R
    N1 = min(128, ER)
    N2 = ER - N1
    CH = [(c0, min(512, T - c0)) for c0 in range(0, T, 512)]
    NCH = len(CH)
    QSC = float(1.0 / math.sqrt(KL))
    EPS = 1e-5

    AR = 4 * T + (512 if 4 * T + 512 <= 4096 else 0)
    CA, CB, CC, CD = 0, T, 2 * T, 3 * T
    KG = 2 * T
    A2C = 2 * T
    FC = 3 * T
    PB = [0, T]
    NPB = 4
    if SKT >= 16:
        QKT, BKT, UKT, DKT = 6, 8, 10, 5
    else:
        QKT, BKT, UKT, DKT = 3, 4, 5, 2

    nc = bass.Bass("TRN2", target_bir_lowering=False)

    dx = nc.dram_tensor("xT_own", [D, T], F32, kind="ExternalInput")
    dxo = nc.dram_tensor("xT_oth", [D, T], F32, kind="ExternalInput")
    dwq = nc.dram_tensor("wq", [D, QL], F32, kind="ExternalInput")
    dwkv = nc.dram_tensor("wkv", [D, KL], F32, kind="ExternalInput")
    dwqk = nc.dram_tensor("wqk", [QL, H * KL], F32, kind="ExternalInput")
    dwvu = nc.dram_tensor("wvu", [KL, D], F32, kind="ExternalInput")
    dbqd = nc.dram_tensor("bqd", [QL, 1], F32, kind="ExternalInput")
    dbkvd = nc.dram_tensor("bkvd", [KL, 1], F32, kind="ExternalInput")
    dbqk = nc.dram_tensor("bqk", [KL, H], F32, kind="ExternalInput")
    dbvu = nc.dram_tensor("bvuf", [1, D], F32, kind="ExternalInput")
    dbo = nc.dram_tensor("bo", [128, KT], F32, kind="ExternalInput")
    dwo = nc.dram_tensor("wo", [D, D], F32, kind="ExternalInput")
    dbwt = nc.dram_tensor("bwt", [D, D], F32, kind="ExternalInput")
    dgwt = nc.dram_tensor("gwt", [D, E], F32, kind="ExternalInput")
    dlat = nc.dram_tensor("lat", [D, ER], F32, kind="ExternalInput")
    dlbt = nc.dram_tensor("lbt", [ER, D], F32, kind="ExternalInput")
    de1 = nc.dram_tensor("e1", [E, N1], F32, kind="ExternalInput")
    de2 = nc.dram_tensor("e2", [E, N2], F32, kind="ExternalInput")
    did = nc.dram_tensor("idm", [128, 128], F32, kind="ExternalInput")
    dout = nc.dram_tensor("outT", [D, T], F32, kind="ExternalOutput")
    dgs = nc.dram_tensor("gscr", [D, T], mybir.dt.float16)

    es = ExitStack()
    with es:
        sbw = lambda st_, name, shape, dt: st_.enter_context(nc.sbuf_tensor(name, shape, dt))
        sbuf = lambda name, shape, dt: sbw(es, name, shape, dt)
        xnT = sbuf("xnT", [128, KT * T], BF16)
        bt1 = sbuf("bt1", [128, D], BF16)
        bt2 = sbuf("bt2", [N2, D], BF16)
        hg1 = sbuf("hg1", [128, T], BF16)
        hg2 = sbuf("hg2", [N2, T], BF16)
        bqk_f = sbuf("bqk_f", [KL, H], F32)

        bo_f = sbuf("bo_f", [128, KT], F32)
        bqd_f = sbuf("bqd_f", [QL, 1], F32)
        bkvd_f = sbuf("bkvd_f", [KL, 1], F32)
        ones_b = sbuf("ones_b", [128, 1], BF16)
        ones8 = sbuf("ones8", [128, 8], BF16)
        ones_f = sbuf("ones_f", [1, 128], F32)
        es_mid = ExitStack()
        cq_b = sbw(es_mid, "cq_b", [QL, T], BF16)
        ckv_b = sbw(es_mid, "ckv_b", [KL, 2 * T], BF16)
        kaug = sbw(es_mid, "kaug", [128, 8 * SKT], BF16)
        wvu8 = sbw(es_mid, "wvu8", [8, D], BF16)
        wqk_b = sbw(es_mid, "wqk_b", [QL, H * KL], BF16)
        rs_bf = sbw(es_mid, "rs_bf", [128, T], BF16)
        bw_bf = sbw(es_mid, "bw_bf", [128, 2 * D], BF16)
        es1 = ExitStack()
        id_f = sbw(es1, "id_f", [128, 128], F32)
        ckv_f = sbw(es1, "ckv_f", [KL, 2 * T], F32)
        rs_f = sbw(es1, "rs_f", [128, T], F32)
        mrs_f = sbw(es1, "mrs_f", [128, T], F32)
        rso_f = sbw(es1, "rso_f", [128, T], F32)
        srow = sbw(es1, "srow", [1, 3 * T], F32)
        xst = sbw(es1, "xst", [128, 8 * T], BF16)
        xsq = sbw(es1, "xsq", [128, 2 * T], BF16)
        tmpa = sbw(es1, "tmpa", [128, T], F32)
        mrsb = sbw(es1, "mrsb", [1, T], BF16)
        cs_kv = sbw(es1, "cs_kv", [1, KL], BF16)
        e1_b = sbw(es1, "e1_b", [E, N1], BF16)
        e2_b = sbw(es1, "e2_b", [E, N2], BF16)
        logits_f = sbw(es1, "logits_f", [E, T], F32)
        gt_b = sbw(es1, "gt_b", [E, T], BF16)
        lt_f = sbw(es1, "lt_f", [128, NT * E], F32)
        mk1 = sbw(es1, "mk1", [128, NT * E], F32)
        mk2 = sbw(es1, "mk2", [128, NT * E], F32)
        msk = sbw(es1, "msk", [128, NT * E], F32)
        gtk = sbw(es1, "gtk", [128, NT * E], F32)
        mrow = sbw(es1, "mrow", [128, 6 * NT], F32)
        gex1 = sbw(es1, "gex1", [128, T], BF16)
        gex2 = sbw(es1, "gex2", [N2, T], BF16)
        wq_b = sbw(es1, "wq_b", [128, KT * QL], BF16)
        wkv_b = sbw(es1, "wkv_b", [128, KT * KL], BF16)
        gwt_b = sbw(es1, "gwt_b", [128, KT * E], BF16)
        at1 = sbw(es1, "at1", [128, KT * N1], BF16)
        at2 = sbw(es1, "at2", [128, KT * N2], BF16)

        arena = es.enter_context(nc.psum_tensor("arena", [128, AR], F32))

        sems = {s: es.enter_context(nc.semaphore(s)) for s in
                ("dW", "dY", "dZ", "sT", "sV", "sS", "dF", "dG", "dH",
                 "dX0", "dX1", "dX2", "dX3", "dX4", "dX5", "dX6", "dX7",
                 "dL0", "dL1", "dO0", "dO1", "dP0", "dP1")}
        cnt = {s: 0 for s in sems}
        EV = {}
        block = es.enter_context(nc.Block())

        def reg(key, sem):
            EV[key] = (sem, cnt[sem])

        def I(instr, sem, key=None, n=1):
            cnt[sem] += n
            instr.then_inc(sems[sem], n)
            if key is not None:
                reg(key, sem)
            return instr

        def Ip(instr, sem, key):
            cnt[sem] += 1
            instr.then_inc(sems[sem], 1)
            assert EV[key] == (sem, cnt[sem]), (key, EV[key], (sem, cnt[sem]))

        def W(eng, *keys):
            best = {}
            for k in keys:
                if k is None or k not in EV:
                    continue
                s, v = EV[k]
                if v > 0:
                    best[s] = max(best.get(s, 0), v)
            for s, v in best.items():
                eng.wait_ge(sems[s], v)

        def vself(v):
            if cnt["sV"] > 0:
                v.wait_ge(sems["sV"], cnt["sV"])

        def sself(s):
            if cnt["sS"] > 0:
                s.wait_ge(sems["sS"], cnt["sS"])

        @block.vector
        def _(v):
            I(v.memset(ones_b[:, :], 1.0), "sV")
            I(v.memset(ones8[:, :], 1.0), "sV")
            I(v.memset(ones_f[:, :], 1.0), "sV")
            I(v.memset(wvu8[:, :], 0.0), "sV")
            I(v.memset(kaug[:, :], 0.0), "sV", ("init",))

        @block.sync
        def _(sy):
            for dsrc, ddst in ((did, id_f), (dbqd, bqd_f), (dbkvd, bkvd_f),
                               (dbqk, bqk_f), (dbo, bo_f)):
                I(sy.dma_start(out=ddst[:, :], in_=dsrc.ap()), "dY", n=16)
            reg(("smallY",), "dY")

        gsz = min(8, KT)
        ng = KT // gsz

        @block.gpsimd
        def _(g):
            for kt in range(min(8, KT)):
                I(g.dma_start(out=xnT[:, kt * T:(kt + 1) * T],
                              in_=dx.ap()[kt * 128:(kt + 1) * 128, :]),
                  "dX%d" % kt, ("so", "x", kt), n=16)

        for kt in range(KT):
            EV[("so", "sq", kt)] = ("sS", kt + 1)
        EV[("so", "rstd")] = ("sS", KT + 2)
        for kt in range(KT):
            EV[("st", "sq", kt)] = ("sS", KT + 2 + kt + 1)
        EV[("st", "rstd")] = ("sS", 2 * KT + 4)
        EV[("gexp",)] = ("sS", 2 * KT + 5)

        slot_ev = {}

        def stats_pass(bk, dsrc, ca, cbk, r2, first, extra=None, inject=None):
            own = bk == "so"
            for kt in range(KT):
                sl = kt % 8
                buf = xnT if own else xst
                off = (kt if own else sl) * T
                if kt >= 8:
                    @block.gpsimd
                    def _(g, kt=kt, sl=sl, buf=buf, off=off):
                        if own:
                            W(g, (bk, "x", kt - 8))
                        else:
                            W(g, (bk, "x", kt - 8), (bk, "sT", kt - 8),
                              (bk, "sq", kt - 8))
                        I(g.dma_start(out=buf[:, off:off + T],
                                      in_=dsrc.ap()[kt * 128:(kt + 1) * 128, :]),
                          "dX%d" % (kt % 8), (bk, "x", kt), n=16)
                        if inject and kt in inject:
                            inject[kt](g)

                @block.tensor
                def _(t, kt=kt, buf=buf, off=off):
                    W(t, (bk, "x", kt), (bk, "sq", kt))
                    if kt == 0:
                        W(t, *first)
                    if extra is not None:
                        extra(t, kt, off)
                    for c0, cw in CH:
                        t.matmul(arena[0:8, ca + c0:ca + c0 + cw], ones8[:, :],
                                 buf[:, off + c0:off + c0 + cw],
                                 start=(kt == 0), stop=(kt == KT - 1),
                                 skip_group_check=True)
                    mm = None
                    for c0, cw in CH:
                        mm = t.matmul(arena[r2:r2 + 8, cbk + c0:cbk + c0 + cw],
                                      ones8[:, :],
                                      xsq[:, (kt % 2) * T + c0:(kt % 2) * T + c0 + cw],
                                      start=(kt == 0), stop=(kt == KT - 1),
                                      skip_group_check=True)
                    I(mm, "sT", (bk, "sT", kt))

        def _w_wq(g):
            I(g.dma_start(out=wq_b.ap().rearrange("p (kt q) -> p kt q", kt=KT),
                          in_=dwq.ap().rearrange("(kt p) q -> p kt q", p=128)), "dW", n=16)

        def _w_wkv(g):
            I(g.dma_start(out=wkv_b.ap().rearrange("p (kt q) -> p kt q", kt=KT),
                          in_=dwkv.ap().rearrange("(kt p) q -> p kt q", p=128)), "dW", n=16)

        def _w_gwt(i):
            def f(g):
                k0, k1 = i * gsz, (i + 1) * gsz
                I(g.dma_start(
                    out=gwt_b[:, k0 * E:k1 * E].rearrange("p (kt e) -> p kt e", kt=gsz),
                    in_=dgwt.ap()[k0 * 128:k1 * 128, :].rearrange("(kt p) e -> p kt e", p=128)),
                  "dW", n=16)
            return f

        def _w_at1(i):
            def f(g):
                k0, k1 = i * gsz, (i + 1) * gsz
                I(g.dma_start(
                    out=at1[:, k0 * N1:k1 * N1].rearrange("p (kt e) -> p kt e", kt=gsz),
                    in_=dlat.ap()[k0 * 128:k1 * 128, 0:N1].rearrange("(kt p) e -> p kt e", p=128)),
                  "dW", n=16)
            return f

        def _w_at2(i, last=False):
            def f(g):
                k0, k1 = i * gsz, (i + 1) * gsz
                I(g.dma_start(
                    out=at2[:, k0 * N2:k1 * N2].rearrange("p (kt e) -> p kt e", kt=gsz),
                    in_=dlat.ap()[k0 * 128:k1 * 128, N1:ER].rearrange("(kt p) e -> p kt e", p=128)),
                  "dW", n=16)
                if last:
                    reg(("wA",), "dW")
            return f

        wparts = [_w_wq, _w_wkv] + [_w_gwt(i) for i in range(ng)] \
            + [_w_at1(i) for i in range(ng)] \
            + [_w_at2(i, last=(i == ng - 1)) for i in range(ng)]
        winj = {}
        if KT >= 16:
            step = max(1, (KT - 10) // len(wparts))
            for n_, fn in enumerate(wparts):
                winj[8 + n_ * step] = fn
            assert 8 + (len(wparts) - 1) * step < KT
        else:
            @block.gpsimd
            def _(g):
                for fn in wparts:
                    fn(g)

        stats_pass("so", dx, CA, CB, 0, (("init",),), inject=winj)


        @block.tensor
        def _(t):
            W(t, ("wA",))
            mm = None
            for kt in range(KT):
                mm = t.matmul(arena[0:8, CD:CD + KL], ones8[:, :],
                              wkv_b[:, kt * KL:(kt + 1) * KL],
                              start=(kt == 0), stop=(kt == KT - 1),
                              skip_group_check=True)
            I(mm, "sT", ("cskvT",))

        @block.vector
        def _(v):
            vself(v)
            W(v, ("cskvT",))
            I(v.tensor_copy(cs_kv[0:1, 0:KL], arena[0:1, CD:CD + KL]), "sV", ("cskv",))


        @block.vector
        def _(v):
            vself(v)
            W(v, ("so", "sT", KT - 1))
            I(v.tensor_scalar(srow[0:1, 0:T], arena[0:1, CA:CA + T],
                              1.0 / D, None, ALU.mult), "sV")
            vself(v)
            I(v.tensor_scalar(srow[0:1, T:2 * T], arena[0:1, CB:CB + T],
                              1.0 / D, EPS, ALU.mult, ALU.add), "sV")
            vself(v)
            I(v.tensor_tensor(srow[0:1, 2 * T:3 * T], srow[0:1, 0:T],
                              srow[0:1, 0:T], ALU.mult), "sV")
            vself(v)
            I(v.tensor_tensor(srow[0:1, 2 * T:3 * T], srow[0:1, T:2 * T],
                              srow[0:1, 2 * T:3 * T], ALU.subtract), "sV", ("so", "var"))

        @block.tensor
        def _(t):
            W(t, ("so", "rstd"))
            mm = None
            for c0, cw in CH:
                mm = t.matmul(arena[0:128, CA + c0:CA + c0 + cw], ones_f[:, :],
                              srow[0:1, 2 * T + c0:2 * T + c0 + cw],
                              start=True, stop=True, skip_group_check=True)
            I(mm, "sT", ("so", "bc1"))

        @block.vector
        def _(v):
            vself(v)
            W(v, ("so", "bc1"))
            I(v.tensor_copy(rs_f[:, :], arena[0:128, CA:CA + T]), "sV", ("so", "rsc"))
            vself(v)
            I(v.tensor_copy(rs_bf[:, :], arena[0:128, CA:CA + T]), "sV", ("rsbf",))

        @block.tensor
        def _(t):
            W(t, ("so", "rsc"), ("rsbf",))
            mm = None
            for c0, cw in CH:
                mm = t.matmul(arena[0:128, CA + c0:CA + c0 + cw], ones_f[:, :],
                              srow[0:1, c0:c0 + cw],
                              start=True, stop=True, skip_group_check=True)
            I(mm, "sT", ("so", "bc2"))

        @block.vector
        def _(v):
            vself(v)
            W(v, ("so", "bc2"))
            I(v.tensor_copy(mrs_f[:, :], arena[0:128, CA:CA + T]), "sV", ("so", "meanc"))

        for kt in range(KT):
            @block.vector
            def _(v, kt=kt):
                vself(v)
                W(v, ("so", "meanc"), ("so", "sT", kt), ("so", "sq", kt))
                I(v.scalar_tensor_tensor(xnT[:, kt * T:(kt + 1) * T],
                                         xnT[:, kt * T:(kt + 1) * T], 1.0,
                                         mrs_f[:, :], ALU.mult, ALU.subtract),
                  "sV", ("b", "xn", kt))

            @block.tensor
            def _(t, kt=kt):
                W(t, ("b", "xn", kt))
                if kt == 0:
                    W(t, ("cskv",))
                xn = xnT[:, kt * T:(kt + 1) * T]
                for c0, cw in CH:
                    t.matmul(arena[0:QL, CC + c0:CC + c0 + cw],
                             wq_b[:, kt * QL:(kt + 1) * QL], xn[:, c0:c0 + cw],
                             start=(kt == 0), stop=(kt == KT - 1), skip_group_check=True)
                for c0, cw in CH:
                    t.matmul(arena[32:32 + KL, CC + c0:CC + c0 + cw],
                             wkv_b[:, kt * KL:(kt + 1) * KL], xn[:, c0:c0 + cw],
                             start=(kt == 0), stop=(kt == KT - 1), skip_group_check=True)
                for c0, cw in CH:
                    t.matmul(arena[0:E, CA + c0:CA + c0 + cw],
                             gwt_b[:, kt * E:(kt + 1) * E], xn[:, c0:c0 + cw],
                             start=(kt == 0), stop=(kt == KT - 1), skip_group_check=True)
                for c0, cw in CH:
                    t.matmul(arena[0:N1, CB + c0:CB + c0 + cw],
                             at1[:, kt * N1:(kt + 1) * N1], xn[:, c0:c0 + cw],
                             start=(kt == 0), stop=(kt == KT - 1), skip_group_check=True)
                mm = None
                for c0, cw in CH:
                    mm = t.matmul(arena[0:N2, CD + c0:CD + c0 + cw],
                                  at2[:, kt * N2:(kt + 1) * N2], xn[:, c0:c0 + cw],
                                  start=(kt == 0), stop=(kt == KT - 1), skip_group_check=True)
                I(mm, "sT", ("b", "sT", kt))


        @block.gpsimd
        def _(g):
            for kt in range(min(8, KT)):
                W(g, ("so", "x", KT - 8 + kt),
                  ("so", "x", kt) if kt < KT else None)
                I(g.dma_start(out=xst[:, (kt % 8) * T:(kt % 8 + 1) * T],
                              in_=dxo.ap()[kt * 128:(kt + 1) * 128, :]),
                  "dX%d" % (kt % 8), ("st", "x", kt), n=16)

        @block.gpsimd
        def _(g):
            I(g.dma_start(out=wqk_b[:, :], in_=dwqk.ap()), "dZ", n=16)
            W(g, ("init",))
            I(g.dma_start(out=wvu8[2:6, :], in_=dwvu.ap()), "dZ", n=16)
            I(g.dma_start(out=wvu8[0:1, :], in_=dbvu.ap()), "dZ", n=16)
            I(g.dma_start(out=e1_b[:, :], in_=de1.ap()), "dZ", n=16)
            I(g.dma_start(out=e2_b[:, :], in_=de2.ap()), "dZ", n=16)
            I(g.dma_start(out=bt1[:, :], in_=dlbt.ap()[0:128, :]), "dZ", n=16)
            I(g.dma_start(out=bt2[:, :], in_=dlbt.ap()[128:ER, :]), "dZ", n=16)
            reg(("lbtW",), "dZ")

        def emit_bwload(mo):
            @block.gpsimd
            def _(g, mo=mo):
                if mo >= 2:
                    W(g, ("fT", mo - 2))
                for i in range(ng):
                    k0, k1 = i * gsz, (i + 1) * gsz
                    I(g.dma_start(
                        out=bw_bf[:, (mo % 2) * D + k0 * 128:(mo % 2) * D + k1 * 128]
                            .rearrange("p (ki c) -> p ki c", ki=gsz),
                        in_=dbwt.ap()[k0 * 128:k1 * 128, mo * 128:(mo + 1) * 128]
                            .rearrange("(ki p) c -> p ki c", p=128)),
                      "dP%d" % (mo % 2), n=16)
                reg(("wLb", mo), "dP%d" % (mo % 2))

        emit_bwload(0)
        emit_bwload(1)

        @block.vector
        def _(v):
            vself(v)
            W(v, ("b", "sT", KT - 1), ("smallY",))
            I(v.tensor_tensor(tmpa[0:QL, 0:T], arena[0:QL, CC:CC + T], rs_f[0:QL, :],
                              ALU.mult), "sV")
            vself(v)
            I(v.tensor_scalar(cq_b[:, :], tmpa[0:QL, 0:T], bqd_f[:, :], None,
                              ALU.add), "sV", ("cqev",))
            vself(v)
            I(v.tensor_tensor(ckv_f[0:KL, 0:T], arena[32:32 + KL, CC:CC + T],
                              rs_f[0:KL, :], ALU.mult), "sV")
            vself(v)
            I(v.tensor_scalar(ckv_f[0:KL, 0:T], ckv_f[0:KL, 0:T], bkvd_f[:, :], None,
                              ALU.add), "sV")
            vself(v)
            I(v.tensor_copy(ckv_b[:, 0:T], ckv_f[0:KL, 0:T]), "sV", ("ckev0",))
            vself(v)
            I(v.tensor_tensor(logits_f[:, :], arena[0:E, CA:CA + T], rs_f[0:E, :],
                              ALU.mult), "sV", ("lgev",))

        @block.tensor
        def _(t):
            W(t, ("lgev",))
            mm = None
            for tt in range(NT):
                mm = t.transpose(arena[0:128, CA + tt * E:CA + (tt + 1) * E],
                                 logits_f[0:E, tt * 128:(tt + 1) * 128], id_f[0:E, 0:E])
            I(mm, "sT", ("ltT",))

        @block.vector
        def _(v):
            vself(v)
            W(v, ("ltT",))
            I(v.tensor_copy(lt_f[:, :], arena[0:128, CA:CA + NT * E]), "sV", ("ltc",))
            for tt in range(NT):
                sl = slice(tt * E, (tt + 1) * E)
                vself(v)
                I(v.reduce_max(mrow[:, tt:tt + 1], lt_f[:, sl], axis=AX.X), "sV")
                vself(v)
                I(v.tensor_scalar(mk1[:, sl], lt_f[:, sl], mrow[:, tt:tt + 1], None,
                                  ALU.is_equal), "sV")
                vself(v)
                I(v.scalar_tensor_tensor(msk[:, sl], mk1[:, sl], -1e30, lt_f[:, sl],
                                         ALU.mult, ALU.add), "sV")
                vself(v)
                I(v.reduce_max(mrow[:, NT + tt:NT + tt + 1], msk[:, sl], axis=AX.X), "sV")
                vself(v)
                I(v.tensor_scalar(mk2[:, sl], msk[:, sl], mrow[:, NT + tt:NT + tt + 1], None,
                                  ALU.is_equal), "sV")
            vself(v)
            I(v.tensor_tensor(mrow[:, 4 * NT:5 * NT], mrow[:, 0:NT], mrow[:, NT:2 * NT],
                              ALU.subtract), "sV", ("gdel",))

        def ckv_oth_extra(t, kt, off):
            for c0, cw in CH:
                t.matmul(arena[32:32 + KL, CC + c0:CC + c0 + cw],
                         wkv_b[:, kt * KL:(kt + 1) * KL],
                         xst[:, off + c0:off + c0 + cw],
                         start=(kt == 0), stop=False, skip_group_check=True)

        stats_pass("st", dxo, CC, CC, 64, (("cqev",), ("ckev0",)), extra=ckv_oth_extra)

        @block.vector
        def _(v):
            vself(v)
            W(v, ("st", "sT", KT - 1), ("so", "bc2"))
            I(v.tensor_scalar(srow[0:1, 0:T], arena[0:1, CC:CC + T],
                              1.0 / D, None, ALU.mult), "sV")
            vself(v)
            I(v.tensor_scalar(srow[0:1, T:2 * T], arena[64:65, CC:CC + T],
                              1.0 / D, EPS, ALU.mult, ALU.add), "sV")
            vself(v)
            I(v.tensor_tensor(srow[0:1, 2 * T:3 * T], srow[0:1, 0:T],
                              srow[0:1, 0:T], ALU.mult), "sV")
            vself(v)
            I(v.tensor_tensor(srow[0:1, 2 * T:3 * T], srow[0:1, T:2 * T],
                              srow[0:1, 2 * T:3 * T], ALU.subtract), "sV", ("st", "var"))

        @block.vector
        def _(v):
            vself(v)
            W(v, ("st", "rstd"))
            I(v.scalar_tensor_tensor(srow[0:1, T:2 * T], srow[0:1, 0:T],
                                     -1.0, srow[0:1, 2 * T:3 * T], ALU.mult, ALU.mult),
              "sV")
            vself(v)
            I(v.tensor_copy(mrsb[0:1, 0:T], srow[0:1, T:2 * T]), "sV", ("mrsb",))

        @block.tensor
        def _(t):
            W(t, ("mrsb",))
            mm = None
            for ci, (c0, cw) in enumerate(CH):
                mm = t.matmul(arena[32:32 + KL, CC + c0:CC + c0 + cw],
                              cs_kv[0:1, 0:KL], mrsb[0:1, c0:c0 + cw],
                              start=False, stop=(ci == NCH - 1), skip_group_check=True)
            I(mm, "sT", ("ckoT",))

        @block.vector
        def _(v):
            vself(v)
            W(v, ("gexp",))
            I(v.tensor_scalar(mrow[:, 4 * NT:5 * NT], mrow[:, 5 * NT:6 * NT], 1.0, None,
                              ALU.add), "sV")
            vself(v)
            I(v.reciprocal(mrow[:, 2 * NT:3 * NT], mrow[:, 4 * NT:5 * NT]), "sV")
            vself(v)
            I(v.tensor_scalar(mrow[:, 3 * NT:4 * NT], mrow[:, 2 * NT:3 * NT], -2.0, 2.0,
                              ALU.mult, ALU.add), "sV")
            for tt in range(NT):
                sl = slice(tt * E, (tt + 1) * E)
                vself(v)
                I(v.tensor_scalar(msk[:, sl], mk1[:, sl], mrow[:, 2 * NT + tt:2 * NT + tt + 1],
                                  2.0, ALU.mult, ALU.mult), "sV")
                vself(v)
                I(v.scalar_tensor_tensor(gtk[:, sl], mk2[:, sl],
                                         mrow[:, 3 * NT + tt:3 * NT + tt + 1], msk[:, sl],
                                         ALU.mult, ALU.add), "sV")
            reg(("gtk",), "sV")

        @block.tensor
        def _(t):
            W(t, ("gtk",), ("ltc",))
            mm = None
            for tt in range(NT):
                mm = t.transpose(arena[0:E, CA + tt * 128:CA + (tt + 1) * 128],
                                 gtk[0:128, tt * E:(tt + 1) * E], id_f[:, :])
            I(mm, "sT", ("gtT",))

        @block.vector
        def _(v):
            vself(v)
            W(v, ("gtT",))
            I(v.tensor_copy(gt_b[:, :], arena[0:E, CA:CA + T]), "sV", ("gtc",))

        @block.tensor
        def _(t):
            W(t, ("gtc",), ("lbtW",))
            mm = None
            for c0, cw in CH:
                mm = t.matmul(arena[0:N1, CA + c0:CA + c0 + cw], e1_b[:, :],
                              gt_b[0:E, c0:c0 + cw], start=True, stop=True,
                              skip_group_check=True)
            I(mm, "sT", ("gx1T",))

        @block.vector
        def _(v):
            vself(v)
            W(v, ("gx1T",))
            I(v.tensor_copy(gex1[:, :], arena[0:N1, CA:CA + T]), "sV", ("gx1c",))
            vself(v)
            I(v.tensor_tensor(hg1[:, :], arena[0:N1, CB:CB + T], gex1[:, :], ALU.mult),
              "sV", ("hg1",))

        @block.tensor
        def _(t):
            W(t, ("hg1",))
            mm = None
            for c0, cw in CH:
                mm = t.matmul(arena[0:N2, CB + c0:CB + c0 + cw], e2_b[:, :],
                              gt_b[0:E, c0:c0 + cw], start=True, stop=True,
                              skip_group_check=True)
            I(mm, "sT", ("gx2T",))

        @block.vector
        def _(v):
            vself(v)
            W(v, ("gx2T",))
            I(v.tensor_copy(gex2[:, :], arena[0:N2, CB:CB + T]), "sV", ("gx2c",))
            vself(v)
            I(v.tensor_tensor(hg2[:, :], arena[0:N2, CD:CD + T], gex2[:, :], ALU.mult),
              "sV", ("hg2",))

        @block.tensor
        def _(t):
            W(t, ("st", "rstd"))
            mm = None
            for c0, cw in CH:
                mm = t.matmul(arena[64:64 + KL, CC + c0:CC + c0 + cw],
                              ones_f[0:1, 0:KL],
                              srow[0:1, 2 * T + c0:2 * T + c0 + cw],
                              start=True, stop=True, skip_group_check=True)
            I(mm, "sT", ("st", "bc1"))

        @block.vector
        def _(v):
            vself(v)
            W(v, ("st", "bc1"))
            I(v.tensor_copy(rso_f[0:KL, :], arena[64:64 + KL, CC:CC + T]),
              "sV", ("st", "rsc"))
            vself(v)
            W(v, ("ckoT",))
            I(v.tensor_tensor(ckv_f[0:KL, T:2 * T], arena[32:32 + KL, CC:CC + T],
                              rso_f[0:KL, :], ALU.mult), "sV")
            vself(v)
            I(v.tensor_scalar(ckv_f[0:KL, T:2 * T], ckv_f[0:KL, T:2 * T],
                              bkvd_f[:, :], None, ALU.add), "sV")
            vself(v)
            I(v.tensor_copy(ckv_b[:, T:2 * T], ckv_f[0:KL, T:2 * T]), "sV", ("ckev1",))

        @block.scalar
        def _(s):
            for bk in ("so", "st"):
                for kt in range(KT):
                    src_ = (xnT[:, kt * T:(kt + 1) * T] if bk == "so"
                            else xst[:, (kt % 8) * T:(kt % 8 + 1) * T])
                    war = (bk, "sT", kt - 2)
                    if kt < 2 and bk == "st":
                        war = ("so", "sT", KT - 2 + kt)
                    sself(s)
                    W(s, (bk, "x", kt), war)
                    Ip(s.activation(xsq[:, (kt % 2) * T:(kt % 2 + 1) * T],
                                    src_, AF.Square),
                       "sS", (bk, "sq", kt))
                sself(s)
                W(s, (bk, "var"))
                I(s.activation(srow[0:1, T:2 * T], srow[0:1, 2 * T:3 * T], AF.Ln), "sS")
                sself(s)
                Ip(s.activation(srow[0:1, 2 * T:3 * T], srow[0:1, T:2 * T], AF.Exp,
                                scale=-0.5), "sS", (bk, "rstd"))

        @block.scalar
        def _(s):
            sself(s)
            W(s, ("gdel",))
            Ip(s.activation(mrow[:, 5 * NT:6 * NT], mrow[:, 4 * NT:5 * NT], AF.Exp,
                            scale=-1.0), "sS", ("gexp",))

        @block.tensor
        def _(t):
            W(t, ("hg2",), ("ckev1",))
            mm = None
            for kt in range(SKT):
                mm = t.transpose(arena[0:128, KG + 4 * kt:KG + 4 * kt + 4],
                                 ckv_f[0:KL, kt * 128:(kt + 1) * 128], id_f[0:KL, 0:KL])
            I(mm, "sT", ("kgT",))

        @block.vector
        def _(v):
            vself(v)
            W(v, ("kgT",))
            I(v.tensor_copy(
                kaug.ap().rearrange("p (g c) -> p g c", g=SKT)[:, :, 2:6],
                arena[0:128, KG:KG + 4 * SKT].rearrange("p (g c) -> p g c", g=SKT)),
              "sV", ("kgc",))
            vself(v)
            I(v.memset(kaug.ap().rearrange("p (g c) -> p g c", g=SKT)[:, :, 0:1], 1.0),
              "sV", ("kgo",))

        es1.close()
        es_ao = ExitStack()
        aoT = sbw(es_ao, "aoT", [128, H * T], BF16)
        es_attn = ExitStack()
        pbuf = sbw(es_attn, "pbuf", [128, NPB * T], BF16)
        qb0 = sbw(es_attn, "qb0", [KL, T], BF16)
        qb1 = sbw(es_attn, "qb1", [KL, T], BF16)
        a2b0 = sbw(es_attn, "a2b0", [8, T], BF16)
        a2b1 = sbw(es_attn, "a2b1", [8, T], BF16)
        dinvb = sbw(es_attn, "dinvb", [128, T], F32)
        dtmp = dinvb
        dinv_bf = sbw(es_attn, "dinv_bf", [1, T], BF16)
        onesb = sbw(es_attn, "onesb", [1, 128], BF16)
        fsb = sbw(es_attn, "fsb", [128, T], mybir.dt.float16)
        qbs = [qb0, qb1]
        a2bs = [a2b0, a2b1]

        assert H == KT
        pair_free = {0: [("gx1c",)] * NCH, 1: [("gx2c",)] * NCH}

        spair = {}
        qloc = {}
        bcdloc = {}
        uploc = {}

        def a2block(t, hh, j, explicit_exp=False):
            if explicit_exp:
                W(t, ("exp", hh, j))
            u = hh * SKT + j
            mm = None
            for c, (c0, cw) in enumerate(CH):
                if j == 0:
                    if hh > 0:
                        W(t, ("a2s", hh - 1, c))
                    elif c == 0:
                        W(t, ("ckev1",), ("kgo",))
                mm = t.matmul(arena[0:8, A2C + c0:A2C + c0 + cw],
                              kaug[:, 8 * j:8 * j + 8],
                              pbuf[:, (u % NPB) * T + c0:(u % NPB) * T + c0 + cw],
                              start=(j == 0), stop=(j == SKT - 1),
                              skip_group_check=True)
            I(mm, "sT", ("a2", hh, j))

        def ffn_mms(mo):
            n = (KT + 2) * NCH
            i = 0
            for ki in range(KT):
                for c0, cw in CH:
                    yield (bw_bf[:, (mo % 2) * D + ki * 128:(mo % 2) * D + (ki + 1) * 128],
                           xnT[:, ki * T + c0:ki * T + c0 + cw], c0, cw,
                           ki == 0, False)
            for c0, cw in CH:
                yield (bt1[:, mo * 128:(mo + 1) * 128], hg1[:, c0:c0 + cw], c0, cw,
                       False, False)
            for ci, (c0, cw) in enumerate(CH):
                yield (bt2[0:N2, mo * 128:(mo + 1) * 128], hg2[0:N2, c0:c0 + cw], c0, cw,
                       False, ci == NCH - 1)

        def emit_bcd(t, hh, pb_):
            W(t, ("dinv", hh))
            for qc, (c0, cw) in enumerate(CH):
                W(t, pair_free[pb_][qc])
                I(t.matmul(arena[0:128, PB[pb_] + c0:PB[pb_] + c0 + cw], onesb[:, :],
                           dinv_bf[0:1, c0:c0 + cw], start=True, stop=True,
                           skip_group_check=True), "sT", ("bcd", hh, qc))
            bcdloc[hh] = PB[pb_]
            pair_free[pb_] = [("dcp", hh, qc) for qc in range(NCH)]

        def emit_up(t, hh, pu):
            W(t, ("a2s", hh, NCH - 1))
            for qc, (c0, cw) in enumerate(CH):
                W(t, pair_free[pu][qc])
                I(t.matmul(arena[0:128, PB[pu] + c0:PB[pu] + c0 + cw],
                           wvu8[0:8, hh * DH:(hh + 1) * DH],
                           a2bs[hh % 2][0:8, c0:c0 + cw], start=True, stop=True,
                           skip_group_check=True), "sT", ("up", hh, qc))
            uploc[hh] = PB[pu]
            pair_free[pu] = [("uptt", hh, qc) for qc in range(NCH)]

        def emit_vtail(v, hh):
            for qc, (c0, cw) in enumerate(CH):
                vself(v)
                W(v, ("bcd", hh, qc))
                Ip(v.tensor_copy(dinvb[:, c0:c0 + cw],
                                 arena[0:128, bcdloc[hh] + c0:bcdloc[hh] + c0 + cw]),
                   "sV", ("dcp", hh, qc))
            for qc, (c0, cw) in enumerate(CH):
                vself(v)
                W(v, ("up", hh, qc))
                Ip(v.tensor_tensor(aoT[:, hh * T + c0:hh * T + c0 + cw],
                                   arena[0:128, uploc[hh] + c0:uploc[hh] + c0 + cw],
                                   dinvb[:, c0:c0 + cw], ALU.mult), "sV", ("uptt", hh, qc))

        p0 = 1

        @block.tensor
        def _(t, p0=p0):
            W(t, ("hg2",), ("cqev",), ("lbtW",), *pair_free[p0])
            for ci, (c0, cw) in enumerate(CH):
                I(t.matmul(arena[0:KL, PB[p0] + c0:PB[p0] + c0 + cw], wqk_b[0:QL, 0:KL],
                           cq_b[0:QL, c0:c0 + cw], start=True, stop=True,
                           skip_group_check=True), "sT", ("q", 0, ci))
        qloc[0] = PB[p0]
        pair_free[p0] = [("qev", 0, ci) for ci in range(NCH)]

        @block.vector
        def _(v, p0=p0):
            for ci, (c0, cw) in enumerate(CH):
                vself(v)
                W(v, ("q", 0, ci))
                I(v.tensor_scalar(qbs[0][0:KL, c0:c0 + cw],
                                  arena[0:KL, PB[p0] + c0:PB[p0] + c0 + cw],
                                  bqk_f[:, 0:1], QSC, ALU.add, ALU.mult),
                  "sV", ("qev", 0, ci))
            vself(v)
            I(v.memset(onesb[:, :], 1.0), "sV", ("onesb",))

        for h in range(H):
            base_s = cnt["sS"]
            for kt in range(SKT):
                extra = 2 if (kt > DKT and h > 0) else 0
                EV[("exp", h, kt)] = ("sS", base_s + kt + 1 + extra)
            base_s += SKT + (2 if h > 0 else 0)
            cnt["sS"] = base_s
            if h == H - 1:
                EV[("dinv", h)] = ("sS", base_s + 2)
            else:
                EV[("dinv", h)] = ("sS", base_s + DKT + 3)
            off = cnt["sV"]
            if h > 0:
                for c in range(NCH):
                    EV[("a2s", h - 1, c)] = ("sV", off + c + 1)
                off += NCH
            if h + 1 < H:
                for ci in range(NCH):
                    EV[("qev", h + 1, ci)] = ("sV", off + ci + 1)
                off += NCH
            if h > 0:
                for qc in range(NCH):
                    EV[("dcp", h - 1, qc)] = ("sV", off + qc + 1)
                off += NCH
                for qc in range(NCH):
                    EV[("uptt", h - 1, qc)] = ("sV", off + qc + 1)
                off += NCH
            EV[("fcp", h)] = ("sV", off + 1)

            @block.tensor
            def _(t, h=h):
                fq = list(ffn_mms(h))
                fpos = [0]

                def emit_ffn(n):
                    while n > 0 and fpos[0] < len(fq):
                        lhsT, rhs, c0, cw, st_, sp_ = fq[fpos[0]]
                        if fpos[0] == 0:
                            W(t, ("wLb", h), ("fcp", h - 1),
                              ("hg2",) if h == 0 else None)
                        mm = t.matmul(arena[0:128, FC + c0:FC + c0 + cw], lhsT, rhs,
                                      start=st_, stop=sp_, skip_group_check=True)
                        if fpos[0] == len(fq) - 1:
                            I(mm, "sT", ("fT", h))
                        fpos[0] += 1
                        n -= 1

                for kt in range(SKT):
                    if kt < SKT - 2:
                        emit_ffn(-(-(len(fq) - fpos[0]) // (SKT - 2 - kt)))
                    if kt < 3:
                        if h > 0:
                            a2block(t, h - 1, SKT - 3 + kt)
                    else:
                        a2block(t, h, kt - 3)
                    p = kt % 2
                    if kt == 0:
                        W(t, ("qev", h, NCH - 1))
                    spair[(h, kt)] = PB[p]
                    mm = None
                    for c, (c0, cw) in enumerate(CH):
                        W(t, pair_free[p][c])
                        mm = t.matmul(arena[0:128, PB[p] + c0:PB[p] + c0 + cw],
                                      ckv_b[0:KL, kt * 128:(kt + 1) * 128],
                                      qbs[h % 2][0:KL, c0:c0 + cw],
                                      start=True, stop=True, skip_group_check=True)
                    I(mm, "sT", ("sc", h, kt))
                    pair_free[p] = [("exp", h, kt)] * NCH
                    if kt == QKT and h + 1 < H:
                        pq = (kt + 1) % 2
                        for ci, (c0, cw) in enumerate(CH):
                            W(t, pair_free[pq][ci])
                            I(t.matmul(arena[0:KL, PB[pq] + c0:PB[pq] + c0 + cw],
                                       wqk_b[0:QL, (h + 1) * KL:(h + 2) * KL],
                                       cq_b[0:QL, c0:c0 + cw], start=True, stop=True,
                                       skip_group_check=True), "sT", ("q", h + 1, ci))
                        qloc[h + 1] = PB[pq]
                        pair_free[pq] = [("qev", h + 1, ci) for ci in range(NCH)]
                    if kt == BKT and h > 0:
                        emit_bcd(t, h - 1, (kt + 1) % 2)
                    if kt == UKT and h > 0:
                        emit_up(t, h - 1, (kt + 1) % 2)
                emit_ffn(len(fq))

            if h > 0:
                @block.gpsimd
                def _(g, h=h):
                    W(g, ("fcp", h - 1))
                    I(g.dma_start(out=dgs.ap()[(h - 1) * 128:h * 128, :], in_=fsb[:, :]),
                      "dF" if (h - 1) % 2 == 0 else "dH", ("fst", h - 1), n=16)

            @block.vector
            def _(v, h=h):
                if h > 0:
                    for c, (c0, cw) in enumerate(CH):
                        vself(v)
                        W(v, ("a2", h - 1, SKT - 1), ("up", h - 3, NCH - 1),
                          ("dinv", h - 3) if h >= 3 else None)
                        Ip(v.tensor_copy(a2bs[(h - 1) % 2][0:8, c0:c0 + cw],
                                         arena[0:8, A2C + c0:A2C + c0 + cw]),
                           "sV", ("a2s", h - 1, c))
                if h + 1 < H:
                    for ci, (c0, cw) in enumerate(CH):
                        vself(v)
                        W(v, ("q", h + 1, ci))
                        inst = v.tensor_scalar(qbs[(h + 1) % 2][0:KL, c0:c0 + cw],
                                               arena[0:KL, qloc[h + 1] + c0:qloc[h + 1] + c0 + cw],
                                               bqk_f[:, h + 1:h + 2], QSC, ALU.add, ALU.mult)
                        Ip(inst, "sV", ("qev", h + 1, ci))
                if h > 0:
                    emit_vtail(v, h - 1)
                vself(v)
                W(v, ("fT", h), ("fst", h - 1))
                Ip(v.tensor_tensor(fsb[:, :], arena[0:128, FC:FC + T], rs_bf[:, :],
                                   ALU.mult), "sV", ("fcp", h))

            if h + 2 < H:
                emit_bwload(h + 2)

        @block.tensor
        def _(t):
            for j in (SKT - 3, SKT - 2, SKT - 1):
                a2block(t, H - 1, j, explicit_exp=True)
            emit_bcd(t, H - 1, 0)
            emit_up(t, H - 1, 1)

        off = cnt["sV"]
        for c in range(NCH):
            EV[("a2s", H - 1, c)] = ("sV", off + c + 1)
        off += NCH
        for qc in range(NCH):
            EV[("dcp", H - 1, qc)] = ("sV", off + qc + 1)
        off += NCH
        for qc in range(NCH):
            EV[("uptt", H - 1, qc)] = ("sV", off + qc + 1)

        @block.vector
        def _(v):
            for c, (c0, cw) in enumerate(CH):
                vself(v)
                W(v, ("a2", H - 1, SKT - 1), ("up", H - 3, NCH - 1), ("dinv", H - 3))
                Ip(v.tensor_copy(a2bs[(H - 1) % 2][0:8, c0:c0 + cw],
                                 arena[0:8, A2C + c0:A2C + c0 + cw]),
                   "sV", ("a2s", H - 1, c))
            emit_vtail(v, H - 1)

        @block.gpsimd
        def _(g):
            W(g, ("fcp", H - 1))
            I(g.dma_start(out=dgs.ap()[(H - 1) * 128:H * 128, :], in_=fsb[:, :]),
              "dF" if (H - 1) % 2 == 0 else "dH", ("fst", H - 1), n=16)

        cnt["sS"] -= H * SKT + 2 * (H - 1)

        @block.scalar
        def _(s):
            for h in range(H):
                for kt in range(SKT):
                    u = h * SKT + kt
                    war = None
                    if u >= NPB:
                        up_, uk = divmod(u - NPB, SKT)
                        war = ("a2", up_, uk)
                    W(s, ("sc", h, kt), war)
                    sl = u % NPB
                    inst = s.activation(pbuf[:, sl * T:(sl + 1) * T],
                                        arena[0:128, spair[(h, kt)]:spair[(h, kt)] + T],
                                        AF.Exp)
                    Ip(inst, "sS", ("exp", h, kt))
                    if kt == DKT and h > 0:
                        W(s, ("a2s", h - 1, NCH - 1),
                          ("bcd", h - 2, NCH - 1) if h >= 2 else None)
                        I(s.activation(dtmp[0:1, 0:T], a2bs[(h - 1) % 2][0:1, 0:T],
                                       AF.Ln), "sS")
                        sself(s)
                        Ip(s.activation(dinv_bf[0:1, 0:T], dtmp[0:1, 0:T], AF.Exp,
                                        scale=-1.0), "sS", ("dinv", h - 1))
            W(s, ("a2s", H - 1, NCH - 1), ("bcd", H - 2, NCH - 1))
            I(s.activation(dtmp[0:1, 0:T], a2bs[(H - 1) % 2][0:1, 0:T], AF.Ln), "sS")
            sself(s)
            Ip(s.activation(dinv_bf[0:1, 0:T], dtmp[0:1, 0:T], AF.Exp, scale=-1.0),
               "sS", ("dinv", H - 1))

        es_attn.close()
        es3 = ExitStack()
        wo_bf = bw_bf
        gsl = sbw(es3, "gsl", [128, 2 * T], mybir.dt.float16)
        ost = sbw(es3, "ost", [128, 2 * T], F32)
        g_f = sbw(es3, "g_f", [128, T], F32)

        tb = cnt["sT"]
        vb = cnt["sV"]
        sb_ = cnt["sS"]
        lb = cnt["dL0"]
        gb = cnt["dL1"]
        gB = cnt["dG"]
        ob = {0: cnt["dO0"], 1: cnt["dO1"]}
        for mo in range(KT):
            EV[("woT", mo)] = ("sT", tb + mo + 1)
            EV[("silu", mo)] = ("sS", sb_ + mo + 1)
            EV[("stt", mo)] = ("sV", vb + 2 * mo + 1)
            EV[("res", mo)] = ("sV", vb + 2 * mo + 2)
            EV[("xq", mo)] = ("dL0", lb + 16 * (mo + 1))
            EV[("gld", mo)] = (("dL1", gb + 16 * (mo // 2 + 1)) if mo % 2 == 0
                               else ("dG", gB + 16 * (mo // 2 + 1)))
            EV[("out", mo)] = ("dO%d" % (mo % 2), ob[mo % 2] + 16 * (mo // 2 + 1))

        @block.gpsimd
        def _(g):
            for mo in range(KT):
                if mo >= 2:
                    W(g, ("woT", mo - 2))
                else:
                    W(g, ("uptt", H - 1, NCH - 1), ("fcp", H - 1),
                      ("fst", H - 1), ("fst", H - 2), ("dinv", H - 1))
                for i in range(ng):
                    k0, k1 = i * gsz, (i + 1) * gsz
                    I(g.dma_start(
                        out=wo_bf[:, (mo % 2) * D + k0 * 128:(mo % 2) * D + k1 * 128]
                            .rearrange("p (ki c) -> p ki c", ki=gsz),
                        in_=dwo.ap()[k0 * 128:k1 * 128, mo * 128:(mo + 1) * 128]
                            .rearrange("(ki p) c -> p ki c", p=128)),
                      "dP%d" % (mo % 2), n=16)
                reg(("wL", mo), "dP%d" % (mo % 2))

        for mo in range(KT):
            @block.tensor
            def _(t, mo=mo):
                PW = PB[mo % 2]
                W(t, ("wL", mo), ("stt", mo - 2))
                if mo < 2:
                    W(t, *pair_free[mo % 2])
                mm = None
                for ki in range(KT):
                    for c0, cw in CH:
                        mm = t.matmul(arena[0:128, PW + c0:PW + c0 + cw],
                                      wo_bf[:, (mo % 2) * D + ki * 128:(mo % 2) * D + (ki + 1) * 128],
                                      aoT[:, ki * T + c0:ki * T + c0 + cw],
                                      start=(ki == 0), stop=(ki == KT - 1),
                                      skip_group_check=True)
                Ip(mm, "sT", ("woT", mo))

        @block.scalar
        def _(s):
            for mo in range(KT):
                sself(s)
                W(s, ("gld", mo), ("res", mo - 1))
                if mo == 0:
                    W(s, ("uptt", H - 1, NCH - 1))
                Ip(s.activation(g_f[:, :], gsl[:, (mo % 2) * T:(mo % 2 + 1) * T],
                                AF.Silu), "sS", ("silu", mo))

        @block.vector
        def _(v):
            for mo in range(KT):
                PW = PB[mo % 2]
                vself(v)
                W(v, ("woT", mo), ("silu", mo), ("out", mo - 2))
                Ip(v.scalar_tensor_tensor(ost[:, (mo % 2) * T:(mo % 2 + 1) * T],
                                          arena[0:128, PW:PW + T], bo_f[:, mo:mo + 1],
                                          g_f[:, :], ALU.add, ALU.mult), "sV", ("stt", mo))
                vself(v)
                W(v, ("xq", mo))
                Ip(v.tensor_tensor(ost[:, (mo % 2) * T:(mo % 2 + 1) * T],
                                   ost[:, (mo % 2) * T:(mo % 2 + 1) * T],
                                   g_f[:, :], ALU.add), "sV", ("res", mo))

        @block.sync
        def _(sy):
            def gld(mo):
                if mo >= KT:
                    return
                if mo >= 2:
                    W(sy, ("silu", mo - 2), ("fst", mo))
                else:
                    W(sy, ("fst", mo), ("dinv", H - 1), ("a2", H - 1, SKT - 1),
                      ("fst", H - 1), ("fst", H - 2), ("uptt", H - 1, NCH - 1))
                sgl = "dL1" if mo % 2 == 0 else "dG"
                inst = sy.dma_start(out=gsl[:, (mo % 2) * T:(mo % 2 + 1) * T],
                                    in_=dgs.ap()[mo * 128:(mo + 1) * 128, :])
                cnt[sgl] += 16
                inst.then_inc(sems[sgl], 16)
                assert EV[("gld", mo)] == (sgl, cnt[sgl]), (mo, EV[("gld", mo)], cnt[sgl])

            gld(0)
            gld(1)
            for mo in range(KT):
                W(sy, ("stt", mo))
                inst = sy.dma_start(out=g_f[:, :], in_=dx.ap()[mo * 128:(mo + 1) * 128, :])
                cnt["dL0"] += 16
                inst.then_inc(sems["dL0"], 16)
                assert EV[("xq", mo)] == ("dL0", cnt["dL0"])
                W(sy, ("res", mo))
                inst = sy.dma_start(out=dout.ap()[mo * 128:(mo + 1) * 128, :],
                                    in_=ost[:, (mo % 2) * T:(mo % 2 + 1) * T])
                so = "dO%d" % (mo % 2)
                cnt[so] += 16
                inst.then_inc(sems[so], 16)
                assert EV[("out", mo)] == (so, cnt[so])
                gld(mo + 2)

        @block.sync
        def _(sy):
            for s in ("dY", "dW", "dF", "dG", "dH", "dO0", "dO1", "dL0", "dL1", "dP0", "dP1",
                      "dX0", "dX1", "dX2", "dX3"):
                if cnt[s] > 0:
                    sy.wait_ge(sems[s], cnt[s])

        es3.close()
        es_ao.close()
        es_mid.close()

    return nc



def _host_layouts(cfg, inputs):
    D, T, H, DH = cfg["D"], cfg["T"], cfg["H"], cfg["DH"]
    QL, KL, E, R = cfg["QL"], cfg["KL"], cfg["E"], cfg["R"]
    KT = D // 128
    ER = E * R
    N1 = min(128, ER)
    N2 = ER - N1
    f32 = np.float32
    C = lambda a: np.ascontiguousarray(a, dtype=f32)

    w = {}
    w["wq"] = C(inputs["L0_attn_Wq_d"])
    w["wkv"] = C(inputs["L0_attn_Wkv_d"])
    w["wqk"] = C(inputs["L0_attn_W_qk"])
    w["wvu"] = C(inputs["L0_attn_Wv_u"])
    w["bqd"] = C(inputs["L0_attn_bq_d"].reshape(QL, 1))
    w["bkvd"] = C(inputs["L0_attn_bkv_d"].reshape(KL, 1))
    w["bqk"] = C(np.asarray(inputs["L0_attn_b_qk"]).reshape(H, KL).T)
    w["bvuf"] = C(np.asarray(inputs["L0_attn_bv_u"]).reshape(1, D))
    w["bo"] = C(np.asarray(inputs["L0_attn_bo"]).reshape(KT, 128).T)
    w["wo"] = C(inputs["L0_attn_Wo"])
    w["bwt"] = C(np.asarray(inputs["L0_ffn_base_W"]).T)
    w["gwt"] = C(np.asarray(inputs["L0_ffn_gate_W"]).T)
    w["lat"] = C(np.asarray(inputs["L0_ffn_lora_A"]).reshape(ER, D).T)
    w["lbt"] = C(np.asarray(inputs["L0_ffn_lora_B"]).transpose(0, 2, 1).reshape(ER, D))
    e_full = np.zeros((E, ER), dtype=f32)
    for e in range(E):
        e_full[e, e * R:(e + 1) * R] = 1.0
    w["e1"] = C(e_full[:, 0:N1])
    w["e2"] = C(e_full[:, N1:ER])
    w["idm"] = np.eye(128, dtype=f32)
    return w


def kernel(**inputs):
    cfg = FULL_CFG
    D, T = cfg["D"], cfg["T"]
    x = np.asarray(inputs["L0_attn_x"], dtype=np.float32)
    B, S, _ = x.shape
    n_cores = 8
    w = _host_layouts(cfg, inputs)

    nc = build(cfg)
    in_maps = []
    for c in range(n_cores):
        b, q = c // 2, c % 2
        own = np.ascontiguousarray(x[b, q * T:(q + 1) * T, :].T)
        oth = np.ascontiguousarray(x[b, (1 - q) * T:(2 - q) * T, :].T)
        m = dict(w)
        m["xT_own"] = own
        m["xT_oth"] = oth
        in_maps.append(m)

    res = run_bass_kernel_spmd(nc, in_maps, core_ids=list(range(n_cores)))
    out = np.empty((B, S, D), dtype=np.float32)
    for c in range(n_cores):
        b, q = c // 2, c % 2
        out[b, q * T:(q + 1) * T, :] = res.results[c]["outT"].T
    return out


# version 30
# speedup vs baseline: 1.0331x; 1.0176x over previous
import math
from contextlib import ExitStack

import numpy as np
import ml_dtypes

import concourse.bass as bass
import concourse.mybir as mybir
from concourse.bass_utils import run_bass_kernel_spmd

F32 = mybir.dt.float32
BF16 = mybir.dt.bfloat16
AF = mybir.ActivationFunctionType
ALU = mybir.AluOpType
AX = mybir.AxisListType
BFT = ml_dtypes.bfloat16

FULL_CFG = dict(D=4096, T=1024, H=32, DH=128, QL=12, KL=4, E=28, R=8)


def build(cfg):
    D, T, H, DH = cfg["D"], cfg["T"], cfg["H"], cfg["DH"]
    QL, KL, E, R = cfg["QL"], cfg["KL"], cfg["E"], cfg["R"]
    KT = D // 128
    SKT = (2 * T) // 128
    NT = T // 128
    ER = E * R
    N1 = min(128, ER)
    N2 = ER - N1
    CH = [(c0, min(512, T - c0)) for c0 in range(0, T, 512)]
    NCH = len(CH)
    QSC = float(1.0 / math.sqrt(KL))
    EPS = 1e-5

    AR = 4 * T + (512 if 4 * T + 512 <= 4096 else 0)
    CA, CB, CC, CD = 0, T, 2 * T, 3 * T
    KG = 2 * T
    A2C = 2 * T
    FC = 3 * T
    PB = [0, T]
    NPB = 4
    if SKT >= 16:
        QKT, BKT, UKT, DKT = 6, 8, 10, 5
    else:
        QKT, BKT, UKT, DKT = 3, 4, 5, 2

    nc = bass.Bass("TRN2", target_bir_lowering=False)

    dx = nc.dram_tensor("xT_own", [D, T], F32, kind="ExternalInput")
    dxo = nc.dram_tensor("xT_oth", [D, T], F32, kind="ExternalInput")
    dwq = nc.dram_tensor("wq", [D, QL], F32, kind="ExternalInput")
    dwkv = nc.dram_tensor("wkv", [D, KL], F32, kind="ExternalInput")
    dwqk = nc.dram_tensor("wqk", [QL, H * KL], F32, kind="ExternalInput")
    dwvu = nc.dram_tensor("wvu", [KL, D], F32, kind="ExternalInput")
    dbqd = nc.dram_tensor("bqd", [QL, 1], F32, kind="ExternalInput")
    dbkvd = nc.dram_tensor("bkvd", [KL, 1], F32, kind="ExternalInput")
    dbqk = nc.dram_tensor("bqk", [KL, H], F32, kind="ExternalInput")
    dbvu = nc.dram_tensor("bvuf", [1, D], F32, kind="ExternalInput")
    dbo = nc.dram_tensor("bo", [128, KT], F32, kind="ExternalInput")
    dwo = nc.dram_tensor("wo", [D, D], F32, kind="ExternalInput")
    dbwt = nc.dram_tensor("bwt", [D, D], F32, kind="ExternalInput")
    dgwt = nc.dram_tensor("gwt", [D, E], F32, kind="ExternalInput")
    dlat = nc.dram_tensor("lat", [D, ER], F32, kind="ExternalInput")
    dlbt = nc.dram_tensor("lbt", [ER, D], F32, kind="ExternalInput")
    de1 = nc.dram_tensor("e1", [E, N1], F32, kind="ExternalInput")
    de2 = nc.dram_tensor("e2", [E, N2], F32, kind="ExternalInput")
    did = nc.dram_tensor("idm", [128, 128], F32, kind="ExternalInput")
    dout = nc.dram_tensor("outT", [D, T], F32, kind="ExternalOutput")
    dgs = nc.dram_tensor("gscr", [D, T], mybir.dt.float16)

    es = ExitStack()
    with es:
        sbw = lambda st_, name, shape, dt: st_.enter_context(nc.sbuf_tensor(name, shape, dt))
        sbuf = lambda name, shape, dt: sbw(es, name, shape, dt)
        xnT = sbuf("xnT", [128, KT * T], BF16)
        bt1 = sbuf("bt1", [128, D], BF16)
        bt2 = sbuf("bt2", [N2, D], BF16)
        hg1 = sbuf("hg1", [128, T], BF16)
        hg2 = sbuf("hg2", [N2, T], BF16)
        bqk_f = sbuf("bqk_f", [KL, H], F32)

        bo_f = sbuf("bo_f", [128, KT], F32)
        bqd_f = sbuf("bqd_f", [QL, 1], F32)
        bkvd_f = sbuf("bkvd_f", [KL, 1], F32)
        ones_b = sbuf("ones_b", [128, 1], BF16)
        ones8 = sbuf("ones8", [128, 8], BF16)
        ones_f = sbuf("ones_f", [1, 128], F32)
        es_mid = ExitStack()
        cq_b = sbw(es_mid, "cq_b", [QL, T], BF16)
        ckv_b = sbw(es_mid, "ckv_b", [KL, 2 * T], BF16)
        kaug = sbw(es_mid, "kaug", [128, 8 * SKT], BF16)
        wvu8 = sbw(es_mid, "wvu8", [8, D], BF16)
        wqk_b = sbw(es_mid, "wqk_b", [QL, H * KL], BF16)
        rs_bf = sbw(es_mid, "rs_bf", [128, T], BF16)
        bw_bf = sbw(es_mid, "bw_bf", [128, 2 * D], BF16)
        es1 = ExitStack()
        id_f = sbw(es1, "id_f", [128, 128], F32)
        ckv_f = sbw(es1, "ckv_f", [KL, 2 * T], F32)
        rs_f = sbw(es1, "rs_f", [128, T], F32)
        mrs_f = sbw(es1, "mrs_f", [128, T], F32)
        rso_f = sbw(es1, "rso_f", [128, T], F32)
        srow = sbw(es1, "srow", [1, 3 * T], F32)
        xst = sbw(es1, "xst", [128, 8 * T], BF16)
        xsq = sbw(es1, "xsq", [128, 2 * T], BF16)
        tmpa = sbw(es1, "tmpa", [128, T], F32)
        mrsb = sbw(es1, "mrsb", [1, T], BF16)
        cs_kv = sbw(es1, "cs_kv", [1, KL], BF16)
        e1_b = sbw(es1, "e1_b", [E, N1], BF16)
        e2_b = sbw(es1, "e2_b", [E, N2], BF16)
        logits_f = sbw(es1, "logits_f", [E, T], F32)
        gt_b = sbw(es1, "gt_b", [E, T], BF16)
        lt_f = sbw(es1, "lt_f", [128, NT * E], F32)
        mk1 = sbw(es1, "mk1", [128, NT * E], F32)
        mk2 = sbw(es1, "mk2", [128, NT * E], F32)
        msk = sbw(es1, "msk", [128, NT * E], F32)
        gtk = sbw(es1, "gtk", [128, NT * E], F32)
        mrow = sbw(es1, "mrow", [128, 6 * NT], F32)
        gex1 = sbw(es1, "gex1", [128, T], BF16)
        gex2 = sbw(es1, "gex2", [N2, T], BF16)
        wq_b = sbw(es1, "wq_b", [128, KT * QL], BF16)
        wkv_b = sbw(es1, "wkv_b", [128, KT * KL], BF16)
        gwt_b = sbw(es1, "gwt_b", [128, KT * E], BF16)
        at1 = sbw(es1, "at1", [128, KT * N1], BF16)
        at2 = sbw(es1, "at2", [128, KT * N2], BF16)

        arena = es.enter_context(nc.psum_tensor("arena", [128, AR], F32))

        sems = {s: es.enter_context(nc.semaphore(s)) for s in
                ("dW", "dY", "dZ", "sT", "sV", "sS", "dF", "dG", "dH",
                 "dX0", "dX1", "dX2", "dX3", "dX4", "dX5", "dX6", "dX7",
                 "dL0", "dL1", "dO0", "dO1", "dP0", "dP1")}
        cnt = {s: 0 for s in sems}
        EV = {}
        block = es.enter_context(nc.Block())

        def reg(key, sem):
            EV[key] = (sem, cnt[sem])

        def I(instr, sem, key=None, n=1):
            cnt[sem] += n
            instr.then_inc(sems[sem], n)
            if key is not None:
                reg(key, sem)
            return instr

        def Ip(instr, sem, key):
            cnt[sem] += 1
            instr.then_inc(sems[sem], 1)
            assert EV[key] == (sem, cnt[sem]), (key, EV[key], (sem, cnt[sem]))

        def W(eng, *keys):
            best = {}
            for k in keys:
                if k is None or k not in EV:
                    continue
                s, v = EV[k]
                if v > 0:
                    best[s] = max(best.get(s, 0), v)
            for s, v in best.items():
                eng.wait_ge(sems[s], v)

        def vself(v):
            if cnt["sV"] > 0:
                v.wait_ge(sems["sV"], cnt["sV"])

        def sself(s):
            if cnt["sS"] > 0:
                s.wait_ge(sems["sS"], cnt["sS"])

        @block.vector
        def _(v):
            I(v.memset(ones_b[:, :], 1.0), "sV")
            I(v.memset(ones8[:, :], 1.0), "sV")
            I(v.memset(ones_f[:, :], 1.0), "sV")
            I(v.memset(wvu8[:, :], 0.0), "sV")
            I(v.memset(kaug[:, :], 0.0), "sV", ("init",))

        @block.sync
        def _(sy):
            for dsrc, ddst in ((did, id_f), (dbqd, bqd_f), (dbkvd, bkvd_f),
                               (dbqk, bqk_f), (dbo, bo_f)):
                I(sy.dma_start(out=ddst[:, :], in_=dsrc.ap()), "dY", n=16)
            reg(("smallY",), "dY")

        gsz = min(8, KT)
        ng = KT // gsz

        @block.gpsimd
        def _(g):
            for kt in range(min(8, KT)):
                I(g.dma_start(out=xnT[:, kt * T:(kt + 1) * T],
                              in_=dx.ap()[kt * 128:(kt + 1) * 128, :]),
                  "dX%d" % kt, ("so", "x", kt), n=16)

        for kt in range(KT):
            EV[("so", "sq", kt)] = ("sS", kt + 1)
        EV[("so", "rstd")] = ("sS", KT + 2)
        GEK = min(13, KT * 5 // 8 - 2) if KT >= 16 else None
        for kt in range(KT):
            off_g = 1 if (GEK is not None and kt > GEK) else 0
            EV[("st", "sq", kt)] = ("sS", KT + 2 + kt + 1 + off_g)
        if GEK is not None:
            EV[("gexp",)] = ("sS", KT + 4 + GEK)
            EV[("st", "rstd")] = ("sS", 2 * KT + 5)
        else:
            EV[("st", "rstd")] = ("sS", 2 * KT + 4)
            EV[("gexp",)] = ("sS", 2 * KT + 5)

        slot_ev = {}

        def stats_pass(bk, dsrc, ca, cbk, r2, first, extra=None, inject=None,
                       tinject=None):
            own = bk == "so"
            for kt in range(KT):
                sl = kt % 8
                buf = xnT if own else xst
                off = (kt if own else sl) * T
                if kt >= 8:
                    @block.gpsimd
                    def _(g, kt=kt, sl=sl, buf=buf, off=off):
                        if own:
                            W(g, (bk, "x", kt - 8))
                        else:
                            W(g, (bk, "x", kt - 8), (bk, "sT", kt - 8),
                              (bk, "sq", kt - 8))
                        I(g.dma_start(out=buf[:, off:off + T],
                                      in_=dsrc.ap()[kt * 128:(kt + 1) * 128, :]),
                          "dX%d" % (kt % 8), (bk, "x", kt), n=16)
                        if inject and kt in inject:
                            inject[kt](g)

                @block.tensor
                def _(t, kt=kt, buf=buf, off=off):
                    W(t, (bk, "x", kt), (bk, "sq", kt))
                    if kt == 0:
                        W(t, *first)
                    if extra is not None:
                        extra(t, kt, off)
                    for c0, cw in CH:
                        t.matmul(arena[0:8, ca + c0:ca + c0 + cw], ones8[:, :],
                                 buf[:, off + c0:off + c0 + cw],
                                 start=(kt == 0), stop=(kt == KT - 1),
                                 skip_group_check=True)
                    mm = None
                    for c0, cw in CH:
                        mm = t.matmul(arena[r2:r2 + 8, cbk + c0:cbk + c0 + cw],
                                      ones8[:, :],
                                      xsq[:, (kt % 2) * T + c0:(kt % 2) * T + c0 + cw],
                                      start=(kt == 0), stop=(kt == KT - 1),
                                      skip_group_check=True)
                    I(mm, "sT", (bk, "sT", kt))
                    if tinject and kt in tinject:
                        tinject[kt](t)

        def _w_wq(g):
            I(g.dma_start(out=wq_b.ap().rearrange("p (kt q) -> p kt q", kt=KT),
                          in_=dwq.ap().rearrange("(kt p) q -> p kt q", p=128)), "dW", n=16)

        def _w_wkv(g):
            I(g.dma_start(out=wkv_b.ap().rearrange("p (kt q) -> p kt q", kt=KT),
                          in_=dwkv.ap().rearrange("(kt p) q -> p kt q", p=128)), "dW", n=16)

        def _w_gwt(i):
            def f(g):
                k0, k1 = i * gsz, (i + 1) * gsz
                I(g.dma_start(
                    out=gwt_b[:, k0 * E:k1 * E].rearrange("p (kt e) -> p kt e", kt=gsz),
                    in_=dgwt.ap()[k0 * 128:k1 * 128, :].rearrange("(kt p) e -> p kt e", p=128)),
                  "dW", n=16)
            return f

        def _w_at1(i):
            def f(g):
                k0, k1 = i * gsz, (i + 1) * gsz
                I(g.dma_start(
                    out=at1[:, k0 * N1:k1 * N1].rearrange("p (kt e) -> p kt e", kt=gsz),
                    in_=dlat.ap()[k0 * 128:k1 * 128, 0:N1].rearrange("(kt p) e -> p kt e", p=128)),
                  "dW", n=16)
            return f

        def _w_at2(i, last=False):
            def f(g):
                k0, k1 = i * gsz, (i + 1) * gsz
                I(g.dma_start(
                    out=at2[:, k0 * N2:k1 * N2].rearrange("p (kt e) -> p kt e", kt=gsz),
                    in_=dlat.ap()[k0 * 128:k1 * 128, N1:ER].rearrange("(kt p) e -> p kt e", p=128)),
                  "dW", n=16)
                if last:
                    reg(("wA",), "dW")
            return f

        wparts = [_w_wq, _w_wkv] + [_w_gwt(i) for i in range(ng)] \
            + [_w_at1(i) for i in range(ng)] \
            + [_w_at2(i, last=(i == ng - 1)) for i in range(ng)]
        winj = {}
        if KT >= 16:
            step = max(1, (KT - 10) // len(wparts))
            for n_, fn in enumerate(wparts):
                winj[8 + n_ * step] = fn
            assert 8 + (len(wparts) - 1) * step < KT
        else:
            @block.gpsimd
            def _(g):
                for fn in wparts:
                    fn(g)

        stats_pass("so", dx, CA, CB, 0, (("init",),), inject=winj)


        @block.tensor
        def _(t):
            W(t, ("wA",))
            mm = None
            for kt in range(KT):
                mm = t.matmul(arena[0:8, CD:CD + KL], ones8[:, :],
                              wkv_b[:, kt * KL:(kt + 1) * KL],
                              start=(kt == 0), stop=(kt == KT - 1),
                              skip_group_check=True)
            I(mm, "sT", ("cskvT",))

        @block.vector
        def _(v):
            vself(v)
            W(v, ("cskvT",))
            I(v.tensor_copy(cs_kv[0:1, 0:KL], arena[0:1, CD:CD + KL]), "sV", ("cskv",))


        @block.vector
        def _(v):
            vself(v)
            W(v, ("so", "sT", KT - 1))
            I(v.tensor_scalar(srow[0:1, 0:T], arena[0:1, CA:CA + T],
                              1.0 / D, None, ALU.mult), "sV")
            vself(v)
            I(v.tensor_scalar(srow[0:1, T:2 * T], arena[0:1, CB:CB + T],
                              1.0 / D, EPS, ALU.mult, ALU.add), "sV")
            vself(v)
            I(v.tensor_tensor(srow[0:1, 2 * T:3 * T], srow[0:1, 0:T],
                              srow[0:1, 0:T], ALU.mult), "sV")
            vself(v)
            I(v.tensor_tensor(srow[0:1, 2 * T:3 * T], srow[0:1, T:2 * T],
                              srow[0:1, 2 * T:3 * T], ALU.subtract), "sV", ("so", "var"))

        @block.tensor
        def _(t):
            W(t, ("so", "rstd"))
            mm = None
            for c0, cw in CH:
                mm = t.matmul(arena[0:128, CA + c0:CA + c0 + cw], ones_f[:, :],
                              srow[0:1, 2 * T + c0:2 * T + c0 + cw],
                              start=True, stop=True, skip_group_check=True)
            I(mm, "sT", ("so", "bc1"))

        @block.vector
        def _(v):
            vself(v)
            W(v, ("so", "bc1"))
            I(v.tensor_copy(rs_f[:, :], arena[0:128, CA:CA + T]), "sV", ("so", "rsc"))
            vself(v)
            I(v.tensor_copy(rs_bf[:, :], arena[0:128, CA:CA + T]), "sV", ("rsbf",))

        @block.tensor
        def _(t):
            W(t, ("so", "rsc"), ("rsbf",))
            mm = None
            for c0, cw in CH:
                mm = t.matmul(arena[0:128, CA + c0:CA + c0 + cw], ones_f[:, :],
                              srow[0:1, c0:c0 + cw],
                              start=True, stop=True, skip_group_check=True)
            I(mm, "sT", ("so", "bc2"))

        @block.vector
        def _(v):
            vself(v)
            W(v, ("so", "bc2"))
            I(v.tensor_copy(mrs_f[:, :], arena[0:128, CA:CA + T]), "sV", ("so", "meanc"))

        for kt in range(KT):
            @block.vector
            def _(v, kt=kt):
                vself(v)
                W(v, ("so", "meanc"), ("so", "sT", kt), ("so", "sq", kt))
                I(v.scalar_tensor_tensor(xnT[:, kt * T:(kt + 1) * T],
                                         xnT[:, kt * T:(kt + 1) * T], 1.0,
                                         mrs_f[:, :], ALU.mult, ALU.subtract),
                  "sV", ("b", "xn", kt))

            @block.tensor
            def _(t, kt=kt):
                W(t, ("b", "xn", kt))
                if kt == 0:
                    W(t, ("cskv",))
                xn = xnT[:, kt * T:(kt + 1) * T]
                for c0, cw in CH:
                    t.matmul(arena[0:QL, CC + c0:CC + c0 + cw],
                             wq_b[:, kt * QL:(kt + 1) * QL], xn[:, c0:c0 + cw],
                             start=(kt == 0), stop=(kt == KT - 1), skip_group_check=True)
                for c0, cw in CH:
                    t.matmul(arena[32:32 + KL, CC + c0:CC + c0 + cw],
                             wkv_b[:, kt * KL:(kt + 1) * KL], xn[:, c0:c0 + cw],
                             start=(kt == 0), stop=(kt == KT - 1), skip_group_check=True)
                for c0, cw in CH:
                    t.matmul(arena[0:E, CA + c0:CA + c0 + cw],
                             gwt_b[:, kt * E:(kt + 1) * E], xn[:, c0:c0 + cw],
                             start=(kt == 0), stop=(kt == KT - 1), skip_group_check=True)
                for c0, cw in CH:
                    t.matmul(arena[0:N1, CB + c0:CB + c0 + cw],
                             at1[:, kt * N1:(kt + 1) * N1], xn[:, c0:c0 + cw],
                             start=(kt == 0), stop=(kt == KT - 1), skip_group_check=True)
                mm = None
                for c0, cw in CH:
                    mm = t.matmul(arena[0:N2, CD + c0:CD + c0 + cw],
                                  at2[:, kt * N2:(kt + 1) * N2], xn[:, c0:c0 + cw],
                                  start=(kt == 0), stop=(kt == KT - 1), skip_group_check=True)
                I(mm, "sT", ("b", "sT", kt))


        @block.gpsimd
        def _(g):
            for kt in range(min(8, KT)):
                W(g, ("so", "x", KT - 8 + kt),
                  ("so", "x", kt) if kt < KT else None)
                I(g.dma_start(out=xst[:, (kt % 8) * T:(kt % 8 + 1) * T],
                              in_=dxo.ap()[kt * 128:(kt + 1) * 128, :]),
                  "dX%d" % (kt % 8), ("st", "x", kt), n=16)

        @block.gpsimd
        def _(g):
            I(g.dma_start(out=wqk_b[:, :], in_=dwqk.ap()), "dZ", n=16)
            W(g, ("init",))
            I(g.dma_start(out=wvu8[2:6, :], in_=dwvu.ap()), "dZ", n=16)
            I(g.dma_start(out=wvu8[0:1, :], in_=dbvu.ap()), "dZ", n=16)
            I(g.dma_start(out=e1_b[:, :], in_=de1.ap()), "dZ", n=16)
            I(g.dma_start(out=e2_b[:, :], in_=de2.ap()), "dZ", n=16)
            I(g.dma_start(out=bt1[:, :], in_=dlbt.ap()[0:128, :]), "dZ", n=16)
            I(g.dma_start(out=bt2[:, :], in_=dlbt.ap()[128:ER, :]), "dZ", n=16)
            reg(("lbtW",), "dZ")

        def emit_bwload(mo):
            @block.gpsimd
            def _(g, mo=mo):
                if mo >= 2:
                    W(g, ("fT", mo - 2))
                for i in range(ng):
                    k0, k1 = i * gsz, (i + 1) * gsz
                    I(g.dma_start(
                        out=bw_bf[:, (mo % 2) * D + k0 * 128:(mo % 2) * D + k1 * 128]
                            .rearrange("p (ki c) -> p ki c", ki=gsz),
                        in_=dbwt.ap()[k0 * 128:k1 * 128, mo * 128:(mo + 1) * 128]
                            .rearrange("(ki p) c -> p ki c", p=128)),
                      "dP%d" % (mo % 2), n=16)
                reg(("wLb", mo), "dP%d" % (mo % 2))

        emit_bwload(0)
        emit_bwload(1)

        @block.vector
        def _(v):
            vself(v)
            W(v, ("b", "sT", KT - 1), ("smallY",))
            I(v.tensor_tensor(tmpa[0:QL, 0:T], arena[0:QL, CC:CC + T], rs_f[0:QL, :],
                              ALU.mult), "sV")
            vself(v)
            I(v.tensor_scalar(cq_b[:, :], tmpa[0:QL, 0:T], bqd_f[:, :], None,
                              ALU.add), "sV", ("cqev",))
            vself(v)
            I(v.tensor_tensor(ckv_f[0:KL, 0:T], arena[32:32 + KL, CC:CC + T],
                              rs_f[0:KL, :], ALU.mult), "sV")
            vself(v)
            I(v.tensor_scalar(ckv_f[0:KL, 0:T], ckv_f[0:KL, 0:T], bkvd_f[:, :], None,
                              ALU.add), "sV")
            vself(v)
            I(v.tensor_copy(ckv_b[:, 0:T], ckv_f[0:KL, 0:T]), "sV", ("ckev0",))
            vself(v)
            I(v.tensor_tensor(logits_f[:, :], arena[0:E, CA:CA + T], rs_f[0:E, :],
                              ALU.mult), "sV", ("lgev",))

        @block.tensor
        def _(t):
            W(t, ("lgev",))
            mm = None
            for tt in range(NT):
                mm = t.transpose(arena[0:128, CA + tt * E:CA + (tt + 1) * E],
                                 logits_f[0:E, tt * 128:(tt + 1) * 128], id_f[0:E, 0:E])
            I(mm, "sT", ("ltT",))

        @block.vector
        def _(v):
            vself(v)
            W(v, ("ltT",))
            I(v.tensor_copy(lt_f[:, :], arena[0:128, CA:CA + NT * E]), "sV", ("ltc",))
            for tt in range(NT):
                sl = slice(tt * E, (tt + 1) * E)
                vself(v)
                I(v.reduce_max(mrow[:, tt:tt + 1], lt_f[:, sl], axis=AX.X), "sV")
                vself(v)
                I(v.tensor_scalar(mk1[:, sl], lt_f[:, sl], mrow[:, tt:tt + 1], None,
                                  ALU.is_equal), "sV")
                vself(v)
                I(v.scalar_tensor_tensor(msk[:, sl], mk1[:, sl], -1e30, lt_f[:, sl],
                                         ALU.mult, ALU.add), "sV")
                vself(v)
                I(v.reduce_max(mrow[:, NT + tt:NT + tt + 1], msk[:, sl], axis=AX.X), "sV")
                vself(v)
                I(v.tensor_scalar(mk2[:, sl], msk[:, sl], mrow[:, NT + tt:NT + tt + 1], None,
                                  ALU.is_equal), "sV")
            vself(v)
            I(v.tensor_tensor(mrow[:, 4 * NT:5 * NT], mrow[:, 0:NT], mrow[:, NT:2 * NT],
                              ALU.subtract), "sV", ("gdel",))

        def ckv_oth_extra(t, kt, off):
            for c0, cw in CH:
                t.matmul(arena[32:32 + KL, CC + c0:CC + c0 + cw],
                         wkv_b[:, kt * KL:(kt + 1) * KL],
                         xst[:, off + c0:off + c0 + cw],
                         start=(kt == 0), stop=False, skip_group_check=True)

        def gating_v2a(v):
            vself(v)
            W(v, ("gexp",))
            I(v.tensor_scalar(mrow[:, 4 * NT:5 * NT], mrow[:, 5 * NT:6 * NT], 1.0, None,
                              ALU.add), "sV")
            vself(v)
            I(v.reciprocal(mrow[:, 2 * NT:3 * NT], mrow[:, 4 * NT:5 * NT]), "sV")
            vself(v)
            I(v.tensor_scalar(mrow[:, 3 * NT:4 * NT], mrow[:, 2 * NT:3 * NT], -2.0, 2.0,
                              ALU.mult, ALU.add), "sV")
            for tt in range(NT):
                sl = slice(tt * E, (tt + 1) * E)
                vself(v)
                I(v.tensor_scalar(msk[:, sl], mk1[:, sl], mrow[:, 2 * NT + tt:2 * NT + tt + 1],
                                  2.0, ALU.mult, ALU.mult), "sV")
                vself(v)
                I(v.scalar_tensor_tensor(gtk[:, sl], mk2[:, sl],
                                         mrow[:, 3 * NT + tt:3 * NT + tt + 1], msk[:, sl],
                                         ALU.mult, ALU.add), "sV")
            reg(("gtk",), "sV")

        def gating_v2b(v):
            vself(v)
            W(v, ("gtT",))
            I(v.tensor_copy(gt_b[:, :], arena[0:E, CA:CA + T]), "sV", ("gtc",))
            vself(v)
            W(v, ("gx1T",))
            I(v.tensor_copy(gex1[:, :], arena[0:N1, CA:CA + T]), "sV", ("gx1c",))
            vself(v)
            I(v.tensor_tensor(hg1[:, :], arena[0:N1, CB:CB + T], gex1[:, :], ALU.mult),
              "sV", ("hg1",))
            vself(v)
            W(v, ("gx2T",))
            I(v.tensor_copy(gex2[:, :], arena[0:N2, CB:CB + T]), "sV", ("gx2c",))
            vself(v)
            I(v.tensor_tensor(hg2[:, :], arena[0:N2, CD:CD + T], gex2[:, :], ALU.mult),
              "sV", ("hg2",))

        def gtT_fn(t):
            W(t, ("gtk",), ("ltc",))
            mm = None
            for tt in range(NT):
                mm = t.transpose(arena[0:E, CA + tt * 128:CA + (tt + 1) * 128],
                                 gtk[0:128, tt * E:(tt + 1) * E], id_f[:, :])
            I(mm, "sT", ("gtT",))

        def gx1_fn(t):
            W(t, ("gtc",), ("lbtW",))
            mm = None
            for c0, cw in CH:
                mm = t.matmul(arena[0:N1, CA + c0:CA + c0 + cw], e1_b[:, :],
                              gt_b[0:E, c0:c0 + cw], start=True, stop=True,
                              skip_group_check=True)
            I(mm, "sT", ("gx1T",))

        def gx2_fn(t):
            W(t, ("hg1",))
            mm = None
            for c0, cw in CH:
                mm = t.matmul(arena[0:N2, CB + c0:CB + c0 + cw], e2_b[:, :],
                              gt_b[0:E, c0:c0 + cw], start=True, stop=True,
                              skip_group_check=True)
            I(mm, "sT", ("gx2T",))

        gtinj = None
        if KT >= 16:
            k1_, k2_, k3_ = KT * 5 // 8, KT * 6 // 8, KT * 7 // 8
            gtinj = {k1_: gtT_fn, k2_: gx1_fn, k3_: gx2_fn}
            tb_ = cnt["sT"]
            EV[("gtT",)] = ("sT", tb_ + k1_ + 2)
            EV[("gx1T",)] = ("sT", tb_ + k2_ + 3)
            EV[("gx2T",)] = ("sT", tb_ + k3_ + 4)

            @block.vector
            def _(v):
                gating_v2a(v)

            @block.vector
            def _(v):
                gating_v2b(v)

        stats_pass("st", dxo, CC, CC, 64, (("cqev",), ("ckev0",)), extra=ckv_oth_extra, tinject=gtinj)

        @block.vector
        def _(v):
            vself(v)
            W(v, ("st", "sT", KT - 1), ("so", "bc2"))
            I(v.tensor_scalar(srow[0:1, 0:T], arena[0:1, CC:CC + T],
                              1.0 / D, None, ALU.mult), "sV")
            vself(v)
            I(v.tensor_scalar(srow[0:1, T:2 * T], arena[64:65, CC:CC + T],
                              1.0 / D, EPS, ALU.mult, ALU.add), "sV")
            vself(v)
            I(v.tensor_tensor(srow[0:1, 2 * T:3 * T], srow[0:1, 0:T],
                              srow[0:1, 0:T], ALU.mult), "sV")
            vself(v)
            I(v.tensor_tensor(srow[0:1, 2 * T:3 * T], srow[0:1, T:2 * T],
                              srow[0:1, 2 * T:3 * T], ALU.subtract), "sV", ("st", "var"))

        @block.vector
        def _(v):
            vself(v)
            W(v, ("st", "rstd"))
            I(v.scalar_tensor_tensor(srow[0:1, T:2 * T], srow[0:1, 0:T],
                                     -1.0, srow[0:1, 2 * T:3 * T], ALU.mult, ALU.mult),
              "sV")
            vself(v)
            I(v.tensor_copy(mrsb[0:1, 0:T], srow[0:1, T:2 * T]), "sV", ("mrsb",))

        @block.tensor
        def _(t):
            W(t, ("mrsb",))
            mm = None
            for ci, (c0, cw) in enumerate(CH):
                mm = t.matmul(arena[32:32 + KL, CC + c0:CC + c0 + cw],
                              cs_kv[0:1, 0:KL], mrsb[0:1, c0:c0 + cw],
                              start=False, stop=(ci == NCH - 1), skip_group_check=True)
            I(mm, "sT", ("ckoT",))

        if KT >= 16:
            assert EV[("gtT",)] == ("sT", tb_ + k1_ + 2)
            assert EV[("gx1T",)] == ("sT", tb_ + k2_ + 3)
            assert EV[("gx2T",)] == ("sT", tb_ + k3_ + 4)
        else:
            @block.vector
            def _(v):
                gating_v2a(v)

            @block.tensor
            def _(t):
                gtT_fn(t)

            @block.vector
            def _(v):
                vself(v)
                W(v, ("gtT",))
                I(v.tensor_copy(gt_b[:, :], arena[0:E, CA:CA + T]), "sV", ("gtc",))

            @block.tensor
            def _(t):
                gx1_fn(t)

            @block.vector
            def _(v):
                vself(v)
                W(v, ("gx1T",))
                I(v.tensor_copy(gex1[:, :], arena[0:N1, CA:CA + T]), "sV", ("gx1c",))
                vself(v)
                I(v.tensor_tensor(hg1[:, :], arena[0:N1, CB:CB + T], gex1[:, :],
                                  ALU.mult), "sV", ("hg1",))

            @block.tensor
            def _(t):
                gx2_fn(t)

            @block.vector
            def _(v):
                vself(v)
                W(v, ("gx2T",))
                I(v.tensor_copy(gex2[:, :], arena[0:N2, CB:CB + T]), "sV", ("gx2c",))
                vself(v)
                I(v.tensor_tensor(hg2[:, :], arena[0:N2, CD:CD + T], gex2[:, :],
                                  ALU.mult), "sV", ("hg2",))


        @block.tensor
        def _(t):
            W(t, ("st", "rstd"))
            mm = None
            for c0, cw in CH:
                mm = t.matmul(arena[64:64 + KL, CC + c0:CC + c0 + cw],
                              ones_f[0:1, 0:KL],
                              srow[0:1, 2 * T + c0:2 * T + c0 + cw],
                              start=True, stop=True, skip_group_check=True)
            I(mm, "sT", ("st", "bc1"))

        @block.vector
        def _(v):
            vself(v)
            W(v, ("st", "bc1"))
            I(v.tensor_copy(rso_f[0:KL, :], arena[64:64 + KL, CC:CC + T]),
              "sV", ("st", "rsc"))
            vself(v)
            W(v, ("ckoT",))
            I(v.tensor_tensor(ckv_f[0:KL, T:2 * T], arena[32:32 + KL, CC:CC + T],
                              rso_f[0:KL, :], ALU.mult), "sV")
            vself(v)
            I(v.tensor_scalar(ckv_f[0:KL, T:2 * T], ckv_f[0:KL, T:2 * T],
                              bkvd_f[:, :], None, ALU.add), "sV")
            vself(v)
            I(v.tensor_copy(ckv_b[:, T:2 * T], ckv_f[0:KL, T:2 * T]), "sV", ("ckev1",))

        @block.scalar
        def _(s):
            for bk in ("so", "st"):
                for kt in range(KT):
                    src_ = (xnT[:, kt * T:(kt + 1) * T] if bk == "so"
                            else xst[:, (kt % 8) * T:(kt % 8 + 1) * T])
                    war = (bk, "sT", kt - 2)
                    if kt < 2 and bk == "st":
                        war = ("so", "sT", KT - 2 + kt)
                    sself(s)
                    W(s, (bk, "x", kt), war)
                    Ip(s.activation(xsq[:, (kt % 2) * T:(kt % 2 + 1) * T],
                                    src_, AF.Square),
                       "sS", (bk, "sq", kt))
                    if bk == "st" and GEK is not None and kt == GEK:
                        sself(s)
                        W(s, ("gdel",))
                        Ip(s.activation(mrow[:, 5 * NT:6 * NT],
                                        mrow[:, 4 * NT:5 * NT], AF.Exp,
                                        scale=-1.0), "sS", ("gexp",))
                sself(s)
                W(s, (bk, "var"))
                I(s.activation(srow[0:1, T:2 * T], srow[0:1, 2 * T:3 * T], AF.Ln), "sS")
                sself(s)
                Ip(s.activation(srow[0:1, 2 * T:3 * T], srow[0:1, T:2 * T], AF.Exp,
                                scale=-0.5), "sS", (bk, "rstd"))

        if GEK is None:
            @block.scalar
            def _(s):
                sself(s)
                W(s, ("gdel",))
                Ip(s.activation(mrow[:, 5 * NT:6 * NT], mrow[:, 4 * NT:5 * NT],
                                AF.Exp, scale=-1.0), "sS", ("gexp",))

        @block.tensor
        def _(t):
            W(t, ("hg2",), ("ckev1",))
            mm = None
            for kt in range(SKT):
                mm = t.transpose(arena[0:128, KG + 4 * kt:KG + 4 * kt + 4],
                                 ckv_f[0:KL, kt * 128:(kt + 1) * 128], id_f[0:KL, 0:KL])
            I(mm, "sT", ("kgT",))

        @block.vector
        def _(v):
            vself(v)
            W(v, ("kgT",))
            I(v.tensor_copy(
                kaug.ap().rearrange("p (g c) -> p g c", g=SKT)[:, :, 2:6],
                arena[0:128, KG:KG + 4 * SKT].rearrange("p (g c) -> p g c", g=SKT)),
              "sV", ("kgc",))
            vself(v)
            I(v.memset(kaug.ap().rearrange("p (g c) -> p g c", g=SKT)[:, :, 0:1], 1.0),
              "sV", ("kgo",))

        es1.close()
        es_ao = ExitStack()
        aoT = sbw(es_ao, "aoT", [128, H * T], BF16)
        es_attn = ExitStack()
        pbuf = sbw(es_attn, "pbuf", [128, NPB * T], BF16)
        qb0 = sbw(es_attn, "qb0", [KL, T], BF16)
        qb1 = sbw(es_attn, "qb1", [KL, T], BF16)
        a2b0 = sbw(es_attn, "a2b0", [8, T], BF16)
        a2b1 = sbw(es_attn, "a2b1", [8, T], BF16)
        dinvb = sbw(es_attn, "dinvb", [128, T], F32)
        dtmp = dinvb
        dinv_bf = sbw(es_attn, "dinv_bf", [1, T], BF16)
        onesb = sbw(es_attn, "onesb", [1, 128], BF16)
        fsb = sbw(es_attn, "fsb", [128, T], mybir.dt.float16)
        qbs = [qb0, qb1]
        a2bs = [a2b0, a2b1]

        assert H == KT
        pair_free = {0: [("gx1c",)] * NCH, 1: [("gx2c",)] * NCH}

        spair = {}
        qloc = {}
        bcdloc = {}
        uploc = {}

        def a2block(t, hh, j, explicit_exp=False):
            if explicit_exp:
                W(t, ("exp", hh, j))
            u = hh * SKT + j
            mm = None
            for c, (c0, cw) in enumerate(CH):
                if j == 0:
                    if hh > 0:
                        W(t, ("a2s", hh - 1, c))
                    elif c == 0:
                        W(t, ("ckev1",), ("kgo",))
                mm = t.matmul(arena[0:8, A2C + c0:A2C + c0 + cw],
                              kaug[:, 8 * j:8 * j + 8],
                              pbuf[:, (u % NPB) * T + c0:(u % NPB) * T + c0 + cw],
                              start=(j == 0), stop=(j == SKT - 1),
                              skip_group_check=True)
            I(mm, "sT", ("a2", hh, j))

        def ffn_mms(mo):
            n = (KT + 2) * NCH
            i = 0
            for ki in range(KT):
                for c0, cw in CH:
                    yield (bw_bf[:, (mo % 2) * D + ki * 128:(mo % 2) * D + (ki + 1) * 128],
                           xnT[:, ki * T + c0:ki * T + c0 + cw], c0, cw,
                           ki == 0, False)
            for c0, cw in CH:
                yield (bt1[:, mo * 128:(mo + 1) * 128], hg1[:, c0:c0 + cw], c0, cw,
                       False, False)
            for ci, (c0, cw) in enumerate(CH):
                yield (bt2[0:N2, mo * 128:(mo + 1) * 128], hg2[0:N2, c0:c0 + cw], c0, cw,
                       False, ci == NCH - 1)

        def emit_bcd(t, hh, pb_):
            W(t, ("dinv", hh))
            for qc, (c0, cw) in enumerate(CH):
                W(t, pair_free[pb_][qc])
                I(t.matmul(arena[0:128, PB[pb_] + c0:PB[pb_] + c0 + cw], onesb[:, :],
                           dinv_bf[0:1, c0:c0 + cw], start=True, stop=True,
                           skip_group_check=True), "sT", ("bcd", hh, qc))
            bcdloc[hh] = PB[pb_]
            pair_free[pb_] = [("dcp", hh, qc) for qc in range(NCH)]

        def emit_up(t, hh, pu):
            W(t, ("a2s", hh, NCH - 1))
            for qc, (c0, cw) in enumerate(CH):
                W(t, pair_free[pu][qc])
                I(t.matmul(arena[0:128, PB[pu] + c0:PB[pu] + c0 + cw],
                           wvu8[0:8, hh * DH:(hh + 1) * DH],
                           a2bs[hh % 2][0:8, c0:c0 + cw], start=True, stop=True,
                           skip_group_check=True), "sT", ("up", hh, qc))
            uploc[hh] = PB[pu]
            pair_free[pu] = [("uptt", hh, qc) for qc in range(NCH)]

        def emit_vtail(v, hh):
            for qc, (c0, cw) in enumerate(CH):
                vself(v)
                W(v, ("bcd", hh, qc))
                Ip(v.tensor_copy(dinvb[:, c0:c0 + cw],
                                 arena[0:128, bcdloc[hh] + c0:bcdloc[hh] + c0 + cw]),
                   "sV", ("dcp", hh, qc))
            for qc, (c0, cw) in enumerate(CH):
                vself(v)
                W(v, ("up", hh, qc))
                Ip(v.tensor_tensor(aoT[:, hh * T + c0:hh * T + c0 + cw],
                                   arena[0:128, uploc[hh] + c0:uploc[hh] + c0 + cw],
                                   dinvb[:, c0:c0 + cw], ALU.mult), "sV", ("uptt", hh, qc))

        p0 = 1

        @block.tensor
        def _(t, p0=p0):
            W(t, ("hg2",), ("cqev",), ("lbtW",), *pair_free[p0])
            for ci, (c0, cw) in enumerate(CH):
                I(t.matmul(arena[0:KL, PB[p0] + c0:PB[p0] + c0 + cw], wqk_b[0:QL, 0:KL],
                           cq_b[0:QL, c0:c0 + cw], start=True, stop=True,
                           skip_group_check=True), "sT", ("q", 0, ci))
        qloc[0] = PB[p0]
        pair_free[p0] = [("qev", 0, ci) for ci in range(NCH)]

        @block.vector
        def _(v, p0=p0):
            for ci, (c0, cw) in enumerate(CH):
                vself(v)
                W(v, ("q", 0, ci))
                I(v.tensor_scalar(qbs[0][0:KL, c0:c0 + cw],
                                  arena[0:KL, PB[p0] + c0:PB[p0] + c0 + cw],
                                  bqk_f[:, 0:1], QSC, ALU.add, ALU.mult),
                  "sV", ("qev", 0, ci))
            vself(v)
            I(v.memset(onesb[:, :], 1.0), "sV", ("onesb",))

        for h in range(H):
            base_s = cnt["sS"]
            for kt in range(SKT):
                extra = 2 if (kt > DKT and h > 0) else 0
                EV[("exp", h, kt)] = ("sS", base_s + kt + 1 + extra)
            base_s += SKT + (2 if h > 0 else 0)
            cnt["sS"] = base_s
            if h == H - 1:
                EV[("dinv", h)] = ("sS", base_s + 2)
            else:
                EV[("dinv", h)] = ("sS", base_s + DKT + 3)
            off = cnt["sV"]
            if h > 0:
                for c in range(NCH):
                    EV[("a2s", h - 1, c)] = ("sV", off + c + 1)
                off += NCH
            if h + 1 < H:
                for ci in range(NCH):
                    EV[("qev", h + 1, ci)] = ("sV", off + ci + 1)
                off += NCH
            if h > 0:
                for qc in range(NCH):
                    EV[("dcp", h - 1, qc)] = ("sV", off + qc + 1)
                off += NCH
                for qc in range(NCH):
                    EV[("uptt", h - 1, qc)] = ("sV", off + qc + 1)
                off += NCH
            EV[("fcp", h)] = ("sV", off + 1)

            @block.tensor
            def _(t, h=h):
                fq = list(ffn_mms(h))
                fpos = [0]

                def emit_ffn(n):
                    while n > 0 and fpos[0] < len(fq):
                        lhsT, rhs, c0, cw, st_, sp_ = fq[fpos[0]]
                        if fpos[0] == 0:
                            W(t, ("wLb", h), ("fcp", h - 1),
                              ("hg2",) if h == 0 else None)
                        mm = t.matmul(arena[0:128, FC + c0:FC + c0 + cw], lhsT, rhs,
                                      start=st_, stop=sp_, skip_group_check=True)
                        if fpos[0] == len(fq) - 1:
                            I(mm, "sT", ("fT", h))
                        fpos[0] += 1
                        n -= 1

                for kt in range(SKT):
                    if kt < SKT - 2:
                        if h == 0 and kt < 3:
                            emit_ffn(10)
                        else:
                            emit_ffn(-(-(len(fq) - fpos[0]) // (SKT - 2 - kt)))
                    if kt < 3:
                        if h > 0:
                            a2block(t, h - 1, SKT - 3 + kt)
                    else:
                        a2block(t, h, kt - 3)
                    p = kt % 2
                    if kt == 0:
                        W(t, ("qev", h, NCH - 1))
                    spair[(h, kt)] = PB[p]
                    mm = None
                    for c, (c0, cw) in enumerate(CH):
                        W(t, pair_free[p][c])
                        mm = t.matmul(arena[0:128, PB[p] + c0:PB[p] + c0 + cw],
                                      ckv_b[0:KL, kt * 128:(kt + 1) * 128],
                                      qbs[h % 2][0:KL, c0:c0 + cw],
                                      start=True, stop=True, skip_group_check=True)
                    I(mm, "sT", ("sc", h, kt))
                    pair_free[p] = [("exp", h, kt)] * NCH
                    if kt == QKT and h + 1 < H:
                        pq = (kt + 1) % 2
                        for ci, (c0, cw) in enumerate(CH):
                            W(t, pair_free[pq][ci])
                            I(t.matmul(arena[0:KL, PB[pq] + c0:PB[pq] + c0 + cw],
                                       wqk_b[0:QL, (h + 1) * KL:(h + 2) * KL],
                                       cq_b[0:QL, c0:c0 + cw], start=True, stop=True,
                                       skip_group_check=True), "sT", ("q", h + 1, ci))
                        qloc[h + 1] = PB[pq]
                        pair_free[pq] = [("qev", h + 1, ci) for ci in range(NCH)]
                    if kt == BKT and h > 0:
                        emit_bcd(t, h - 1, (kt + 1) % 2)
                    if kt == UKT and h > 0:
                        emit_up(t, h - 1, (kt + 1) % 2)
                emit_ffn(len(fq))

            if h > 0:
                @block.gpsimd
                def _(g, h=h):
                    W(g, ("fcp", h - 1))
                    I(g.dma_start(out=dgs.ap()[(h - 1) * 128:h * 128, :], in_=fsb[:, :]),
                      "dF" if (h - 1) % 2 == 0 else "dH", ("fst", h - 1), n=16)

            @block.vector
            def _(v, h=h):
                if h > 0:
                    for c, (c0, cw) in enumerate(CH):
                        vself(v)
                        W(v, ("a2", h - 1, SKT - 1), ("up", h - 3, NCH - 1),
                          ("dinv", h - 3) if h >= 3 else None)
                        Ip(v.tensor_copy(a2bs[(h - 1) % 2][0:8, c0:c0 + cw],
                                         arena[0:8, A2C + c0:A2C + c0 + cw]),
                           "sV", ("a2s", h - 1, c))
                if h + 1 < H:
                    for ci, (c0, cw) in enumerate(CH):
                        vself(v)
                        W(v, ("q", h + 1, ci))
                        inst = v.tensor_scalar(qbs[(h + 1) % 2][0:KL, c0:c0 + cw],
                                               arena[0:KL, qloc[h + 1] + c0:qloc[h + 1] + c0 + cw],
                                               bqk_f[:, h + 1:h + 2], QSC, ALU.add, ALU.mult)
                        Ip(inst, "sV", ("qev", h + 1, ci))
                if h > 0:
                    emit_vtail(v, h - 1)
                vself(v)
                W(v, ("fT", h), ("fst", h - 1))
                Ip(v.tensor_tensor(fsb[:, :], arena[0:128, FC:FC + T], rs_bf[:, :],
                                   ALU.mult), "sV", ("fcp", h))

            if h + 2 < H:
                emit_bwload(h + 2)

        @block.tensor
        def _(t):
            for j in (SKT - 3, SKT - 2, SKT - 1):
                a2block(t, H - 1, j, explicit_exp=True)
            emit_bcd(t, H - 1, 0)
            emit_up(t, H - 1, 1)

        off = cnt["sV"]
        for c in range(NCH):
            EV[("a2s", H - 1, c)] = ("sV", off + c + 1)
        off += NCH
        for qc in range(NCH):
            EV[("dcp", H - 1, qc)] = ("sV", off + qc + 1)
        off += NCH
        for qc in range(NCH):
            EV[("uptt", H - 1, qc)] = ("sV", off + qc + 1)

        @block.vector
        def _(v):
            for c, (c0, cw) in enumerate(CH):
                vself(v)
                W(v, ("a2", H - 1, SKT - 1), ("up", H - 3, NCH - 1), ("dinv", H - 3))
                Ip(v.tensor_copy(a2bs[(H - 1) % 2][0:8, c0:c0 + cw],
                                 arena[0:8, A2C + c0:A2C + c0 + cw]),
                   "sV", ("a2s", H - 1, c))
            emit_vtail(v, H - 1)

        @block.gpsimd
        def _(g):
            W(g, ("fcp", H - 1))
            I(g.dma_start(out=dgs.ap()[(H - 1) * 128:H * 128, :], in_=fsb[:, :]),
              "dF" if (H - 1) % 2 == 0 else "dH", ("fst", H - 1), n=16)

        cnt["sS"] -= H * SKT + 2 * (H - 1)

        @block.scalar
        def _(s):
            for h in range(H):
                for kt in range(SKT):
                    u = h * SKT + kt
                    war = None
                    if u >= NPB:
                        up_, uk = divmod(u - NPB, SKT)
                        war = ("a2", up_, uk)
                    W(s, ("sc", h, kt), war)
                    sl = u % NPB
                    inst = s.activation(pbuf[:, sl * T:(sl + 1) * T],
                                        arena[0:128, spair[(h, kt)]:spair[(h, kt)] + T],
                                        AF.Exp)
                    Ip(inst, "sS", ("exp", h, kt))
                    if kt == DKT and h > 0:
                        W(s, ("a2s", h - 1, NCH - 1),
                          ("bcd", h - 2, NCH - 1) if h >= 2 else None)
                        I(s.activation(dtmp[0:1, 0:T], a2bs[(h - 1) % 2][0:1, 0:T],
                                       AF.Ln), "sS")
                        sself(s)
                        Ip(s.activation(dinv_bf[0:1, 0:T], dtmp[0:1, 0:T], AF.Exp,
                                        scale=-1.0), "sS", ("dinv", h - 1))
            W(s, ("a2s", H - 1, NCH - 1), ("bcd", H - 2, NCH - 1))
            I(s.activation(dtmp[0:1, 0:T], a2bs[(H - 1) % 2][0:1, 0:T], AF.Ln), "sS")
            sself(s)
            Ip(s.activation(dinv_bf[0:1, 0:T], dtmp[0:1, 0:T], AF.Exp, scale=-1.0),
               "sS", ("dinv", H - 1))

        es_attn.close()
        es3 = ExitStack()
        wo_bf = bw_bf
        gsl = sbw(es3, "gsl", [128, 2 * T], mybir.dt.float16)
        ost = sbw(es3, "ost", [128, 2 * T], F32)
        g_f = sbw(es3, "g_f", [128, T], F32)

        tb = cnt["sT"]
        vb = cnt["sV"]
        sb_ = cnt["sS"]
        lb = cnt["dL0"]
        gb = cnt["dL1"]
        gB = cnt["dG"]
        ob = {0: cnt["dO0"], 1: cnt["dO1"]}
        for mo in range(KT):
            EV[("woT", mo)] = ("sT", tb + mo + 1)
            EV[("silu", mo)] = ("sS", sb_ + mo + 1)
            EV[("stt", mo)] = ("sV", vb + 2 * mo + 1)
            EV[("res", mo)] = ("sV", vb + 2 * mo + 2)
            EV[("xq", mo)] = ("dL0", lb + 16 * (mo + 1))
            EV[("gld", mo)] = (("dL1", gb + 16 * (mo // 2 + 1)) if mo % 2 == 0
                               else ("dG", gB + 16 * (mo // 2 + 1)))
            EV[("out", mo)] = ("dO%d" % (mo % 2), ob[mo % 2] + 16 * (mo // 2 + 1))

        @block.gpsimd
        def _(g):
            for mo in range(KT):
                if mo >= 2:
                    W(g, ("woT", mo - 2))
                else:
                    W(g, ("fT", H - 2 + mo))
                for i in range(ng):
                    k0, k1 = i * gsz, (i + 1) * gsz
                    I(g.dma_start(
                        out=wo_bf[:, (mo % 2) * D + k0 * 128:(mo % 2) * D + k1 * 128]
                            .rearrange("p (ki c) -> p ki c", ki=gsz),
                        in_=dwo.ap()[k0 * 128:k1 * 128, mo * 128:(mo + 1) * 128]
                            .rearrange("(ki p) c -> p ki c", p=128)),
                      "dP%d" % (mo % 2), n=16)
                reg(("wL", mo), "dP%d" % (mo % 2))

        for mo in range(KT):
            @block.tensor
            def _(t, mo=mo):
                PW = PB[mo % 2]
                W(t, ("wL", mo), ("stt", mo - 2))
                if mo < 2:
                    W(t, ("uptt", H - 1, NCH - 1), *pair_free[mo % 2])
                mm = None
                for ki in range(KT):
                    for c0, cw in CH:
                        mm = t.matmul(arena[0:128, PW + c0:PW + c0 + cw],
                                      wo_bf[:, (mo % 2) * D + ki * 128:(mo % 2) * D + (ki + 1) * 128],
                                      aoT[:, ki * T + c0:ki * T + c0 + cw],
                                      start=(ki == 0), stop=(ki == KT - 1),
                                      skip_group_check=True)
                Ip(mm, "sT", ("woT", mo))

        @block.scalar
        def _(s):
            for mo in range(KT):
                sself(s)
                W(s, ("gld", mo), ("res", mo - 1))
                if mo == 0:
                    W(s, ("uptt", H - 1, NCH - 1))
                Ip(s.activation(g_f[:, :], gsl[:, (mo % 2) * T:(mo % 2 + 1) * T],
                                AF.Silu), "sS", ("silu", mo))

        @block.vector
        def _(v):
            for mo in range(KT):
                PW = PB[mo % 2]
                vself(v)
                W(v, ("woT", mo), ("silu", mo), ("out", mo - 2))
                Ip(v.scalar_tensor_tensor(ost[:, (mo % 2) * T:(mo % 2 + 1) * T],
                                          arena[0:128, PW:PW + T], bo_f[:, mo:mo + 1],
                                          g_f[:, :], ALU.add, ALU.mult), "sV", ("stt", mo))
                vself(v)
                W(v, ("xq", mo))
                Ip(v.tensor_tensor(ost[:, (mo % 2) * T:(mo % 2 + 1) * T],
                                   ost[:, (mo % 2) * T:(mo % 2 + 1) * T],
                                   g_f[:, :], ALU.add), "sV", ("res", mo))

        @block.sync
        def _(sy):
            def gld(mo):
                if mo >= KT:
                    return
                if mo >= 2:
                    W(sy, ("silu", mo - 2), ("fst", mo))
                else:
                    W(sy, ("fst", mo), ("dinv", H - 1), ("a2", H - 1, SKT - 1),
                      ("fst", H - 1), ("fst", H - 2), ("uptt", H - 1, NCH - 1))
                sgl = "dL1" if mo % 2 == 0 else "dG"
                inst = sy.dma_start(out=gsl[:, (mo % 2) * T:(mo % 2 + 1) * T],
                                    in_=dgs.ap()[mo * 128:(mo + 1) * 128, :])
                cnt[sgl] += 16
                inst.then_inc(sems[sgl], 16)
                assert EV[("gld", mo)] == (sgl, cnt[sgl]), (mo, EV[("gld", mo)], cnt[sgl])

            gld(0)
            gld(1)
            for mo in range(KT):
                W(sy, ("stt", mo))
                inst = sy.dma_start(out=g_f[:, :], in_=dx.ap()[mo * 128:(mo + 1) * 128, :])
                cnt["dL0"] += 16
                inst.then_inc(sems["dL0"], 16)
                assert EV[("xq", mo)] == ("dL0", cnt["dL0"])
                W(sy, ("res", mo))
                inst = sy.dma_start(out=dout.ap()[mo * 128:(mo + 1) * 128, :],
                                    in_=ost[:, (mo % 2) * T:(mo % 2 + 1) * T])
                so = "dO%d" % (mo % 2)
                cnt[so] += 16
                inst.then_inc(sems[so], 16)
                assert EV[("out", mo)] == (so, cnt[so])
                gld(mo + 2)

        @block.sync
        def _(sy):
            for s in ("dY", "dW", "dF", "dG", "dH", "dO0", "dO1", "dL0", "dL1", "dP0", "dP1",
                      "dX0", "dX1", "dX2", "dX3"):
                if cnt[s] > 0:
                    sy.wait_ge(sems[s], cnt[s])

        es3.close()
        es_ao.close()
        es_mid.close()

    return nc



def _host_layouts(cfg, inputs):
    D, T, H, DH = cfg["D"], cfg["T"], cfg["H"], cfg["DH"]
    QL, KL, E, R = cfg["QL"], cfg["KL"], cfg["E"], cfg["R"]
    KT = D // 128
    ER = E * R
    N1 = min(128, ER)
    N2 = ER - N1
    f32 = np.float32
    C = lambda a: np.ascontiguousarray(a, dtype=f32)

    w = {}
    w["wq"] = C(inputs["L0_attn_Wq_d"])
    w["wkv"] = C(inputs["L0_attn_Wkv_d"])
    w["wqk"] = C(inputs["L0_attn_W_qk"])
    w["wvu"] = C(inputs["L0_attn_Wv_u"])
    w["bqd"] = C(inputs["L0_attn_bq_d"].reshape(QL, 1))
    w["bkvd"] = C(inputs["L0_attn_bkv_d"].reshape(KL, 1))
    w["bqk"] = C(np.asarray(inputs["L0_attn_b_qk"]).reshape(H, KL).T)
    w["bvuf"] = C(np.asarray(inputs["L0_attn_bv_u"]).reshape(1, D))
    w["bo"] = C(np.asarray(inputs["L0_attn_bo"]).reshape(KT, 128).T)
    w["wo"] = C(inputs["L0_attn_Wo"])
    w["bwt"] = C(np.asarray(inputs["L0_ffn_base_W"]).T)
    w["gwt"] = C(np.asarray(inputs["L0_ffn_gate_W"]).T)
    w["lat"] = C(np.asarray(inputs["L0_ffn_lora_A"]).reshape(ER, D).T)
    w["lbt"] = C(np.asarray(inputs["L0_ffn_lora_B"]).transpose(0, 2, 1).reshape(ER, D))
    e_full = np.zeros((E, ER), dtype=f32)
    for e in range(E):
        e_full[e, e * R:(e + 1) * R] = 1.0
    w["e1"] = C(e_full[:, 0:N1])
    w["e2"] = C(e_full[:, N1:ER])
    w["idm"] = np.eye(128, dtype=f32)
    return w


def kernel(**inputs):
    cfg = FULL_CFG
    D, T = cfg["D"], cfg["T"]
    x = np.asarray(inputs["L0_attn_x"], dtype=np.float32)
    B, S, _ = x.shape
    n_cores = 8
    w = _host_layouts(cfg, inputs)

    nc = build(cfg)
    in_maps = []
    for c in range(n_cores):
        b, q = c // 2, c % 2
        own = np.ascontiguousarray(x[b, q * T:(q + 1) * T, :].T)
        oth = np.ascontiguousarray(x[b, (1 - q) * T:(2 - q) * T, :].T)
        m = dict(w)
        m["xT_own"] = own
        m["xT_oth"] = oth
        in_maps.append(m)

    res = run_bass_kernel_spmd(nc, in_maps, core_ids=list(range(n_cores)))
    out = np.empty((B, S, D), dtype=np.float32)
    for c in range(n_cores):
        b, q = c // 2, c % 2
        out[b, q * T:(q + 1) * T, :] = res.results[c]["outT"].T
    return out


# version 31
# speedup vs baseline: 1.2311x; 1.1917x over previous
import math
from contextlib import ExitStack

import numpy as np
import ml_dtypes

import concourse.bass as bass
import concourse.mybir as mybir
from concourse.bass_utils import run_bass_kernel_spmd

F32 = mybir.dt.float32
BF16 = mybir.dt.bfloat16
AF = mybir.ActivationFunctionType
ALU = mybir.AluOpType
AX = mybir.AxisListType
BFT = ml_dtypes.bfloat16

FULL_CFG = dict(D=4096, T=1024, H=32, DH=128, QL=12, KL=4, E=28, R=8)


def build(cfg):
    D, T, H, DH = cfg["D"], cfg["T"], cfg["H"], cfg["DH"]
    QL, KL, E, R = cfg["QL"], cfg["KL"], cfg["E"], cfg["R"]
    KT = D // 128
    SKT = (2 * T) // 128
    NT = T // 128
    ER = E * R
    N1 = min(128, ER)
    N2 = ER - N1
    CH = [(c0, min(512, T - c0)) for c0 in range(0, T, 512)]
    NCH = len(CH)
    QSC = float(1.0 / math.sqrt(KL))
    EPS = 1e-5

    AR = 4 * T + (512 if 4 * T + 512 <= 4096 else 0)
    CA, CB, CC, CD = 0, T, 2 * T, 3 * T
    KG = 2 * T
    A2C = 2 * T
    FC = 3 * T
    PB = [0, T]
    NPB = 4
    if SKT >= 16:
        QKT, BKT, UKT, DKT = 6, 8, 10, 5
    else:
        QKT, BKT, UKT, DKT = 3, 4, 5, 2

    nc = bass.Bass("TRN2", target_bir_lowering=False)

    dx = nc.dram_tensor("xT_own", [D, T], F32, kind="ExternalInput")
    dxo = nc.dram_tensor("xT_oth", [D, T], F32, kind="ExternalInput")
    dwq = nc.dram_tensor("wq", [D, QL], F32, kind="ExternalInput")
    dwkv = nc.dram_tensor("wkv", [D, KL], F32, kind="ExternalInput")
    dwqk = nc.dram_tensor("wqk", [QL, H * KL], F32, kind="ExternalInput")
    dwvu = nc.dram_tensor("wvu", [KL, D], F32, kind="ExternalInput")
    dbqd = nc.dram_tensor("bqd", [QL, 1], F32, kind="ExternalInput")
    dbkvd = nc.dram_tensor("bkvd", [KL, 1], F32, kind="ExternalInput")
    dbqk = nc.dram_tensor("bqk", [KL, H], F32, kind="ExternalInput")
    dbvu = nc.dram_tensor("bvuf", [1, D], F32, kind="ExternalInput")
    dbo = nc.dram_tensor("bo", [128, KT], F32, kind="ExternalInput")
    dwo = nc.dram_tensor("wo", [D, D], F32, kind="ExternalInput")
    dbwt = nc.dram_tensor("bwt", [D, D], F32, kind="ExternalInput")
    dgwt = nc.dram_tensor("gwt", [D, E], F32, kind="ExternalInput")
    dlat = nc.dram_tensor("lat", [D, ER], F32, kind="ExternalInput")
    dlbt = nc.dram_tensor("lbt", [ER, D], F32, kind="ExternalInput")
    de1 = nc.dram_tensor("e1", [E, N1], F32, kind="ExternalInput")
    de2 = nc.dram_tensor("e2", [E, N2], F32, kind="ExternalInput")
    did = nc.dram_tensor("idm", [128, 128], F32, kind="ExternalInput")
    dout = nc.dram_tensor("outT", [D, T], F32, kind="ExternalOutput")
    dgs = nc.dram_tensor("gscr", [D, T], mybir.dt.float16)

    es = ExitStack()
    with es:
        sbw = lambda st_, name, shape, dt: st_.enter_context(nc.sbuf_tensor(name, shape, dt))
        sbuf = lambda name, shape, dt: sbw(es, name, shape, dt)
        xnT = sbuf("xnT", [128, KT * T], BF16)
        bt1 = sbuf("bt1", [128, D], BF16)
        bt2 = sbuf("bt2", [N2, D], BF16)
        hg1 = sbuf("hg1", [128, T], BF16)
        hg2 = sbuf("hg2", [N2, T], BF16)
        bqk_f = sbuf("bqk_f", [KL, H], F32)

        bo_f = sbuf("bo_f", [128, KT], F32)
        bqd_f = sbuf("bqd_f", [QL, 1], F32)
        bkvd_f = sbuf("bkvd_f", [KL, 1], F32)
        ones_b = sbuf("ones_b", [128, 1], BF16)
        ones8 = sbuf("ones8", [128, 8], BF16)
        ones_f = sbuf("ones_f", [1, 128], F32)
        es_mid = ExitStack()
        cq_b = sbw(es_mid, "cq_b", [QL, T], BF16)
        ckv_b = sbw(es_mid, "ckv_b", [KL, 2 * T], BF16)
        kaug = sbw(es_mid, "kaug", [128, 8 * SKT], BF16)
        wvu8 = sbw(es_mid, "wvu8", [8, D], BF16)
        wqk_b = sbw(es_mid, "wqk_b", [QL, H * KL], BF16)
        rs_bf = sbw(es_mid, "rs_bf", [128, T], BF16)
        bw_bf = sbw(es_mid, "bw_bf", [128, 2 * D], BF16)
        es1 = ExitStack()
        id_f = sbw(es1, "id_f", [128, 128], F32)
        ckv_f = sbw(es1, "ckv_f", [KL, 2 * T], F32)
        rs_f = sbw(es1, "rs_f", [128, T], F32)
        mrs_f = sbw(es1, "mrs_f", [128, T], F32)
        rso_f = sbw(es1, "rso_f", [128, T], F32)
        srow = sbw(es1, "srow", [1, 3 * T], F32)
        xst = sbw(es1, "xst", [128, 8 * T], BF16)
        xsq = sbw(es1, "xsq", [128, 2 * T], BF16)
        tmpa = sbw(es1, "tmpa", [128, T], F32)
        mrsb = sbw(es1, "mrsb", [1, T], BF16)
        cs_kv = sbw(es1, "cs_kv", [1, KL], BF16)
        e1_b = sbw(es1, "e1_b", [E, N1], BF16)
        e2_b = sbw(es1, "e2_b", [E, N2], BF16)
        logits_f = sbw(es1, "logits_f", [E, T], F32)
        gt_b = sbw(es1, "gt_b", [E, T], BF16)
        lt_f = sbw(es1, "lt_f", [128, NT * E], F32)
        mk1 = sbw(es1, "mk1", [128, NT * E], F32)
        mk2 = sbw(es1, "mk2", [128, NT * E], F32)
        msk = sbw(es1, "msk", [128, NT * E], F32)
        gtk = sbw(es1, "gtk", [128, NT * E], F32)
        mrow = sbw(es1, "mrow", [128, 6 * NT], F32)
        gex1 = sbw(es1, "gex1", [128, T], BF16)
        gex2 = sbw(es1, "gex2", [N2, T], BF16)
        wq_b = sbw(es1, "wq_b", [128, KT * QL], BF16)
        wkv_b = sbw(es1, "wkv_b", [128, KT * KL], BF16)
        gwt_b = sbw(es1, "gwt_b", [128, KT * E], BF16)
        at1 = sbw(es1, "at1", [128, KT * N1], BF16)
        at2 = sbw(es1, "at2", [128, KT * N2], BF16)

        arena = es.enter_context(nc.psum_tensor("arena", [128, AR], F32))

        sems = {s: es.enter_context(nc.semaphore(s)) for s in
                ("dW", "dY", "dZ", "sT", "sV", "sS", "dF", "dG", "dH",
                 "dX0", "dX1", "dX2", "dX3", "dX4", "dX5", "dX6", "dX7",
                 "dL0", "dL1", "dO0", "dO1", "dP0", "dP1")}
        cnt = {s: 0 for s in sems}
        EV = {}
        block = es.enter_context(nc.Block())

        def reg(key, sem):
            EV[key] = (sem, cnt[sem])

        def I(instr, sem, key=None, n=1):
            cnt[sem] += n
            instr.then_inc(sems[sem], n)
            if key is not None:
                reg(key, sem)
            return instr

        def Ip(instr, sem, key):
            cnt[sem] += 1
            instr.then_inc(sems[sem], 1)
            assert EV[key] == (sem, cnt[sem]), (key, EV[key], (sem, cnt[sem]))

        def W(eng, *keys):
            best = {}
            for k in keys:
                if k is None or k not in EV:
                    continue
                s, v = EV[k]
                if v > 0:
                    best[s] = max(best.get(s, 0), v)
            for s, v in best.items():
                eng.wait_ge(sems[s], v)

        def vself(v):
            if cnt["sV"] > 0:
                v.wait_ge(sems["sV"], cnt["sV"])

        def sself(s):
            if cnt["sS"] > 0:
                s.wait_ge(sems["sS"], cnt["sS"])

        @block.vector
        def _(v):
            I(v.memset(ones_b[:, :], 1.0), "sV")
            I(v.memset(ones8[:, :], 1.0), "sV")
            I(v.memset(ones_f[:, :], 1.0), "sV")
            I(v.memset(wvu8[:, :], 0.0), "sV")
            I(v.memset(kaug[:, :], 0.0), "sV", ("init",))

        @block.sync
        def _(sy):
            for dsrc, ddst in ((did, id_f), (dbqd, bqd_f), (dbkvd, bkvd_f),
                               (dbqk, bqk_f), (dbo, bo_f)):
                I(sy.dma_start(out=ddst[:, :], in_=dsrc.ap()), "dY", n=16)
            reg(("smallY",), "dY")

        gsz = min(8, KT)
        ng = KT // gsz

        @block.gpsimd
        def _(g):
            for kt in range(min(8, KT)):
                I(g.dma_start(out=xnT[:, kt * T:(kt + 1) * T],
                              in_=dx.ap()[kt * 128:(kt + 1) * 128, :]),
                  "dX%d" % kt, ("so", "x", kt), n=16)

        for kt in range(KT):
            EV[("so", "sq", kt)] = ("sS", kt + 1)
        EV[("so", "rstd")] = ("sS", KT + 2)
        GEK = min(13, KT * 5 // 8 - 2) if KT >= 16 else None
        for kt in range(KT):
            off_g = 1 if (GEK is not None and kt > GEK) else 0
            EV[("st", "sq", kt)] = ("sS", KT + 2 + kt + 1 + off_g)
        if GEK is not None:
            EV[("gexp",)] = ("sS", KT + 4 + GEK)
            EV[("st", "rstd")] = ("sS", 2 * KT + 5)
        else:
            EV[("st", "rstd")] = ("sS", 2 * KT + 4)
            EV[("gexp",)] = ("sS", 2 * KT + 5)

        slot_ev = {}

        def stats_pass(bk, dsrc, ca, cbk, r2, first, extra=None, inject=None,
                       tinject=None):
            own = bk == "so"
            for kt in range(KT):
                sl = kt % 8
                buf = xnT if own else xst
                off = (kt if own else sl) * T
                if kt >= 8:
                    @block.gpsimd
                    def _(g, kt=kt, sl=sl, buf=buf, off=off):
                        if own:
                            W(g, (bk, "x", kt - 8))
                        else:
                            W(g, (bk, "x", kt - 8), (bk, "sT", kt - 8),
                              (bk, "sq", kt - 8))
                        I(g.dma_start(out=buf[:, off:off + T],
                                      in_=dsrc.ap()[kt * 128:(kt + 1) * 128, :]),
                          "dX%d" % (kt % 8), (bk, "x", kt), n=16)
                        if inject and kt in inject:
                            inject[kt](g)

                @block.tensor
                def _(t, kt=kt, buf=buf, off=off):
                    W(t, (bk, "x", kt), (bk, "sq", kt))
                    if kt == 0:
                        W(t, *first)
                    if extra is not None:
                        extra(t, kt, off)
                    for c0, cw in CH:
                        t.matmul(arena[0:8, ca + c0:ca + c0 + cw], ones8[:, :],
                                 buf[:, off + c0:off + c0 + cw],
                                 start=(kt == 0), stop=(kt == KT - 1),
                                 skip_group_check=True)
                    mm = None
                    for c0, cw in CH:
                        mm = t.matmul(arena[r2:r2 + 8, cbk + c0:cbk + c0 + cw],
                                      ones8[:, :],
                                      xsq[:, (kt % 2) * T + c0:(kt % 2) * T + c0 + cw],
                                      start=(kt == 0), stop=(kt == KT - 1),
                                      skip_group_check=True)
                    I(mm, "sT", (bk, "sT", kt))
                    if tinject and kt in tinject:
                        tinject[kt](t)

        def _w_wq(g):
            I(g.dma_start(out=wq_b.ap().rearrange("p (kt q) -> p kt q", kt=KT),
                          in_=dwq.ap().rearrange("(kt p) q -> p kt q", p=128)), "dW", n=16)

        def _w_wkv(g):
            I(g.dma_start(out=wkv_b.ap().rearrange("p (kt q) -> p kt q", kt=KT),
                          in_=dwkv.ap().rearrange("(kt p) q -> p kt q", p=128)), "dW", n=16)

        def _w_gwt(i):
            def f(g):
                k0, k1 = i * gsz, (i + 1) * gsz
                I(g.dma_start(
                    out=gwt_b[:, k0 * E:k1 * E].rearrange("p (kt e) -> p kt e", kt=gsz),
                    in_=dgwt.ap()[k0 * 128:k1 * 128, :].rearrange("(kt p) e -> p kt e", p=128)),
                  "dW", n=16)
            return f

        def _w_at1(i):
            def f(g):
                k0, k1 = i * gsz, (i + 1) * gsz
                I(g.dma_start(
                    out=at1[:, k0 * N1:k1 * N1].rearrange("p (kt e) -> p kt e", kt=gsz),
                    in_=dlat.ap()[k0 * 128:k1 * 128, 0:N1].rearrange("(kt p) e -> p kt e", p=128)),
                  "dW", n=16)
            return f

        def _w_at2(i, last=False):
            def f(g):
                k0, k1 = i * gsz, (i + 1) * gsz
                I(g.dma_start(
                    out=at2[:, k0 * N2:k1 * N2].rearrange("p (kt e) -> p kt e", kt=gsz),
                    in_=dlat.ap()[k0 * 128:k1 * 128, N1:ER].rearrange("(kt p) e -> p kt e", p=128)),
                  "dW", n=16)
                if last:
                    reg(("wA",), "dW")
            return f

        wparts = [_w_wq, _w_wkv] + [_w_gwt(i) for i in range(ng)] \
            + [_w_at1(i) for i in range(ng)] \
            + [_w_at2(i, last=(i == ng - 1)) for i in range(ng)]
        winj = {}
        if KT >= 16:
            step = max(1, (KT - 10) // len(wparts))
            for n_, fn in enumerate(wparts):
                winj[8 + n_ * step] = fn
            assert 8 + (len(wparts) - 1) * step < KT
        else:
            @block.gpsimd
            def _(g):
                for fn in wparts:
                    fn(g)

        stats_pass("so", dx, CA, CB, 0, (("init",),), inject=winj)


        @block.tensor
        def _(t):
            W(t, ("wA",))
            mm = None
            for kt in range(KT):
                mm = t.matmul(arena[0:8, CD:CD + KL], ones8[:, :],
                              wkv_b[:, kt * KL:(kt + 1) * KL],
                              start=(kt == 0), stop=(kt == KT - 1),
                              skip_group_check=True)
            I(mm, "sT", ("cskvT",))

        @block.vector
        def _(v):
            vself(v)
            W(v, ("cskvT",))
            I(v.tensor_copy(cs_kv[0:1, 0:KL], arena[0:1, CD:CD + KL]), "sV", ("cskv",))


        @block.vector
        def _(v):
            vself(v)
            W(v, ("so", "sT", KT - 1))
            I(v.tensor_scalar(srow[0:1, 0:T], arena[0:1, CA:CA + T],
                              1.0 / D, None, ALU.mult), "sV")
            vself(v)
            I(v.tensor_scalar(srow[0:1, T:2 * T], arena[0:1, CB:CB + T],
                              1.0 / D, EPS, ALU.mult, ALU.add), "sV")
            vself(v)
            I(v.tensor_tensor(srow[0:1, 2 * T:3 * T], srow[0:1, 0:T],
                              srow[0:1, 0:T], ALU.mult), "sV")
            vself(v)
            I(v.tensor_tensor(srow[0:1, 2 * T:3 * T], srow[0:1, T:2 * T],
                              srow[0:1, 2 * T:3 * T], ALU.subtract), "sV", ("so", "var"))

        @block.tensor
        def _(t):
            W(t, ("so", "rstd"))
            mm = None
            for c0, cw in CH:
                mm = t.matmul(arena[0:128, CA + c0:CA + c0 + cw], ones_f[:, :],
                              srow[0:1, 2 * T + c0:2 * T + c0 + cw],
                              start=True, stop=True, skip_group_check=True)
            I(mm, "sT", ("so", "bc1"))

        @block.vector
        def _(v):
            vself(v)
            W(v, ("so", "bc1"))
            I(v.tensor_copy(rs_f[:, :], arena[0:128, CA:CA + T]), "sV", ("so", "rsc"))
            vself(v)
            I(v.tensor_copy(rs_bf[:, :], arena[0:128, CA:CA + T]), "sV", ("rsbf",))

        @block.tensor
        def _(t):
            W(t, ("so", "rsc"), ("rsbf",))
            mm = None
            for c0, cw in CH:
                mm = t.matmul(arena[0:128, CA + c0:CA + c0 + cw], ones_f[:, :],
                              srow[0:1, c0:c0 + cw],
                              start=True, stop=True, skip_group_check=True)
            I(mm, "sT", ("so", "bc2"))

        @block.vector
        def _(v):
            vself(v)
            W(v, ("so", "bc2"))
            I(v.tensor_copy(mrs_f[:, :], arena[0:128, CA:CA + T]), "sV", ("so", "meanc"))

        for kt in range(KT):
            @block.vector
            def _(v, kt=kt):
                vself(v)
                W(v, ("so", "meanc"), ("so", "sT", kt), ("so", "sq", kt))
                I(v.scalar_tensor_tensor(xnT[:, kt * T:(kt + 1) * T],
                                         xnT[:, kt * T:(kt + 1) * T], 1.0,
                                         mrs_f[:, :], ALU.mult, ALU.subtract),
                  "sV", ("b", "xn", kt))

            @block.tensor
            def _(t, kt=kt):
                W(t, ("b", "xn", kt))
                if kt == 0:
                    W(t, ("cskv",))
                xn = xnT[:, kt * T:(kt + 1) * T]
                for c0, cw in CH:
                    t.matmul(arena[0:QL, CC + c0:CC + c0 + cw],
                             wq_b[:, kt * QL:(kt + 1) * QL], xn[:, c0:c0 + cw],
                             start=(kt == 0), stop=(kt == KT - 1), skip_group_check=True)
                for c0, cw in CH:
                    t.matmul(arena[32:32 + KL, CC + c0:CC + c0 + cw],
                             wkv_b[:, kt * KL:(kt + 1) * KL], xn[:, c0:c0 + cw],
                             start=(kt == 0), stop=(kt == KT - 1), skip_group_check=True)
                for c0, cw in CH:
                    t.matmul(arena[0:E, CA + c0:CA + c0 + cw],
                             gwt_b[:, kt * E:(kt + 1) * E], xn[:, c0:c0 + cw],
                             start=(kt == 0), stop=(kt == KT - 1), skip_group_check=True)
                for c0, cw in CH:
                    t.matmul(arena[0:N1, CB + c0:CB + c0 + cw],
                             at1[:, kt * N1:(kt + 1) * N1], xn[:, c0:c0 + cw],
                             start=(kt == 0), stop=(kt == KT - 1), skip_group_check=True)
                mm = None
                for c0, cw in CH:
                    mm = t.matmul(arena[0:N2, CD + c0:CD + c0 + cw],
                                  at2[:, kt * N2:(kt + 1) * N2], xn[:, c0:c0 + cw],
                                  start=(kt == 0), stop=(kt == KT - 1), skip_group_check=True)
                I(mm, "sT", ("b", "sT", kt))


        @block.gpsimd
        def _(g):
            for kt in range(min(8, KT)):
                W(g, ("so", "x", KT - 8 + kt),
                  ("so", "x", kt) if kt < KT else None)
                I(g.dma_start(out=xst[:, (kt % 8) * T:(kt % 8 + 1) * T],
                              in_=dxo.ap()[kt * 128:(kt + 1) * 128, :]),
                  "dX%d" % (kt % 8), ("st", "x", kt), n=16)

        @block.gpsimd
        def _(g):
            I(g.dma_start(out=wqk_b[:, :], in_=dwqk.ap()), "dZ", n=16)
            W(g, ("init",))
            I(g.dma_start(out=wvu8[2:6, :], in_=dwvu.ap()), "dZ", n=16)
            I(g.dma_start(out=wvu8[0:1, :], in_=dbvu.ap()), "dZ", n=16)
            I(g.dma_start(out=e1_b[:, :], in_=de1.ap()), "dZ", n=16)
            I(g.dma_start(out=e2_b[:, :], in_=de2.ap()), "dZ", n=16)
            I(g.dma_start(out=bt1[:, :], in_=dlbt.ap()[0:128, :]), "dZ", n=16)
            I(g.dma_start(out=bt2[:, :], in_=dlbt.ap()[128:ER, :]), "dZ", n=16)
            reg(("lbtW",), "dZ")

        def emit_bwload(mo):
            @block.gpsimd
            def _(g, mo=mo):
                if mo >= 2:
                    W(g, ("fT", mo - 2))
                for i in range(ng):
                    k0, k1 = i * gsz, (i + 1) * gsz
                    I(g.dma_start(
                        out=bw_bf[:, (mo % 2) * D + k0 * 128:(mo % 2) * D + k1 * 128]
                            .rearrange("p (ki c) -> p ki c", ki=gsz),
                        in_=dbwt.ap()[k0 * 128:k1 * 128, mo * 128:(mo + 1) * 128]
                            .rearrange("(ki p) c -> p ki c", p=128)),
                      "dP%d" % (mo % 2), n=16)
                reg(("wLb", mo), "dP%d" % (mo % 2))

        emit_bwload(0)
        emit_bwload(1)

        @block.vector
        def _(v):
            vself(v)
            W(v, ("b", "sT", KT - 1), ("smallY",))
            I(v.tensor_tensor(tmpa[0:QL, 0:T], arena[0:QL, CC:CC + T], rs_f[0:QL, :],
                              ALU.mult), "sV")
            vself(v)
            I(v.tensor_scalar(cq_b[:, :], tmpa[0:QL, 0:T], bqd_f[:, :], None,
                              ALU.add), "sV", ("cqev",))
            vself(v)
            I(v.tensor_tensor(ckv_f[0:KL, 0:T], arena[32:32 + KL, CC:CC + T],
                              rs_f[0:KL, :], ALU.mult), "sV")
            vself(v)
            I(v.tensor_scalar(ckv_f[0:KL, 0:T], ckv_f[0:KL, 0:T], bkvd_f[:, :], None,
                              ALU.add), "sV")
            vself(v)
            I(v.tensor_copy(ckv_b[:, 0:T], ckv_f[0:KL, 0:T]), "sV", ("ckev0",))
            vself(v)
            I(v.tensor_tensor(logits_f[:, :], arena[0:E, CA:CA + T], rs_f[0:E, :],
                              ALU.mult), "sV", ("lgev",))

        @block.tensor
        def _(t):
            W(t, ("lgev",))
            mm = None
            for tt in range(NT):
                mm = t.transpose(arena[0:128, CA + tt * E:CA + (tt + 1) * E],
                                 logits_f[0:E, tt * 128:(tt + 1) * 128], id_f[0:E, 0:E])
            I(mm, "sT", ("ltT",))

        @block.vector
        def _(v):
            vself(v)
            W(v, ("ltT",))
            I(v.tensor_copy(lt_f[:, :], arena[0:128, CA:CA + NT * E]), "sV", ("ltc",))
            for tt in range(NT):
                sl = slice(tt * E, (tt + 1) * E)
                vself(v)
                I(v.reduce_max(mrow[:, tt:tt + 1], lt_f[:, sl], axis=AX.X), "sV")
                vself(v)
                I(v.tensor_scalar(mk1[:, sl], lt_f[:, sl], mrow[:, tt:tt + 1], None,
                                  ALU.is_equal), "sV")
                vself(v)
                I(v.scalar_tensor_tensor(msk[:, sl], mk1[:, sl], -1e30, lt_f[:, sl],
                                         ALU.mult, ALU.add), "sV")
                vself(v)
                I(v.reduce_max(mrow[:, NT + tt:NT + tt + 1], msk[:, sl], axis=AX.X), "sV")
                vself(v)
                I(v.tensor_scalar(mk2[:, sl], msk[:, sl], mrow[:, NT + tt:NT + tt + 1], None,
                                  ALU.is_equal), "sV")
            vself(v)
            I(v.tensor_tensor(mrow[:, 4 * NT:5 * NT], mrow[:, 0:NT], mrow[:, NT:2 * NT],
                              ALU.subtract), "sV", ("gdel",))

        def ckv_oth_extra(t, kt, off):
            for c0, cw in CH:
                t.matmul(arena[32:32 + KL, CC + c0:CC + c0 + cw],
                         wkv_b[:, kt * KL:(kt + 1) * KL],
                         xst[:, off + c0:off + c0 + cw],
                         start=(kt == 0), stop=False, skip_group_check=True)

        def gating_v2a(v):
            vself(v)
            W(v, ("gexp",))
            I(v.tensor_scalar(mrow[:, 4 * NT:5 * NT], mrow[:, 5 * NT:6 * NT], 1.0, None,
                              ALU.add), "sV")
            vself(v)
            I(v.reciprocal(mrow[:, 2 * NT:3 * NT], mrow[:, 4 * NT:5 * NT]), "sV")
            vself(v)
            I(v.tensor_scalar(mrow[:, 3 * NT:4 * NT], mrow[:, 2 * NT:3 * NT], -2.0, 2.0,
                              ALU.mult, ALU.add), "sV")
            for tt in range(NT):
                sl = slice(tt * E, (tt + 1) * E)
                vself(v)
                I(v.tensor_scalar(msk[:, sl], mk1[:, sl], mrow[:, 2 * NT + tt:2 * NT + tt + 1],
                                  2.0, ALU.mult, ALU.mult), "sV")
                vself(v)
                I(v.scalar_tensor_tensor(gtk[:, sl], mk2[:, sl],
                                         mrow[:, 3 * NT + tt:3 * NT + tt + 1], msk[:, sl],
                                         ALU.mult, ALU.add), "sV")
            reg(("gtk",), "sV")

        def gating_v2b(v):
            vself(v)
            W(v, ("gtT",))
            I(v.tensor_copy(gt_b[:, :], arena[0:E, CA:CA + T]), "sV", ("gtc",))
            vself(v)
            W(v, ("gx1T",))
            I(v.tensor_copy(gex1[:, :], arena[0:N1, CA:CA + T]), "sV", ("gx1c",))
            vself(v)
            I(v.tensor_tensor(hg1[:, :], arena[0:N1, CB:CB + T], gex1[:, :], ALU.mult),
              "sV", ("hg1",))
            vself(v)
            W(v, ("gx2T",))
            I(v.tensor_copy(gex2[:, :], arena[0:N2, CB:CB + T]), "sV", ("gx2c",))
            vself(v)
            I(v.tensor_tensor(hg2[:, :], arena[0:N2, CD:CD + T], gex2[:, :], ALU.mult),
              "sV", ("hg2",))

        def gtT_fn(t):
            W(t, ("gtk",), ("ltc",))
            mm = None
            for tt in range(NT):
                mm = t.transpose(arena[0:E, CA + tt * 128:CA + (tt + 1) * 128],
                                 gtk[0:128, tt * E:(tt + 1) * E], id_f[:, :])
            I(mm, "sT", ("gtT",))

        def gx1_fn(t):
            W(t, ("gtc",), ("lbtW",))
            mm = None
            for c0, cw in CH:
                mm = t.matmul(arena[0:N1, CA + c0:CA + c0 + cw], e1_b[:, :],
                              gt_b[0:E, c0:c0 + cw], start=True, stop=True,
                              skip_group_check=True)
            I(mm, "sT", ("gx1T",))

        def gx2_fn(t):
            W(t, ("hg1",))
            mm = None
            for c0, cw in CH:
                mm = t.matmul(arena[0:N2, CB + c0:CB + c0 + cw], e2_b[:, :],
                              gt_b[0:E, c0:c0 + cw], start=True, stop=True,
                              skip_group_check=True)
            I(mm, "sT", ("gx2T",))

        gtinj = None
        if KT >= 16:
            k1_, k2_, k3_ = KT * 5 // 8, KT * 6 // 8, KT * 7 // 8
            gtinj = {k1_: gtT_fn, k2_: gx1_fn, k3_: gx2_fn}
            tb_ = cnt["sT"]
            EV[("gtT",)] = ("sT", tb_ + k1_ + 2)
            EV[("gx1T",)] = ("sT", tb_ + k2_ + 3)
            EV[("gx2T",)] = ("sT", tb_ + k3_ + 4)

            @block.vector
            def _(v):
                gating_v2a(v)

            @block.vector
            def _(v):
                gating_v2b(v)

        stats_pass("st", dxo, CC, CC, 64, (("cqev",), ("ckev0",)), extra=ckv_oth_extra, tinject=gtinj)

        @block.vector
        def _(v):
            vself(v)
            W(v, ("st", "sT", KT - 1), ("so", "bc2"))
            I(v.tensor_scalar(srow[0:1, 0:T], arena[0:1, CC:CC + T],
                              1.0 / D, None, ALU.mult), "sV")
            vself(v)
            I(v.tensor_scalar(srow[0:1, T:2 * T], arena[64:65, CC:CC + T],
                              1.0 / D, EPS, ALU.mult, ALU.add), "sV")
            vself(v)
            I(v.tensor_tensor(srow[0:1, 2 * T:3 * T], srow[0:1, 0:T],
                              srow[0:1, 0:T], ALU.mult), "sV")
            vself(v)
            I(v.tensor_tensor(srow[0:1, 2 * T:3 * T], srow[0:1, T:2 * T],
                              srow[0:1, 2 * T:3 * T], ALU.subtract), "sV", ("st", "var"))

        @block.vector
        def _(v):
            vself(v)
            W(v, ("st", "rstd"))
            I(v.scalar_tensor_tensor(srow[0:1, T:2 * T], srow[0:1, 0:T],
                                     -1.0, srow[0:1, 2 * T:3 * T], ALU.mult, ALU.mult),
              "sV")
            vself(v)
            I(v.tensor_copy(mrsb[0:1, 0:T], srow[0:1, T:2 * T]), "sV", ("mrsb",))

        @block.tensor
        def _(t):
            W(t, ("mrsb",))
            mm = None
            for ci, (c0, cw) in enumerate(CH):
                mm = t.matmul(arena[32:32 + KL, CC + c0:CC + c0 + cw],
                              cs_kv[0:1, 0:KL], mrsb[0:1, c0:c0 + cw],
                              start=False, stop=(ci == NCH - 1), skip_group_check=True)
            I(mm, "sT", ("ckoT",))

        if KT >= 16:
            assert EV[("gtT",)] == ("sT", tb_ + k1_ + 2)
            assert EV[("gx1T",)] == ("sT", tb_ + k2_ + 3)
            assert EV[("gx2T",)] == ("sT", tb_ + k3_ + 4)
        else:
            @block.vector
            def _(v):
                gating_v2a(v)

            @block.tensor
            def _(t):
                gtT_fn(t)

            @block.vector
            def _(v):
                vself(v)
                W(v, ("gtT",))
                I(v.tensor_copy(gt_b[:, :], arena[0:E, CA:CA + T]), "sV", ("gtc",))

            @block.tensor
            def _(t):
                gx1_fn(t)

            @block.vector
            def _(v):
                vself(v)
                W(v, ("gx1T",))
                I(v.tensor_copy(gex1[:, :], arena[0:N1, CA:CA + T]), "sV", ("gx1c",))
                vself(v)
                I(v.tensor_tensor(hg1[:, :], arena[0:N1, CB:CB + T], gex1[:, :],
                                  ALU.mult), "sV", ("hg1",))

            @block.tensor
            def _(t):
                gx2_fn(t)

            @block.vector
            def _(v):
                vself(v)
                W(v, ("gx2T",))
                I(v.tensor_copy(gex2[:, :], arena[0:N2, CB:CB + T]), "sV", ("gx2c",))
                vself(v)
                I(v.tensor_tensor(hg2[:, :], arena[0:N2, CD:CD + T], gex2[:, :],
                                  ALU.mult), "sV", ("hg2",))


        @block.tensor
        def _(t):
            W(t, ("st", "rstd"))
            mm = None
            for c0, cw in CH:
                mm = t.matmul(arena[64:64 + KL, CC + c0:CC + c0 + cw],
                              ones_f[0:1, 0:KL],
                              srow[0:1, 2 * T + c0:2 * T + c0 + cw],
                              start=True, stop=True, skip_group_check=True)
            I(mm, "sT", ("st", "bc1"))

        @block.vector
        def _(v):
            vself(v)
            W(v, ("st", "bc1"))
            I(v.tensor_copy(rso_f[0:KL, :], arena[64:64 + KL, CC:CC + T]),
              "sV", ("st", "rsc"))
            vself(v)
            W(v, ("ckoT",))
            I(v.tensor_tensor(ckv_f[0:KL, T:2 * T], arena[32:32 + KL, CC:CC + T],
                              rso_f[0:KL, :], ALU.mult), "sV")
            vself(v)
            I(v.tensor_scalar(ckv_f[0:KL, T:2 * T], ckv_f[0:KL, T:2 * T],
                              bkvd_f[:, :], None, ALU.add), "sV")
            vself(v)
            I(v.tensor_copy(ckv_b[:, T:2 * T], ckv_f[0:KL, T:2 * T]), "sV", ("ckev1",))

        @block.scalar
        def _(s):
            for bk in ("so", "st"):
                for kt in range(KT):
                    src_ = (xnT[:, kt * T:(kt + 1) * T] if bk == "so"
                            else xst[:, (kt % 8) * T:(kt % 8 + 1) * T])
                    war = (bk, "sT", kt - 2)
                    if kt < 2 and bk == "st":
                        war = ("so", "sT", KT - 2 + kt)
                    sself(s)
                    W(s, (bk, "x", kt), war)
                    Ip(s.activation(xsq[:, (kt % 2) * T:(kt % 2 + 1) * T],
                                    src_, AF.Square),
                       "sS", (bk, "sq", kt))
                    if bk == "st" and GEK is not None and kt == GEK:
                        sself(s)
                        W(s, ("gdel",))
                        Ip(s.activation(mrow[:, 5 * NT:6 * NT],
                                        mrow[:, 4 * NT:5 * NT], AF.Exp,
                                        scale=-1.0), "sS", ("gexp",))
                sself(s)
                W(s, (bk, "var"))
                I(s.activation(srow[0:1, T:2 * T], srow[0:1, 2 * T:3 * T], AF.Ln), "sS")
                sself(s)
                Ip(s.activation(srow[0:1, 2 * T:3 * T], srow[0:1, T:2 * T], AF.Exp,
                                scale=-0.5), "sS", (bk, "rstd"))

        if GEK is None:
            @block.scalar
            def _(s):
                sself(s)
                W(s, ("gdel",))
                Ip(s.activation(mrow[:, 5 * NT:6 * NT], mrow[:, 4 * NT:5 * NT],
                                AF.Exp, scale=-1.0), "sS", ("gexp",))

        @block.tensor
        def _(t):
            W(t, ("hg2",), ("ckev1",))
            mm = None
            for kt in range(SKT):
                mm = t.transpose(arena[0:128, KG + 4 * kt:KG + 4 * kt + 4],
                                 ckv_f[0:KL, kt * 128:(kt + 1) * 128], id_f[0:KL, 0:KL])
            I(mm, "sT", ("kgT",))

        @block.vector
        def _(v):
            vself(v)
            W(v, ("kgT",))
            I(v.tensor_copy(
                kaug.ap().rearrange("p (g c) -> p g c", g=SKT)[:, :, 2:6],
                arena[0:128, KG:KG + 4 * SKT].rearrange("p (g c) -> p g c", g=SKT)),
              "sV", ("kgc",))
            vself(v)
            I(v.memset(kaug.ap().rearrange("p (g c) -> p g c", g=SKT)[:, :, 0:1], 1.0),
              "sV", ("kgo",))

        es1.close()
        es_ao = ExitStack()
        aoT = sbw(es_ao, "aoT", [128, H * T], BF16)
        es_attn = ExitStack()
        pbuf = sbw(es_attn, "pbuf", [128, NPB * T], BF16)
        qb0 = sbw(es_attn, "qb0", [KL, T], BF16)
        qb1 = sbw(es_attn, "qb1", [KL, T], BF16)
        a2b0 = sbw(es_attn, "a2b0", [8, T], BF16)
        a2b1 = sbw(es_attn, "a2b1", [8, T], BF16)
        dinvb = sbw(es_attn, "dinvb", [128, T], F32)
        dtmp = dinvb
        dinv_bf = sbw(es_attn, "dinv_bf", [1, T], BF16)
        onesb = sbw(es_attn, "onesb", [1, 128], BF16)
        fsb = sbw(es_attn, "fsb", [128, T], mybir.dt.float16)
        qbs = [qb0, qb1]
        a2bs = [a2b0, a2b1]

        assert H == KT
        pair_free = {0: [("gx1c",)] * NCH, 1: [("gx2c",)] * NCH}

        spair = {}
        qloc = {}
        bcdloc = {}
        uploc = {}

        def a2block(t, hh, j, explicit_exp=False):
            if explicit_exp:
                W(t, ("exp", hh, j))
            u = hh * SKT + j
            mm = None
            for c, (c0, cw) in enumerate(CH):
                if j == 0:
                    if hh > 0:
                        W(t, ("a2s", hh - 1, c))
                    elif c == 0:
                        W(t, ("ckev1",), ("kgo",))
                mm = t.matmul(arena[0:8, A2C + c0:A2C + c0 + cw],
                              kaug[:, 8 * j:8 * j + 8],
                              pbuf[:, (u % NPB) * T + c0:(u % NPB) * T + c0 + cw],
                              start=(j == 0), stop=(j == SKT - 1),
                              skip_group_check=True)
            I(mm, "sT", ("a2", hh, j))

        def ffn_mms(mo):
            n = (KT + 2) * NCH
            i = 0
            for ki in range(KT):
                for c0, cw in CH:
                    yield (bw_bf[:, (mo % 2) * D + ki * 128:(mo % 2) * D + (ki + 1) * 128],
                           xnT[:, ki * T + c0:ki * T + c0 + cw], c0, cw,
                           ki == 0, False)
            for c0, cw in CH:
                yield (bt1[:, mo * 128:(mo + 1) * 128], hg1[:, c0:c0 + cw], c0, cw,
                       False, False)
            for ci, (c0, cw) in enumerate(CH):
                yield (bt2[0:N2, mo * 128:(mo + 1) * 128], hg2[0:N2, c0:c0 + cw], c0, cw,
                       False, ci == NCH - 1)

        def emit_bcd(t, hh, pb_):
            W(t, ("dinv", hh))
            for qc, (c0, cw) in enumerate(CH):
                W(t, pair_free[pb_][qc])
                I(t.matmul(arena[0:128, PB[pb_] + c0:PB[pb_] + c0 + cw], onesb[:, :],
                           dinv_bf[0:1, c0:c0 + cw], start=True, stop=True,
                           skip_group_check=True), "sT", ("bcd", hh, qc))
            bcdloc[hh] = PB[pb_]
            pair_free[pb_] = [("dcp", hh, qc) for qc in range(NCH)]

        def emit_up(t, hh, pu):
            W(t, ("a2s", hh, NCH - 1))
            for qc, (c0, cw) in enumerate(CH):
                W(t, pair_free[pu][qc])
                I(t.matmul(arena[0:128, PB[pu] + c0:PB[pu] + c0 + cw],
                           wvu8[0:8, hh * DH:(hh + 1) * DH],
                           a2bs[hh % 2][0:8, c0:c0 + cw], start=True, stop=True,
                           skip_group_check=True), "sT", ("up", hh, qc))
            uploc[hh] = PB[pu]
            pair_free[pu] = [("uptt", hh, qc) for qc in range(NCH)]

        def emit_vtail(v, hh):
            for qc, (c0, cw) in enumerate(CH):
                vself(v)
                W(v, ("bcd", hh, qc))
                Ip(v.tensor_copy(dinvb[:, c0:c0 + cw],
                                 arena[0:128, bcdloc[hh] + c0:bcdloc[hh] + c0 + cw]),
                   "sV", ("dcp", hh, qc))
            for qc, (c0, cw) in enumerate(CH):
                vself(v)
                W(v, ("up", hh, qc))
                Ip(v.tensor_tensor(aoT[:, hh * T + c0:hh * T + c0 + cw],
                                   arena[0:128, uploc[hh] + c0:uploc[hh] + c0 + cw],
                                   dinvb[:, c0:c0 + cw], ALU.mult), "sV", ("uptt", hh, qc))

        p0 = 1

        @block.tensor
        def _(t, p0=p0):
            W(t, ("hg2",), ("cqev",), ("lbtW",), *pair_free[p0])
            for ci, (c0, cw) in enumerate(CH):
                I(t.matmul(arena[0:KL, PB[p0] + c0:PB[p0] + c0 + cw], wqk_b[0:QL, 0:KL],
                           cq_b[0:QL, c0:c0 + cw], start=True, stop=True,
                           skip_group_check=True), "sT", ("q", 0, ci))
        qloc[0] = PB[p0]
        pair_free[p0] = [("qev", 0, ci) for ci in range(NCH)]

        @block.vector
        def _(v, p0=p0):
            for ci, (c0, cw) in enumerate(CH):
                vself(v)
                W(v, ("q", 0, ci))
                I(v.tensor_scalar(qbs[0][0:KL, c0:c0 + cw],
                                  arena[0:KL, PB[p0] + c0:PB[p0] + c0 + cw],
                                  bqk_f[:, 0:1], QSC, ALU.add, ALU.mult),
                  "sV", ("qev", 0, ci))
            vself(v)
            I(v.memset(onesb[:, :], 1.0), "sV", ("onesb",))

        for h in range(H):
            base_s = cnt["sS"]
            for kt in range(SKT):
                extra = 2 if (kt > DKT and h > 0) else 0
                EV[("exp", h, kt)] = ("sS", base_s + kt + 1 + extra)
            base_s += SKT + (2 if h > 0 else 0)
            cnt["sS"] = base_s
            if h == H - 1:
                EV[("dinv", h)] = ("sS", base_s + 2)
            else:
                EV[("dinv", h)] = ("sS", base_s + DKT + 3)
            off = cnt["sV"]
            if h > 0:
                for c in range(NCH):
                    EV[("a2s", h - 1, c)] = ("sV", off + c + 1)
                off += NCH
            if h + 1 < H:
                for ci in range(NCH):
                    EV[("qev", h + 1, ci)] = ("sV", off + ci + 1)
                off += NCH
            if h > 0:
                for qc in range(NCH):
                    EV[("dcp", h - 1, qc)] = ("sV", off + qc + 1)
                off += NCH
                for qc in range(NCH):
                    EV[("uptt", h - 1, qc)] = ("sV", off + qc + 1)
                off += NCH
            EV[("fcp", h)] = ("sV", off + 1)

            @block.tensor
            def _(t, h=h):
                fq = list(ffn_mms(h))
                fpos = [0]

                def emit_ffn(n):
                    while n > 0 and fpos[0] < len(fq):
                        lhsT, rhs, c0, cw, st_, sp_ = fq[fpos[0]]
                        if fpos[0] == 0:
                            W(t, ("wLb", h), ("fcp", h - 1),
                              ("hg2",) if h == 0 else None)
                        mm = t.matmul(arena[0:128, FC + c0:FC + c0 + cw], lhsT, rhs,
                                      start=st_, stop=sp_, skip_group_check=True)
                        if fpos[0] == len(fq) - 1:
                            I(mm, "sT", ("fT", h))
                        fpos[0] += 1
                        n -= 1

                for kt in range(SKT):
                    if kt < SKT - 2:
                        if h == 0 and kt < 3:
                            emit_ffn(10)
                        else:
                            emit_ffn(-(-(len(fq) - fpos[0]) // (SKT - 2 - kt)))
                    if kt < 3:
                        if h > 0:
                            a2block(t, h - 1, SKT - 3 + kt)
                    else:
                        a2block(t, h, kt - 3)
                    p = kt % 2
                    if kt == 0:
                        W(t, ("qev", h, NCH - 1))
                    spair[(h, kt)] = PB[p]
                    mm = None
                    for c, (c0, cw) in enumerate(CH):
                        W(t, pair_free[p][c])
                        mm = t.matmul(arena[0:128, PB[p] + c0:PB[p] + c0 + cw],
                                      ckv_b[0:KL, kt * 128:(kt + 1) * 128],
                                      qbs[h % 2][0:KL, c0:c0 + cw],
                                      start=True, stop=True, skip_group_check=True)
                    I(mm, "sT", ("sc", h, kt))
                    pair_free[p] = [("exp", h, kt)] * NCH
                    if kt == QKT and h + 1 < H:
                        pq = (kt + 1) % 2
                        for ci, (c0, cw) in enumerate(CH):
                            W(t, pair_free[pq][ci])
                            I(t.matmul(arena[0:KL, PB[pq] + c0:PB[pq] + c0 + cw],
                                       wqk_b[0:QL, (h + 1) * KL:(h + 2) * KL],
                                       cq_b[0:QL, c0:c0 + cw], start=True, stop=True,
                                       skip_group_check=True), "sT", ("q", h + 1, ci))
                        qloc[h + 1] = PB[pq]
                        pair_free[pq] = [("qev", h + 1, ci) for ci in range(NCH)]
                    if kt == BKT and h > 0:
                        emit_bcd(t, h - 1, (kt + 1) % 2)
                    if kt == UKT and h > 0:
                        emit_up(t, h - 1, (kt + 1) % 2)
                emit_ffn(len(fq))

            if h > 0:
                @block.gpsimd
                def _(g, h=h):
                    W(g, ("fcp", h - 1))
                    I(g.dma_start(out=dgs.ap()[(h - 1) * 128:h * 128, :], in_=fsb[:, :]),
                      "dF" if (h - 1) % 2 == 0 else "dH", ("fst", h - 1), n=16)

            @block.vector
            def _(v, h=h):
                if h > 0:
                    for c, (c0, cw) in enumerate(CH):
                        vself(v)
                        W(v, ("a2", h - 1, SKT - 1), ("up", h - 3, NCH - 1),
                          ("dinv", h - 3) if h >= 3 else None)
                        Ip(v.tensor_copy(a2bs[(h - 1) % 2][0:8, c0:c0 + cw],
                                         arena[0:8, A2C + c0:A2C + c0 + cw]),
                           "sV", ("a2s", h - 1, c))
                if h + 1 < H:
                    for ci, (c0, cw) in enumerate(CH):
                        vself(v)
                        W(v, ("q", h + 1, ci))
                        inst = v.tensor_scalar(qbs[(h + 1) % 2][0:KL, c0:c0 + cw],
                                               arena[0:KL, qloc[h + 1] + c0:qloc[h + 1] + c0 + cw],
                                               bqk_f[:, h + 1:h + 2], QSC, ALU.add, ALU.mult)
                        Ip(inst, "sV", ("qev", h + 1, ci))
                if h > 0:
                    emit_vtail(v, h - 1)
                vself(v)
                W(v, ("fT", h), ("fst", h - 1))
                Ip(v.tensor_tensor(fsb[:, :], arena[0:128, FC:FC + T], rs_bf[:, :],
                                   ALU.mult), "sV", ("fcp", h))

            if h + 2 < H:
                emit_bwload(h + 2)

        @block.tensor
        def _(t):
            for j in (SKT - 3, SKT - 2, SKT - 1):
                a2block(t, H - 1, j, explicit_exp=True)

        PW4 = [FC, A2C, PB[0], PB[1]]
        EV[("wL", 0)] = ("dP0", cnt["dP0"] + 16 * ng)

        @block.tensor
        def _(t):
            W(t, ("wL", 0), ("fcp", H - 1))
            for ki in range(KT - 1):
                for c0, cw in CH:
                    t.matmul(arena[0:128, FC + c0:FC + c0 + cw],
                             bw_bf[:, ki * 128:(ki + 1) * 128],
                             aoT[:, ki * T + c0:ki * T + c0 + cw],
                             start=(ki == 0), stop=False, skip_group_check=True)

        @block.tensor
        def _(t):
            emit_bcd(t, H - 1, 0)
            emit_up(t, H - 1, 1)

        off = cnt["sV"]
        for c in range(NCH):
            EV[("a2s", H - 1, c)] = ("sV", off + c + 1)
        off += NCH
        for qc in range(NCH):
            EV[("dcp", H - 1, qc)] = ("sV", off + qc + 1)
        off += NCH
        for qc in range(NCH):
            EV[("uptt", H - 1, qc)] = ("sV", off + qc + 1)

        @block.vector
        def _(v):
            for c, (c0, cw) in enumerate(CH):
                vself(v)
                W(v, ("a2", H - 1, SKT - 1), ("up", H - 3, NCH - 1), ("dinv", H - 3))
                Ip(v.tensor_copy(a2bs[(H - 1) % 2][0:8, c0:c0 + cw],
                                 arena[0:8, A2C + c0:A2C + c0 + cw]),
                   "sV", ("a2s", H - 1, c))
            emit_vtail(v, H - 1)

        @block.gpsimd
        def _(g):
            W(g, ("fcp", H - 1))
            I(g.dma_start(out=dgs.ap()[(H - 1) * 128:H * 128, :], in_=fsb[:, :]),
              "dF" if (H - 1) % 2 == 0 else "dH", ("fst", H - 1), n=16)

        cnt["sS"] -= H * SKT + 2 * (H - 1)

        @block.scalar
        def _(s):
            for h in range(H):
                for kt in range(SKT):
                    u = h * SKT + kt
                    war = None
                    if u >= NPB:
                        up_, uk = divmod(u - NPB, SKT)
                        war = ("a2", up_, uk)
                    W(s, ("sc", h, kt), war)
                    sl = u % NPB
                    inst = s.activation(pbuf[:, sl * T:(sl + 1) * T],
                                        arena[0:128, spair[(h, kt)]:spair[(h, kt)] + T],
                                        AF.Exp)
                    Ip(inst, "sS", ("exp", h, kt))
                    if kt == DKT and h > 0:
                        W(s, ("a2s", h - 1, NCH - 1),
                          ("bcd", h - 2, NCH - 1) if h >= 2 else None)
                        I(s.activation(dtmp[0:1, 0:T], a2bs[(h - 1) % 2][0:1, 0:T],
                                       AF.Ln), "sS")
                        sself(s)
                        Ip(s.activation(dinv_bf[0:1, 0:T], dtmp[0:1, 0:T], AF.Exp,
                                        scale=-1.0), "sS", ("dinv", h - 1))
            W(s, ("a2s", H - 1, NCH - 1), ("bcd", H - 2, NCH - 1))
            I(s.activation(dtmp[0:1, 0:T], a2bs[(H - 1) % 2][0:1, 0:T], AF.Ln), "sS")
            sself(s)
            Ip(s.activation(dinv_bf[0:1, 0:T], dtmp[0:1, 0:T], AF.Exp, scale=-1.0),
               "sS", ("dinv", H - 1))

        es_attn.close()
        es3 = ExitStack()
        wo_bf = bw_bf
        gsl = sbw(es3, "gsl", [128, 2 * T], mybir.dt.float16)
        ost = sbw(es3, "ost", [128, 2 * T], F32)
        g_f = sbw(es3, "g_f", [128, T], F32)

        tb = cnt["sT"]
        vb = cnt["sV"]
        sb_ = cnt["sS"]
        lb = cnt["dL0"]
        gb = cnt["dL1"]
        gB = cnt["dG"]
        ob = {0: cnt["dO0"], 1: cnt["dO1"]}
        for mo in range(KT):
            EV[("woT", mo)] = ("sT", tb + mo + 1)
            EV[("silu", mo)] = ("sS", sb_ + mo + 1)
            EV[("stt", mo)] = ("sV", vb + 2 * mo + 1)
            EV[("res", mo)] = ("sV", vb + 2 * mo + 2)
            EV[("xq", mo)] = ("dL0", lb + 16 * (mo + 1))
            EV[("gld", mo)] = (("dL1", gb + 16 * (mo // 2 + 1)) if mo % 2 == 0
                               else ("dG", gB + 16 * (mo // 2 + 1)))
            EV[("out", mo)] = ("dO%d" % (mo % 2), ob[mo % 2] + 16 * (mo // 2 + 1))

        @block.gpsimd
        def _(g):
            for mo in range(KT):
                if mo >= 2:
                    W(g, ("woT", mo - 2))
                else:
                    W(g, ("fT", H - 2 + mo))
                for i in range(ng):
                    k0, k1 = i * gsz, (i + 1) * gsz
                    I(g.dma_start(
                        out=wo_bf[:, (mo % 2) * D + k0 * 128:(mo % 2) * D + k1 * 128]
                            .rearrange("p (ki c) -> p ki c", ki=gsz),
                        in_=dwo.ap()[k0 * 128:k1 * 128, mo * 128:(mo + 1) * 128]
                            .rearrange("(ki p) c -> p ki c", p=128)),
                      "dP%d" % (mo % 2), n=16)
                reg(("wL", mo), "dP%d" % (mo % 2))

        for mo in range(KT):
            @block.tensor
            def _(t, mo=mo):
                PW = PW4[mo % 4]
                W(t, ("wL", mo), ("stt", mo - 4))
                if mo == 0:
                    W(t, ("uptt", H - 1, NCH - 1))
                    kis = [KT - 1]
                elif mo == 1:
                    W(t, ("a2s", H - 1, NCH - 1), ("uptt", H - 1, NCH - 1))
                    kis = range(KT)
                elif mo == 2:
                    W(t, ("uptt", H - 1, NCH - 1), *pair_free[0])
                    kis = range(KT)
                elif mo == 3:
                    W(t, *pair_free[1])
                    kis = range(KT)
                else:
                    kis = range(KT)
                mm = None
                for ki in kis:
                    for c0, cw in CH:
                        mm = t.matmul(arena[0:128, PW + c0:PW + c0 + cw],
                                      wo_bf[:, (mo % 2) * D + ki * 128:(mo % 2) * D + (ki + 1) * 128],
                                      aoT[:, ki * T + c0:ki * T + c0 + cw],
                                      start=(ki == 0 and mo != 0),
                                      stop=(ki == KT - 1),
                                      skip_group_check=True)
                Ip(mm, "sT", ("woT", mo))

        @block.scalar
        def _(s):
            for mo in range(KT):
                sself(s)
                W(s, ("gld", mo), ("res", mo - 1))
                if mo == 0:
                    W(s, ("uptt", H - 1, NCH - 1))
                Ip(s.activation(g_f[:, :], gsl[:, (mo % 2) * T:(mo % 2 + 1) * T],
                                AF.Silu), "sS", ("silu", mo))

        @block.vector
        def _(v):
            for mo in range(KT):
                PW = PW4[mo % 4]
                vself(v)
                W(v, ("woT", mo), ("silu", mo), ("out", mo - 2))
                Ip(v.scalar_tensor_tensor(ost[:, (mo % 2) * T:(mo % 2 + 1) * T],
                                          arena[0:128, PW:PW + T], bo_f[:, mo:mo + 1],
                                          g_f[:, :], ALU.add, ALU.mult), "sV", ("stt", mo))
                vself(v)
                W(v, ("xq", mo))
                Ip(v.tensor_tensor(ost[:, (mo % 2) * T:(mo % 2 + 1) * T],
                                   ost[:, (mo % 2) * T:(mo % 2 + 1) * T],
                                   g_f[:, :], ALU.add), "sV", ("res", mo))

        @block.sync
        def _(sy):
            def gld(mo):
                if mo >= KT:
                    return
                if mo >= 2:
                    W(sy, ("silu", mo - 2), ("fst", mo))
                else:
                    W(sy, ("fst", mo), ("dinv", H - 1), ("a2", H - 1, SKT - 1),
                      ("fst", H - 1), ("fst", H - 2), ("uptt", H - 1, NCH - 1))
                sgl = "dL1" if mo % 2 == 0 else "dG"
                inst = sy.dma_start(out=gsl[:, (mo % 2) * T:(mo % 2 + 1) * T],
                                    in_=dgs.ap()[mo * 128:(mo + 1) * 128, :])
                cnt[sgl] += 16
                inst.then_inc(sems[sgl], 16)
                assert EV[("gld", mo)] == (sgl, cnt[sgl]), (mo, EV[("gld", mo)], cnt[sgl])

            gld(0)
            gld(1)
            for mo in range(KT):
                W(sy, ("stt", mo))
                inst = sy.dma_start(out=g_f[:, :], in_=dx.ap()[mo * 128:(mo + 1) * 128, :])
                cnt["dL0"] += 16
                inst.then_inc(sems["dL0"], 16)
                assert EV[("xq", mo)] == ("dL0", cnt["dL0"])
                W(sy, ("res", mo))
                inst = sy.dma_start(out=dout.ap()[mo * 128:(mo + 1) * 128, :],
                                    in_=ost[:, (mo % 2) * T:(mo % 2 + 1) * T])
                so = "dO%d" % (mo % 2)
                cnt[so] += 16
                inst.then_inc(sems[so], 16)
                assert EV[("out", mo)] == (so, cnt[so])
                gld(mo + 2)

        @block.sync
        def _(sy):
            for s in ("dY", "dW", "dF", "dG", "dH", "dO0", "dO1", "dL0", "dL1", "dP0", "dP1",
                      "dX0", "dX1", "dX2", "dX3"):
                if cnt[s] > 0:
                    sy.wait_ge(sems[s], cnt[s])

        es3.close()
        es_ao.close()
        es_mid.close()

    return nc



def _host_layouts(cfg, inputs):
    D, T, H, DH = cfg["D"], cfg["T"], cfg["H"], cfg["DH"]
    QL, KL, E, R = cfg["QL"], cfg["KL"], cfg["E"], cfg["R"]
    KT = D // 128
    ER = E * R
    N1 = min(128, ER)
    N2 = ER - N1
    f32 = np.float32
    C = lambda a: np.ascontiguousarray(a, dtype=f32)

    w = {}
    w["wq"] = C(inputs["L0_attn_Wq_d"])
    w["wkv"] = C(inputs["L0_attn_Wkv_d"])
    w["wqk"] = C(inputs["L0_attn_W_qk"])
    w["wvu"] = C(inputs["L0_attn_Wv_u"])
    w["bqd"] = C(inputs["L0_attn_bq_d"].reshape(QL, 1))
    w["bkvd"] = C(inputs["L0_attn_bkv_d"].reshape(KL, 1))
    w["bqk"] = C(np.asarray(inputs["L0_attn_b_qk"]).reshape(H, KL).T)
    w["bvuf"] = C(np.asarray(inputs["L0_attn_bv_u"]).reshape(1, D))
    w["bo"] = C(np.asarray(inputs["L0_attn_bo"]).reshape(KT, 128).T)
    w["wo"] = C(inputs["L0_attn_Wo"])
    w["bwt"] = C(np.asarray(inputs["L0_ffn_base_W"]).T)
    w["gwt"] = C(np.asarray(inputs["L0_ffn_gate_W"]).T)
    w["lat"] = C(np.asarray(inputs["L0_ffn_lora_A"]).reshape(ER, D).T)
    w["lbt"] = C(np.asarray(inputs["L0_ffn_lora_B"]).transpose(0, 2, 1).reshape(ER, D))
    e_full = np.zeros((E, ER), dtype=f32)
    for e in range(E):
        e_full[e, e * R:(e + 1) * R] = 1.0
    w["e1"] = C(e_full[:, 0:N1])
    w["e2"] = C(e_full[:, N1:ER])
    w["idm"] = np.eye(128, dtype=f32)
    return w


def kernel(**inputs):
    cfg = FULL_CFG
    D, T = cfg["D"], cfg["T"]
    x = np.asarray(inputs["L0_attn_x"], dtype=np.float32)
    B, S, _ = x.shape
    n_cores = 8
    w = _host_layouts(cfg, inputs)

    nc = build(cfg)
    in_maps = []
    for c in range(n_cores):
        b, q = c // 2, c % 2
        own = np.ascontiguousarray(x[b, q * T:(q + 1) * T, :].T)
        oth = np.ascontiguousarray(x[b, (1 - q) * T:(2 - q) * T, :].T)
        m = dict(w)
        m["xT_own"] = own
        m["xT_oth"] = oth
        in_maps.append(m)

    res = run_bass_kernel_spmd(nc, in_maps, core_ids=list(range(n_cores)))
    out = np.empty((B, S, D), dtype=np.float32)
    for c in range(n_cores):
        b, q = c // 2, c % 2
        out[b, q * T:(q + 1) * T, :] = res.results[c]["outT"].T
    return out


# version 33 (final)
# speedup vs baseline: 1.2331x; 1.0016x over previous
import math
from contextlib import ExitStack

import numpy as np
import ml_dtypes

import concourse.bass as bass
import concourse.mybir as mybir
from concourse.bass_utils import run_bass_kernel_spmd

F32 = mybir.dt.float32
BF16 = mybir.dt.bfloat16
AF = mybir.ActivationFunctionType
ALU = mybir.AluOpType
AX = mybir.AxisListType
BFT = ml_dtypes.bfloat16

FULL_CFG = dict(D=4096, T=1024, H=32, DH=128, QL=12, KL=4, E=28, R=8)


def build(cfg):
    D, T, H, DH = cfg["D"], cfg["T"], cfg["H"], cfg["DH"]
    QL, KL, E, R = cfg["QL"], cfg["KL"], cfg["E"], cfg["R"]
    KT = D // 128
    SKT = (2 * T) // 128
    NT = T // 128
    ER = E * R
    N1 = min(128, ER)
    N2 = ER - N1
    CH = [(c0, min(512, T - c0)) for c0 in range(0, T, 512)]
    NCH = len(CH)
    QSC = float(1.0 / math.sqrt(KL))
    EPS = 1e-5

    AR = 4 * T + (512 if 4 * T + 512 <= 4096 else 0)
    CA, CB, CC, CD = 0, T, 2 * T, 3 * T
    KG = 2 * T
    A2C = 2 * T
    FC = 3 * T
    PB = [0, T]
    NPB = 4
    if SKT >= 16:
        QKT, BKT, UKT, DKT = 6, 8, 10, 5
    else:
        QKT, BKT, UKT, DKT = 3, 4, 5, 2

    nc = bass.Bass("TRN2", target_bir_lowering=False)

    dx = nc.dram_tensor("xT_own", [D, T], F32, kind="ExternalInput")
    dxo = nc.dram_tensor("xT_oth", [D, T], F32, kind="ExternalInput")
    dwq = nc.dram_tensor("wq", [D, QL], F32, kind="ExternalInput")
    dwkv = nc.dram_tensor("wkv", [D, KL], F32, kind="ExternalInput")
    dwqk = nc.dram_tensor("wqk", [QL, H * KL], F32, kind="ExternalInput")
    dwvu = nc.dram_tensor("wvu", [KL, D], F32, kind="ExternalInput")
    dbqd = nc.dram_tensor("bqd", [QL, 1], F32, kind="ExternalInput")
    dbkvd = nc.dram_tensor("bkvd", [KL, 1], F32, kind="ExternalInput")
    dbqk = nc.dram_tensor("bqk", [KL, H], F32, kind="ExternalInput")
    dbvu = nc.dram_tensor("bvuf", [1, D], F32, kind="ExternalInput")
    dbo = nc.dram_tensor("bo", [128, KT], F32, kind="ExternalInput")
    dwo = nc.dram_tensor("wo", [D, D], F32, kind="ExternalInput")
    dbwt = nc.dram_tensor("bwt", [D, D], F32, kind="ExternalInput")
    dgwt = nc.dram_tensor("gwt", [D, E], F32, kind="ExternalInput")
    dlat = nc.dram_tensor("lat", [D, ER], F32, kind="ExternalInput")
    dlbt = nc.dram_tensor("lbt", [ER, D], F32, kind="ExternalInput")
    de1 = nc.dram_tensor("e1", [E, N1], F32, kind="ExternalInput")
    de2 = nc.dram_tensor("e2", [E, N2], F32, kind="ExternalInput")
    did = nc.dram_tensor("idm", [128, 128], F32, kind="ExternalInput")
    dout = nc.dram_tensor("outT", [D, T], F32, kind="ExternalOutput")
    dgs = nc.dram_tensor("gscr", [D, T], mybir.dt.float16)

    es = ExitStack()
    with es:
        sbw = lambda st_, name, shape, dt: st_.enter_context(nc.sbuf_tensor(name, shape, dt))
        sbuf = lambda name, shape, dt: sbw(es, name, shape, dt)
        xnT = sbuf("xnT", [128, KT * T], BF16)
        bt1 = sbuf("bt1", [128, D], BF16)
        bt2 = sbuf("bt2", [N2, D], BF16)
        hg1 = sbuf("hg1", [128, T], BF16)
        hg2 = sbuf("hg2", [N2, T], BF16)
        bqk_f = sbuf("bqk_f", [KL, H], F32)

        bo_f = sbuf("bo_f", [128, KT], F32)
        bqd_f = sbuf("bqd_f", [QL, 1], F32)
        bkvd_f = sbuf("bkvd_f", [KL, 1], F32)
        ones_b = sbuf("ones_b", [128, 1], BF16)
        ones8 = sbuf("ones8", [128, 8], BF16)
        ones_f = sbuf("ones_f", [1, 128], F32)
        es_mid = ExitStack()
        cq_b = sbw(es_mid, "cq_b", [QL, T], BF16)
        ckv_b = sbw(es_mid, "ckv_b", [KL, 2 * T], BF16)
        kaug = sbw(es_mid, "kaug", [128, 8 * SKT], BF16)
        wvu8 = sbw(es_mid, "wvu8", [8, D], BF16)
        wqk_b = sbw(es_mid, "wqk_b", [QL, H * KL], BF16)
        rs_bf = sbw(es_mid, "rs_bf", [128, T], BF16)
        bw_bf = sbw(es_mid, "bw_bf", [128, 2 * D], BF16)
        es1 = ExitStack()
        id_f = sbw(es1, "id_f", [128, 128], F32)
        ckv_f = sbw(es1, "ckv_f", [KL, 2 * T], F32)
        rs_f = sbw(es1, "rs_f", [128, T], F32)
        mrs_f = sbw(es1, "mrs_f", [128, T], F32)
        rso_f = sbw(es1, "rso_f", [128, T], F32)
        srow = sbw(es1, "srow", [1, 3 * T], F32)
        xst = sbw(es1, "xst", [128, 8 * T], BF16)
        xsq = sbw(es1, "xsq", [128, 4 * T], BF16)
        mrsb = sbw(es1, "mrsb", [1, T], BF16)
        cs_kv = sbw(es1, "cs_kv", [1, KL], BF16)
        e1_b = sbw(es1, "e1_b", [E, N1], BF16)
        e2_b = sbw(es1, "e2_b", [E, N2], BF16)
        logits_f = sbw(es1, "logits_f", [E, T], F32)
        gt_b = sbw(es1, "gt_b", [E, T], BF16)
        lt_f = sbw(es1, "lt_f", [128, NT * E], F32)
        mk1 = sbw(es1, "mk1", [128, NT * E], F32)
        mk2 = sbw(es1, "mk2", [128, NT * E], F32)
        msk = sbw(es1, "msk", [128, NT * E], F32)
        gtk = sbw(es1, "gtk", [128, NT * E], F32)
        mrow = sbw(es1, "mrow", [128, 6 * NT], F32)
        gex1 = sbw(es1, "gex1", [128, T], BF16)
        gex2 = sbw(es1, "gex2", [N2, T], BF16)
        wq_b = sbw(es1, "wq_b", [128, KT * QL], BF16)
        wkv_b = sbw(es1, "wkv_b", [128, KT * KL], BF16)
        gwt_b = sbw(es1, "gwt_b", [128, KT * E], BF16)
        at1 = sbw(es1, "at1", [128, KT * N1], BF16)
        at2 = sbw(es1, "at2", [128, KT * N2], BF16)

        arena = es.enter_context(nc.psum_tensor("arena", [128, AR], F32))

        sems = {s: es.enter_context(nc.semaphore(s)) for s in
                ("dW", "dY", "dZ", "sT", "sV", "sS", "dF", "dG", "dH",
                 "dX0", "dX1", "dX2", "dX3", "dX4", "dX5", "dX6", "dX7",
                 "dL0", "dL1", "dO0", "dO1", "dP0", "dP1")}
        cnt = {s: 0 for s in sems}
        EV = {}
        block = es.enter_context(nc.Block())

        def reg(key, sem):
            EV[key] = (sem, cnt[sem])

        def I(instr, sem, key=None, n=1):
            cnt[sem] += n
            instr.then_inc(sems[sem], n)
            if key is not None:
                reg(key, sem)
            return instr

        def Ip(instr, sem, key):
            cnt[sem] += 1
            instr.then_inc(sems[sem], 1)
            assert EV[key] == (sem, cnt[sem]), (key, EV[key], (sem, cnt[sem]))

        def W(eng, *keys):
            best = {}
            for k in keys:
                if k is None or k not in EV:
                    continue
                s, v = EV[k]
                if v > 0:
                    best[s] = max(best.get(s, 0), v)
            for s, v in best.items():
                eng.wait_ge(sems[s], v)

        def vself(v):
            if cnt["sV"] > 0:
                v.wait_ge(sems["sV"], cnt["sV"])

        def sself(s):
            if cnt["sS"] > 0:
                s.wait_ge(sems["sS"], cnt["sS"])

        @block.vector
        def _(v):
            I(v.memset(ones_b[:, :], 1.0), "sV")
            I(v.memset(ones8[:, :], 1.0), "sV")
            I(v.memset(ones_f[:, :], 1.0), "sV")
            I(v.memset(wvu8[:, :], 0.0), "sV")
            I(v.memset(kaug[:, :], 0.0), "sV", ("init",))

        @block.sync
        def _(sy):
            for dsrc, ddst in ((did, id_f), (dbqd, bqd_f), (dbkvd, bkvd_f),
                               (dbqk, bqk_f), (dbo, bo_f)):
                I(sy.dma_start(out=ddst[:, :], in_=dsrc.ap()), "dY", n=16)
            reg(("smallY",), "dY")

        gsz = min(8, KT)
        ng = KT // gsz

        @block.gpsimd
        def _(g):
            for kt in range(min(8, KT)):
                I(g.dma_start(out=xnT[:, kt * T:(kt + 1) * T],
                              in_=dx.ap()[kt * 128:(kt + 1) * 128, :]),
                  "dX%d" % kt, ("so", "x", kt), n=16)

        for kt in range(KT):
            EV[("so", "sq", kt)] = ("sS", kt + 1)
        EV[("so", "rstd")] = ("sS", KT + 2)
        GEK = min(13, KT * 5 // 8 - 2) if KT >= 16 else None
        for kt in range(KT):
            off_g = 1 if (GEK is not None and kt > GEK) else 0
            EV[("st", "sq", kt)] = ("sS", KT + 2 + kt + 1 + off_g)
        if GEK is not None:
            EV[("gexp",)] = ("sS", KT + 4 + GEK)
            EV[("st", "rstd")] = ("sS", 2 * KT + 5)
        else:
            EV[("st", "rstd")] = ("sS", 2 * KT + 4)
            EV[("gexp",)] = ("sS", 2 * KT + 5)

        slot_ev = {}

        def stats_pass(bk, dsrc, ca, cbk, r2, first, extra=None, inject=None,
                       tinject=None):
            own = bk == "so"
            for kt in range(KT):
                sl = kt % 8
                buf = xnT if own else xst
                off = (kt if own else sl) * T
                if kt >= 8:
                    @block.gpsimd
                    def _(g, kt=kt, sl=sl, buf=buf, off=off):
                        if own:
                            W(g, (bk, "x", kt - 8))
                        else:
                            W(g, (bk, "x", kt - 8), (bk, "sT", kt - 8),
                              (bk, "sq", kt - 8))
                        I(g.dma_start(out=buf[:, off:off + T],
                                      in_=dsrc.ap()[kt * 128:(kt + 1) * 128, :]),
                          "dX%d" % (kt % 8), (bk, "x", kt), n=16)
                        if inject and kt in inject:
                            inject[kt](g)

                @block.tensor
                def _(t, kt=kt, buf=buf, off=off):
                    W(t, (bk, "x", kt), (bk, "sq", kt))
                    if kt == 0:
                        W(t, *first)
                    if extra is not None:
                        extra(t, kt, off)
                    for c0, cw in CH:
                        t.matmul(arena[0:8, ca + c0:ca + c0 + cw], ones8[:, :],
                                 buf[:, off + c0:off + c0 + cw],
                                 start=(kt == 0), stop=(kt == KT - 1),
                                 skip_group_check=True)
                    mm = None
                    for c0, cw in CH:
                        mm = t.matmul(arena[r2:r2 + 8, cbk + c0:cbk + c0 + cw],
                                      ones8[:, :],
                                      xsq[:, (kt % 4) * T + c0:(kt % 4) * T + c0 + cw],
                                      start=(kt == 0), stop=(kt == KT - 1),
                                      skip_group_check=True)
                    I(mm, "sT", (bk, "sT", kt))
                    if tinject and kt in tinject:
                        tinject[kt](t)

        def _w_wq(g):
            I(g.dma_start(out=wq_b.ap().rearrange("p (kt q) -> p kt q", kt=KT),
                          in_=dwq.ap().rearrange("(kt p) q -> p kt q", p=128)), "dW", n=16)

        def _w_wkv(g):
            I(g.dma_start(out=wkv_b.ap().rearrange("p (kt q) -> p kt q", kt=KT),
                          in_=dwkv.ap().rearrange("(kt p) q -> p kt q", p=128)), "dW", n=16)

        def _w_gwt(i):
            def f(g):
                k0, k1 = i * gsz, (i + 1) * gsz
                I(g.dma_start(
                    out=gwt_b[:, k0 * E:k1 * E].rearrange("p (kt e) -> p kt e", kt=gsz),
                    in_=dgwt.ap()[k0 * 128:k1 * 128, :].rearrange("(kt p) e -> p kt e", p=128)),
                  "dW", n=16)
            return f

        def _w_at1(i):
            def f(g):
                k0, k1 = i * gsz, (i + 1) * gsz
                I(g.dma_start(
                    out=at1[:, k0 * N1:k1 * N1].rearrange("p (kt e) -> p kt e", kt=gsz),
                    in_=dlat.ap()[k0 * 128:k1 * 128, 0:N1].rearrange("(kt p) e -> p kt e", p=128)),
                  "dW", n=16)
            return f

        def _w_at2(i, last=False):
            def f(g):
                k0, k1 = i * gsz, (i + 1) * gsz
                I(g.dma_start(
                    out=at2[:, k0 * N2:k1 * N2].rearrange("p (kt e) -> p kt e", kt=gsz),
                    in_=dlat.ap()[k0 * 128:k1 * 128, N1:ER].rearrange("(kt p) e -> p kt e", p=128)),
                  "dW", n=16)
                if last:
                    reg(("wA",), "dW")
            return f

        wparts = [_w_wq, _w_wkv] + [_w_gwt(i) for i in range(ng)] \
            + [_w_at1(i) for i in range(ng)] \
            + [_w_at2(i, last=(i == ng - 1)) for i in range(ng)]
        winj = {}
        if KT >= 16:
            step = max(1, (KT - 10) // len(wparts))
            for n_, fn in enumerate(wparts):
                winj[8 + n_ * step] = fn
            assert 8 + (len(wparts) - 1) * step < KT
        else:
            @block.gpsimd
            def _(g):
                for fn in wparts:
                    fn(g)

        stats_pass("so", dx, CA, CB, 0, (("init",),), inject=winj)


        @block.tensor
        def _(t):
            W(t, ("wA",))
            mm = None
            for kt in range(KT):
                mm = t.matmul(arena[0:8, CD:CD + KL], ones8[:, :],
                              wkv_b[:, kt * KL:(kt + 1) * KL],
                              start=(kt == 0), stop=(kt == KT - 1),
                              skip_group_check=True)
            I(mm, "sT", ("cskvT",))

        @block.vector
        def _(v):
            vself(v)
            W(v, ("cskvT",))
            I(v.tensor_copy(cs_kv[0:1, 0:KL], arena[0:1, CD:CD + KL]), "sV", ("cskv",))


        @block.vector
        def _(v):
            vself(v)
            W(v, ("so", "sT", KT - 1))
            I(v.tensor_scalar(srow[0:1, 0:T], arena[0:1, CA:CA + T],
                              1.0 / D, None, ALU.mult), "sV")
            vself(v)
            I(v.tensor_scalar(srow[0:1, T:2 * T], arena[0:1, CB:CB + T],
                              1.0 / D, EPS, ALU.mult, ALU.add), "sV")
            vself(v)
            I(v.tensor_tensor(srow[0:1, 2 * T:3 * T], srow[0:1, 0:T],
                              srow[0:1, 0:T], ALU.mult), "sV")
            vself(v)
            I(v.tensor_tensor(srow[0:1, 2 * T:3 * T], srow[0:1, T:2 * T],
                              srow[0:1, 2 * T:3 * T], ALU.subtract), "sV", ("so", "var"))

        @block.tensor
        def _(t):
            W(t, ("so", "rstd"))
            mm = None
            for c0, cw in CH:
                mm = t.matmul(arena[0:128, CA + c0:CA + c0 + cw], ones_f[:, :],
                              srow[0:1, 2 * T + c0:2 * T + c0 + cw],
                              start=True, stop=True, skip_group_check=True)
            I(mm, "sT", ("so", "bc1"))

        @block.vector
        def _(v):
            vself(v)
            W(v, ("so", "bc1"))
            I(v.tensor_copy(rs_f[:, :], arena[0:128, CA:CA + T]), "sV", ("so", "rsc"))
            vself(v)
            I(v.tensor_copy(rs_bf[:, :], arena[0:128, CA:CA + T]), "sV", ("rsbf",))

        @block.tensor
        def _(t):
            W(t, ("so", "rsc"), ("rsbf",))
            mm = None
            for c0, cw in CH:
                mm = t.matmul(arena[0:128, CA + c0:CA + c0 + cw], ones_f[:, :],
                              srow[0:1, c0:c0 + cw],
                              start=True, stop=True, skip_group_check=True)
            I(mm, "sT", ("so", "bc2"))

        @block.vector
        def _(v):
            vself(v)
            W(v, ("so", "bc2"))
            I(v.tensor_copy(mrs_f[:, :], arena[0:128, CA:CA + T]), "sV", ("so", "meanc"))

        for kt in range(KT):
            @block.vector
            def _(v, kt=kt):
                vself(v)
                W(v, ("so", "meanc"), ("so", "sT", kt), ("so", "sq", kt))
                I(v.scalar_tensor_tensor(xnT[:, kt * T:(kt + 1) * T],
                                         xnT[:, kt * T:(kt + 1) * T], 1.0,
                                         mrs_f[:, :], ALU.mult, ALU.subtract),
                  "sV", ("b", "xn", kt))

            @block.tensor
            def _(t, kt=kt):
                W(t, ("b", "xn", kt))
                if kt == 0:
                    W(t, ("cskv",))
                xn = xnT[:, kt * T:(kt + 1) * T]
                for c0, cw in CH:
                    t.matmul(arena[0:QL, CC + c0:CC + c0 + cw],
                             wq_b[:, kt * QL:(kt + 1) * QL], xn[:, c0:c0 + cw],
                             start=(kt == 0), stop=(kt == KT - 1), skip_group_check=True)
                for c0, cw in CH:
                    t.matmul(arena[32:32 + KL, CC + c0:CC + c0 + cw],
                             wkv_b[:, kt * KL:(kt + 1) * KL], xn[:, c0:c0 + cw],
                             start=(kt == 0), stop=(kt == KT - 1), skip_group_check=True)
                for c0, cw in CH:
                    t.matmul(arena[0:E, CA + c0:CA + c0 + cw],
                             gwt_b[:, kt * E:(kt + 1) * E], xn[:, c0:c0 + cw],
                             start=(kt == 0), stop=(kt == KT - 1), skip_group_check=True)
                for c0, cw in CH:
                    t.matmul(arena[0:N1, CB + c0:CB + c0 + cw],
                             at1[:, kt * N1:(kt + 1) * N1], xn[:, c0:c0 + cw],
                             start=(kt == 0), stop=(kt == KT - 1), skip_group_check=True)
                mm = None
                for c0, cw in CH:
                    mm = t.matmul(arena[0:N2, CD + c0:CD + c0 + cw],
                                  at2[:, kt * N2:(kt + 1) * N2], xn[:, c0:c0 + cw],
                                  start=(kt == 0), stop=(kt == KT - 1), skip_group_check=True)
                I(mm, "sT", ("b", "sT", kt))


        @block.gpsimd
        def _(g):
            for kt in range(min(8, KT)):
                W(g, ("so", "x", KT - 8 + kt),
                  ("so", "x", kt) if kt < KT else None)
                I(g.dma_start(out=xst[:, (kt % 8) * T:(kt % 8 + 1) * T],
                              in_=dxo.ap()[kt * 128:(kt + 1) * 128, :]),
                  "dX%d" % (kt % 8), ("st", "x", kt), n=16)

        @block.gpsimd
        def _(g):
            I(g.dma_start(out=wqk_b[:, :], in_=dwqk.ap()), "dZ", n=16)
            W(g, ("init",))
            I(g.dma_start(out=wvu8[2:6, :], in_=dwvu.ap()), "dZ", n=16)
            I(g.dma_start(out=wvu8[0:1, :], in_=dbvu.ap()), "dZ", n=16)
            I(g.dma_start(out=e1_b[:, :], in_=de1.ap()), "dZ", n=16)
            I(g.dma_start(out=e2_b[:, :], in_=de2.ap()), "dZ", n=16)
            I(g.dma_start(out=bt1[:, :], in_=dlbt.ap()[0:128, :]), "dZ", n=16)
            I(g.dma_start(out=bt2[:, :], in_=dlbt.ap()[128:ER, :]), "dZ", n=16)
            reg(("lbtW",), "dZ")

        def emit_bwload(mo):
            @block.gpsimd
            def _(g, mo=mo):
                if mo >= 2:
                    W(g, ("fT", mo - 2))
                for i in range(ng):
                    k0, k1 = i * gsz, (i + 1) * gsz
                    I(g.dma_start(
                        out=bw_bf[:, (mo % 2) * D + k0 * 128:(mo % 2) * D + k1 * 128]
                            .rearrange("p (ki c) -> p ki c", ki=gsz),
                        in_=dbwt.ap()[k0 * 128:k1 * 128, mo * 128:(mo + 1) * 128]
                            .rearrange("(ki p) c -> p ki c", p=128)),
                      "dP%d" % (mo % 2), n=16)
                reg(("wLb", mo), "dP%d" % (mo % 2))

        emit_bwload(0)
        emit_bwload(1)

        @block.vector
        def _(v):
            vself(v)
            W(v, ("b", "sT", KT - 1), ("smallY",))
            I(v.tensor_tensor(rso_f[0:QL, 0:T], arena[0:QL, CC:CC + T], rs_f[0:QL, :],
                              ALU.mult), "sV")
            vself(v)
            I(v.tensor_scalar(cq_b[:, :], rso_f[0:QL, 0:T], bqd_f[:, :], None,
                              ALU.add), "sV", ("cqev",))
            vself(v)
            I(v.tensor_tensor(ckv_f[0:KL, 0:T], arena[32:32 + KL, CC:CC + T],
                              rs_f[0:KL, :], ALU.mult), "sV")
            vself(v)
            I(v.tensor_scalar(ckv_f[0:KL, 0:T], ckv_f[0:KL, 0:T], bkvd_f[:, :], None,
                              ALU.add), "sV")
            vself(v)
            I(v.tensor_copy(ckv_b[:, 0:T], ckv_f[0:KL, 0:T]), "sV", ("ckev0",))
            vself(v)
            I(v.tensor_tensor(logits_f[:, :], arena[0:E, CA:CA + T], rs_f[0:E, :],
                              ALU.mult), "sV", ("lgev",))

        @block.tensor
        def _(t):
            W(t, ("lgev",))
            mm = None
            for tt in range(NT):
                mm = t.transpose(arena[0:128, CA + tt * E:CA + (tt + 1) * E],
                                 logits_f[0:E, tt * 128:(tt + 1) * 128], id_f[0:E, 0:E])
            I(mm, "sT", ("ltT",))

        @block.vector
        def _(v):
            vself(v)
            W(v, ("ltT",))
            I(v.tensor_copy(lt_f[:, :], arena[0:128, CA:CA + NT * E]), "sV", ("ltc",))
            for tt in range(NT):
                sl = slice(tt * E, (tt + 1) * E)
                vself(v)
                I(v.reduce_max(mrow[:, tt:tt + 1], lt_f[:, sl], axis=AX.X), "sV")
                vself(v)
                I(v.tensor_scalar(mk1[:, sl], lt_f[:, sl], mrow[:, tt:tt + 1], None,
                                  ALU.is_equal), "sV")
                vself(v)
                I(v.scalar_tensor_tensor(msk[:, sl], mk1[:, sl], -1e30, lt_f[:, sl],
                                         ALU.mult, ALU.add), "sV")
                vself(v)
                I(v.reduce_max(mrow[:, NT + tt:NT + tt + 1], msk[:, sl], axis=AX.X), "sV")
                vself(v)
                I(v.tensor_scalar(mk2[:, sl], msk[:, sl], mrow[:, NT + tt:NT + tt + 1], None,
                                  ALU.is_equal), "sV")
            vself(v)
            I(v.tensor_tensor(mrow[:, 4 * NT:5 * NT], mrow[:, 0:NT], mrow[:, NT:2 * NT],
                              ALU.subtract), "sV", ("gdel",))

        def ckv_oth_extra(t, kt, off):
            for c0, cw in CH:
                t.matmul(arena[32:32 + KL, CC + c0:CC + c0 + cw],
                         wkv_b[:, kt * KL:(kt + 1) * KL],
                         xst[:, off + c0:off + c0 + cw],
                         start=(kt == 0), stop=False, skip_group_check=True)

        def gating_v2a(v):
            vself(v)
            W(v, ("gexp",))
            I(v.tensor_scalar(mrow[:, 4 * NT:5 * NT], mrow[:, 5 * NT:6 * NT], 1.0, None,
                              ALU.add), "sV")
            vself(v)
            I(v.reciprocal(mrow[:, 2 * NT:3 * NT], mrow[:, 4 * NT:5 * NT]), "sV")
            vself(v)
            I(v.tensor_scalar(mrow[:, 3 * NT:4 * NT], mrow[:, 2 * NT:3 * NT], -2.0, 2.0,
                              ALU.mult, ALU.add), "sV")
            for tt in range(NT):
                sl = slice(tt * E, (tt + 1) * E)
                vself(v)
                I(v.tensor_scalar(msk[:, sl], mk1[:, sl], mrow[:, 2 * NT + tt:2 * NT + tt + 1],
                                  2.0, ALU.mult, ALU.mult), "sV")
                vself(v)
                I(v.scalar_tensor_tensor(gtk[:, sl], mk2[:, sl],
                                         mrow[:, 3 * NT + tt:3 * NT + tt + 1], msk[:, sl],
                                         ALU.mult, ALU.add), "sV")
            reg(("gtk",), "sV")

        def gating_v2b(v):
            vself(v)
            W(v, ("gtT",))
            I(v.tensor_copy(gt_b[:, :], arena[0:E, CA:CA + T]), "sV", ("gtc",))
            vself(v)
            W(v, ("gx1T",))
            I(v.tensor_copy(gex1[:, :], arena[0:N1, CA:CA + T]), "sV", ("gx1c",))
            vself(v)
            I(v.tensor_tensor(hg1[:, :], arena[0:N1, CB:CB + T], gex1[:, :], ALU.mult),
              "sV", ("hg1",))
            vself(v)
            W(v, ("gx2T",))
            I(v.tensor_copy(gex2[:, :], arena[0:N2, CB:CB + T]), "sV", ("gx2c",))
            vself(v)
            I(v.tensor_tensor(hg2[:, :], arena[0:N2, CD:CD + T], gex2[:, :], ALU.mult),
              "sV", ("hg2",))

        def gtT_fn(t):
            W(t, ("gtk",), ("ltc",))
            mm = None
            for tt in range(NT):
                mm = t.transpose(arena[0:E, CA + tt * 128:CA + (tt + 1) * 128],
                                 gtk[0:128, tt * E:(tt + 1) * E], id_f[:, :])
            I(mm, "sT", ("gtT",))

        def gx1_fn(t):
            W(t, ("gtc",), ("lbtW",))
            mm = None
            for c0, cw in CH:
                mm = t.matmul(arena[0:N1, CA + c0:CA + c0 + cw], e1_b[:, :],
                              gt_b[0:E, c0:c0 + cw], start=True, stop=True,
                              skip_group_check=True)
            I(mm, "sT", ("gx1T",))

        def gx2_fn(t):
            W(t, ("hg1",))
            mm = None
            for c0, cw in CH:
                mm = t.matmul(arena[0:N2, CB + c0:CB + c0 + cw], e2_b[:, :],
                              gt_b[0:E, c0:c0 + cw], start=True, stop=True,
                              skip_group_check=True)
            I(mm, "sT", ("gx2T",))

        gtinj = None
        if KT >= 16:
            k1_, k2_, k3_ = KT * 5 // 8, KT * 6 // 8, KT * 7 // 8
            gtinj = {k1_: gtT_fn, k2_: gx1_fn, k3_: gx2_fn}
            tb_ = cnt["sT"]
            EV[("gtT",)] = ("sT", tb_ + k1_ + 2)
            EV[("gx1T",)] = ("sT", tb_ + k2_ + 3)
            EV[("gx2T",)] = ("sT", tb_ + k3_ + 4)

            @block.vector
            def _(v):
                gating_v2a(v)

            @block.vector
            def _(v):
                gating_v2b(v)

        stats_pass("st", dxo, CC, CC, 64, (("cqev",), ("ckev0",)), extra=ckv_oth_extra, tinject=gtinj)

        @block.vector
        def _(v):
            vself(v)
            W(v, ("st", "sT", KT - 1), ("so", "bc2"))
            I(v.tensor_scalar(srow[0:1, 0:T], arena[0:1, CC:CC + T],
                              1.0 / D, None, ALU.mult), "sV")
            vself(v)
            I(v.tensor_scalar(srow[0:1, T:2 * T], arena[64:65, CC:CC + T],
                              1.0 / D, EPS, ALU.mult, ALU.add), "sV")
            vself(v)
            I(v.tensor_tensor(srow[0:1, 2 * T:3 * T], srow[0:1, 0:T],
                              srow[0:1, 0:T], ALU.mult), "sV")
            vself(v)
            I(v.tensor_tensor(srow[0:1, 2 * T:3 * T], srow[0:1, T:2 * T],
                              srow[0:1, 2 * T:3 * T], ALU.subtract), "sV", ("st", "var"))

        @block.vector
        def _(v):
            vself(v)
            W(v, ("st", "rstd"))
            I(v.scalar_tensor_tensor(srow[0:1, T:2 * T], srow[0:1, 0:T],
                                     -1.0, srow[0:1, 2 * T:3 * T], ALU.mult, ALU.mult),
              "sV")
            vself(v)
            I(v.tensor_copy(mrsb[0:1, 0:T], srow[0:1, T:2 * T]), "sV", ("mrsb",))

        @block.tensor
        def _(t):
            W(t, ("mrsb",))
            mm = None
            for ci, (c0, cw) in enumerate(CH):
                mm = t.matmul(arena[32:32 + KL, CC + c0:CC + c0 + cw],
                              cs_kv[0:1, 0:KL], mrsb[0:1, c0:c0 + cw],
                              start=False, stop=(ci == NCH - 1), skip_group_check=True)
            I(mm, "sT", ("ckoT",))

        if KT >= 16:
            assert EV[("gtT",)] == ("sT", tb_ + k1_ + 2)
            assert EV[("gx1T",)] == ("sT", tb_ + k2_ + 3)
            assert EV[("gx2T",)] == ("sT", tb_ + k3_ + 4)
        else:
            @block.vector
            def _(v):
                gating_v2a(v)

            @block.tensor
            def _(t):
                gtT_fn(t)

            @block.vector
            def _(v):
                vself(v)
                W(v, ("gtT",))
                I(v.tensor_copy(gt_b[:, :], arena[0:E, CA:CA + T]), "sV", ("gtc",))

            @block.tensor
            def _(t):
                gx1_fn(t)

            @block.vector
            def _(v):
                vself(v)
                W(v, ("gx1T",))
                I(v.tensor_copy(gex1[:, :], arena[0:N1, CA:CA + T]), "sV", ("gx1c",))
                vself(v)
                I(v.tensor_tensor(hg1[:, :], arena[0:N1, CB:CB + T], gex1[:, :],
                                  ALU.mult), "sV", ("hg1",))

            @block.tensor
            def _(t):
                gx2_fn(t)

            @block.vector
            def _(v):
                vself(v)
                W(v, ("gx2T",))
                I(v.tensor_copy(gex2[:, :], arena[0:N2, CB:CB + T]), "sV", ("gx2c",))
                vself(v)
                I(v.tensor_tensor(hg2[:, :], arena[0:N2, CD:CD + T], gex2[:, :],
                                  ALU.mult), "sV", ("hg2",))


        @block.tensor
        def _(t):
            W(t, ("st", "rstd"))
            mm = None
            for c0, cw in CH:
                mm = t.matmul(arena[64:64 + KL, CC + c0:CC + c0 + cw],
                              ones_f[0:1, 0:KL],
                              srow[0:1, 2 * T + c0:2 * T + c0 + cw],
                              start=True, stop=True, skip_group_check=True)
            I(mm, "sT", ("st", "bc1"))

        @block.vector
        def _(v):
            vself(v)
            W(v, ("st", "bc1"))
            I(v.tensor_copy(rso_f[0:KL, :], arena[64:64 + KL, CC:CC + T]),
              "sV", ("st", "rsc"))
            vself(v)
            W(v, ("ckoT",))
            I(v.tensor_tensor(ckv_f[0:KL, T:2 * T], arena[32:32 + KL, CC:CC + T],
                              rso_f[0:KL, :], ALU.mult), "sV")
            vself(v)
            I(v.tensor_scalar(ckv_f[0:KL, T:2 * T], ckv_f[0:KL, T:2 * T],
                              bkvd_f[:, :], None, ALU.add), "sV")
            vself(v)
            I(v.tensor_copy(ckv_b[:, T:2 * T], ckv_f[0:KL, T:2 * T]), "sV", ("ckev1",))

        @block.scalar
        def _(s):
            for bk in ("so", "st"):
                for kt in range(KT):
                    src_ = (xnT[:, kt * T:(kt + 1) * T] if bk == "so"
                            else xst[:, (kt % 8) * T:(kt % 8 + 1) * T])
                    war = (bk, "sT", kt - 4)
                    if kt < 4 and bk == "st":
                        war = ("so", "sT", KT - 4 + kt)
                    sself(s)
                    W(s, (bk, "x", kt), war)
                    Ip(s.activation(xsq[:, (kt % 4) * T:(kt % 4 + 1) * T],
                                    src_, AF.Square),
                       "sS", (bk, "sq", kt))
                    if bk == "st" and GEK is not None and kt == GEK:
                        sself(s)
                        W(s, ("gdel",))
                        Ip(s.activation(mrow[:, 5 * NT:6 * NT],
                                        mrow[:, 4 * NT:5 * NT], AF.Exp,
                                        scale=-1.0), "sS", ("gexp",))
                sself(s)
                W(s, (bk, "var"))
                I(s.activation(srow[0:1, T:2 * T], srow[0:1, 2 * T:3 * T], AF.Ln), "sS")
                sself(s)
                Ip(s.activation(srow[0:1, 2 * T:3 * T], srow[0:1, T:2 * T], AF.Exp,
                                scale=-0.5), "sS", (bk, "rstd"))

        if GEK is None:
            @block.scalar
            def _(s):
                sself(s)
                W(s, ("gdel",))
                Ip(s.activation(mrow[:, 5 * NT:6 * NT], mrow[:, 4 * NT:5 * NT],
                                AF.Exp, scale=-1.0), "sS", ("gexp",))

        @block.tensor
        def _(t):
            W(t, ("hg2",), ("ckev1",))
            mm = None
            for kt in range(SKT):
                mm = t.transpose(arena[0:128, KG + 4 * kt:KG + 4 * kt + 4],
                                 ckv_f[0:KL, kt * 128:(kt + 1) * 128], id_f[0:KL, 0:KL])
            I(mm, "sT", ("kgT",))

        @block.vector
        def _(v):
            vself(v)
            W(v, ("kgT",))
            I(v.tensor_copy(
                kaug.ap().rearrange("p (g c) -> p g c", g=SKT)[:, :, 2:6],
                arena[0:128, KG:KG + 4 * SKT].rearrange("p (g c) -> p g c", g=SKT)),
              "sV", ("kgc",))
            vself(v)
            I(v.memset(kaug.ap().rearrange("p (g c) -> p g c", g=SKT)[:, :, 0:1], 1.0),
              "sV", ("kgo",))

        es1.close()
        es_ao = ExitStack()
        aoT = sbw(es_ao, "aoT", [128, H * T], BF16)
        es_attn = ExitStack()
        pbuf = sbw(es_attn, "pbuf", [128, NPB * T], BF16)
        qb0 = sbw(es_attn, "qb0", [KL, T], BF16)
        qb1 = sbw(es_attn, "qb1", [KL, T], BF16)
        a2b0 = sbw(es_attn, "a2b0", [8, T], BF16)
        a2b1 = sbw(es_attn, "a2b1", [8, T], BF16)
        dinvb = sbw(es_attn, "dinvb", [128, T], F32)
        dtmp = dinvb
        dinv_bf = sbw(es_attn, "dinv_bf", [1, T], BF16)
        onesb = sbw(es_attn, "onesb", [1, 128], BF16)
        fsb = sbw(es_attn, "fsb", [128, T], mybir.dt.float16)
        qbs = [qb0, qb1]
        a2bs = [a2b0, a2b1]

        assert H == KT
        pair_free = {0: [("gx1c",)] * NCH, 1: [("gx2c",)] * NCH}

        spair = {}
        qloc = {}
        bcdloc = {}
        uploc = {}

        def a2block(t, hh, j, explicit_exp=False):
            if explicit_exp:
                W(t, ("exp", hh, j))
            u = hh * SKT + j
            mm = None
            for c, (c0, cw) in enumerate(CH):
                if j == 0:
                    if hh > 0:
                        W(t, ("a2s", hh - 1, c))
                    elif c == 0:
                        W(t, ("ckev1",), ("kgo",))
                mm = t.matmul(arena[0:8, A2C + c0:A2C + c0 + cw],
                              kaug[:, 8 * j:8 * j + 8],
                              pbuf[:, (u % NPB) * T + c0:(u % NPB) * T + c0 + cw],
                              start=(j == 0), stop=(j == SKT - 1),
                              skip_group_check=True)
            I(mm, "sT", ("a2", hh, j))

        def ffn_mms(mo):
            n = (KT + 2) * NCH
            i = 0
            for ki in range(KT):
                for c0, cw in CH:
                    yield (bw_bf[:, (mo % 2) * D + ki * 128:(mo % 2) * D + (ki + 1) * 128],
                           xnT[:, ki * T + c0:ki * T + c0 + cw], c0, cw,
                           ki == 0, False)
            for c0, cw in CH:
                yield (bt1[:, mo * 128:(mo + 1) * 128], hg1[:, c0:c0 + cw], c0, cw,
                       False, False)
            for ci, (c0, cw) in enumerate(CH):
                yield (bt2[0:N2, mo * 128:(mo + 1) * 128], hg2[0:N2, c0:c0 + cw], c0, cw,
                       False, ci == NCH - 1)

        def emit_bcd(t, hh, pb_):
            W(t, ("dinv", hh))
            for qc, (c0, cw) in enumerate(CH):
                W(t, pair_free[pb_][qc])
                I(t.matmul(arena[0:128, PB[pb_] + c0:PB[pb_] + c0 + cw], onesb[:, :],
                           dinv_bf[0:1, c0:c0 + cw], start=True, stop=True,
                           skip_group_check=True), "sT", ("bcd", hh, qc))
            bcdloc[hh] = PB[pb_]
            pair_free[pb_] = [("dcp", hh, qc) for qc in range(NCH)]

        def emit_up(t, hh, pu):
            W(t, ("a2s", hh, NCH - 1))
            for qc, (c0, cw) in enumerate(CH):
                W(t, pair_free[pu][qc])
                I(t.matmul(arena[0:128, PB[pu] + c0:PB[pu] + c0 + cw],
                           wvu8[0:8, hh * DH:(hh + 1) * DH],
                           a2bs[hh % 2][0:8, c0:c0 + cw], start=True, stop=True,
                           skip_group_check=True), "sT", ("up", hh, qc))
            uploc[hh] = PB[pu]
            pair_free[pu] = [("uptt", hh, qc) for qc in range(NCH)]

        def emit_vtail(v, hh):
            for qc, (c0, cw) in enumerate(CH):
                vself(v)
                W(v, ("bcd", hh, qc))
                Ip(v.tensor_copy(dinvb[:, c0:c0 + cw],
                                 arena[0:128, bcdloc[hh] + c0:bcdloc[hh] + c0 + cw]),
                   "sV", ("dcp", hh, qc))
            for qc, (c0, cw) in enumerate(CH):
                vself(v)
                W(v, ("up", hh, qc))
                Ip(v.tensor_tensor(aoT[:, hh * T + c0:hh * T + c0 + cw],
                                   arena[0:128, uploc[hh] + c0:uploc[hh] + c0 + cw],
                                   dinvb[:, c0:c0 + cw], ALU.mult), "sV", ("uptt", hh, qc))

        p0 = 1

        @block.tensor
        def _(t, p0=p0):
            W(t, ("hg2",), ("cqev",), ("lbtW",), *pair_free[p0])
            for ci, (c0, cw) in enumerate(CH):
                I(t.matmul(arena[0:KL, PB[p0] + c0:PB[p0] + c0 + cw], wqk_b[0:QL, 0:KL],
                           cq_b[0:QL, c0:c0 + cw], start=True, stop=True,
                           skip_group_check=True), "sT", ("q", 0, ci))
        qloc[0] = PB[p0]
        pair_free[p0] = [("qev", 0, ci) for ci in range(NCH)]

        @block.vector
        def _(v, p0=p0):
            for ci, (c0, cw) in enumerate(CH):
                vself(v)
                W(v, ("q", 0, ci))
                I(v.tensor_scalar(qbs[0][0:KL, c0:c0 + cw],
                                  arena[0:KL, PB[p0] + c0:PB[p0] + c0 + cw],
                                  bqk_f[:, 0:1], QSC, ALU.add, ALU.mult),
                  "sV", ("qev", 0, ci))
            vself(v)
            I(v.memset(onesb[:, :], 1.0), "sV", ("onesb",))

        for h in range(H):
            base_s = cnt["sS"]
            for kt in range(SKT):
                extra = 2 if (kt > DKT and h > 0) else 0
                EV[("exp", h, kt)] = ("sS", base_s + kt + 1 + extra)
            base_s += SKT + (2 if h > 0 else 0)
            cnt["sS"] = base_s
            if h == H - 1:
                EV[("dinv", h)] = ("sS", base_s + 2)
            else:
                EV[("dinv", h)] = ("sS", base_s + DKT + 3)
            off = cnt["sV"]
            if h > 0:
                for c in range(NCH):
                    EV[("a2s", h - 1, c)] = ("sV", off + c + 1)
                off += NCH
            if h + 1 < H:
                for ci in range(NCH):
                    EV[("qev", h + 1, ci)] = ("sV", off + ci + 1)
                off += NCH
            if h > 0:
                for qc in range(NCH):
                    EV[("dcp", h - 1, qc)] = ("sV", off + qc + 1)
                off += NCH
                for qc in range(NCH):
                    EV[("uptt", h - 1, qc)] = ("sV", off + qc + 1)
                off += NCH
            EV[("fcp", h)] = ("sV", off + 1)

            @block.tensor
            def _(t, h=h):
                fq = list(ffn_mms(h))
                fpos = [0]

                def emit_ffn(n):
                    while n > 0 and fpos[0] < len(fq):
                        lhsT, rhs, c0, cw, st_, sp_ = fq[fpos[0]]
                        if fpos[0] == 0:
                            W(t, ("wLb", h), ("fcp", h - 1),
                              ("hg2",) if h == 0 else None)
                        mm = t.matmul(arena[0:128, FC + c0:FC + c0 + cw], lhsT, rhs,
                                      start=st_, stop=sp_, skip_group_check=True)
                        if fpos[0] == len(fq) - 1:
                            I(mm, "sT", ("fT", h))
                        fpos[0] += 1
                        n -= 1

                for kt in range(SKT):
                    if kt < SKT - 2:
                        if h == 0 and kt < 3:
                            emit_ffn(10)
                        else:
                            emit_ffn(-(-(len(fq) - fpos[0]) // (SKT - 2 - kt)))
                    if kt < 3:
                        if h > 0:
                            a2block(t, h - 1, SKT - 3 + kt)
                    else:
                        a2block(t, h, kt - 3)
                    p = kt % 2
                    if kt == 0:
                        W(t, ("qev", h, NCH - 1))
                    spair[(h, kt)] = PB[p]
                    mm = None
                    for c, (c0, cw) in enumerate(CH):
                        W(t, pair_free[p][c])
                        mm = t.matmul(arena[0:128, PB[p] + c0:PB[p] + c0 + cw],
                                      ckv_b[0:KL, kt * 128:(kt + 1) * 128],
                                      qbs[h % 2][0:KL, c0:c0 + cw],
                                      start=True, stop=True, skip_group_check=True)
                    I(mm, "sT", ("sc", h, kt))
                    pair_free[p] = [("exp", h, kt)] * NCH
                    if kt == QKT and h + 1 < H:
                        pq = (kt + 1) % 2
                        for ci, (c0, cw) in enumerate(CH):
                            W(t, pair_free[pq][ci])
                            I(t.matmul(arena[0:KL, PB[pq] + c0:PB[pq] + c0 + cw],
                                       wqk_b[0:QL, (h + 1) * KL:(h + 2) * KL],
                                       cq_b[0:QL, c0:c0 + cw], start=True, stop=True,
                                       skip_group_check=True), "sT", ("q", h + 1, ci))
                        qloc[h + 1] = PB[pq]
                        pair_free[pq] = [("qev", h + 1, ci) for ci in range(NCH)]
                    if kt == BKT and h > 0:
                        emit_bcd(t, h - 1, (kt + 1) % 2)
                    if kt == UKT and h > 0:
                        emit_up(t, h - 1, (kt + 1) % 2)
                emit_ffn(len(fq))

            if h > 0:
                @block.gpsimd
                def _(g, h=h):
                    W(g, ("fcp", h - 1))
                    I(g.dma_start(out=dgs.ap()[(h - 1) * 128:h * 128, :], in_=fsb[:, :]),
                      "dF" if (h - 1) % 2 == 0 else "dH", ("fst", h - 1), n=16)

            @block.vector
            def _(v, h=h):
                if h > 0:
                    for c, (c0, cw) in enumerate(CH):
                        vself(v)
                        W(v, ("a2", h - 1, SKT - 1), ("up", h - 3, NCH - 1),
                          ("dinv", h - 3) if h >= 3 else None)
                        Ip(v.tensor_copy(a2bs[(h - 1) % 2][0:8, c0:c0 + cw],
                                         arena[0:8, A2C + c0:A2C + c0 + cw]),
                           "sV", ("a2s", h - 1, c))
                if h + 1 < H:
                    for ci, (c0, cw) in enumerate(CH):
                        vself(v)
                        W(v, ("q", h + 1, ci))
                        inst = v.tensor_scalar(qbs[(h + 1) % 2][0:KL, c0:c0 + cw],
                                               arena[0:KL, qloc[h + 1] + c0:qloc[h + 1] + c0 + cw],
                                               bqk_f[:, h + 1:h + 2], QSC, ALU.add, ALU.mult)
                        Ip(inst, "sV", ("qev", h + 1, ci))
                if h > 0:
                    emit_vtail(v, h - 1)
                vself(v)
                W(v, ("fT", h), ("fst", h - 1))
                Ip(v.tensor_tensor(fsb[:, :], arena[0:128, FC:FC + T], rs_bf[:, :],
                                   ALU.mult), "sV", ("fcp", h))

            if h + 2 < H:
                emit_bwload(h + 2)

        @block.tensor
        def _(t):
            for j in (SKT - 3, SKT - 2, SKT - 1):
                a2block(t, H - 1, j, explicit_exp=True)

        PW4 = [FC, A2C, PB[0], PB[1]]
        EV[("wL", 0)] = ("dP0", cnt["dP0"] + 16 * ng)

        @block.tensor
        def _(t):
            W(t, ("wL", 0), ("fcp", H - 1))
            for ki in range(KT - 1):
                for c0, cw in CH:
                    t.matmul(arena[0:128, FC + c0:FC + c0 + cw],
                             bw_bf[:, ki * 128:(ki + 1) * 128],
                             aoT[:, ki * T + c0:ki * T + c0 + cw],
                             start=(ki == 0), stop=False, skip_group_check=True)

        @block.tensor
        def _(t):
            emit_bcd(t, H - 1, 0)
            emit_up(t, H - 1, 1)

        off = cnt["sV"]
        for c in range(NCH):
            EV[("a2s", H - 1, c)] = ("sV", off + c + 1)
        off += NCH
        for qc in range(NCH):
            EV[("dcp", H - 1, qc)] = ("sV", off + qc + 1)
        off += NCH
        for qc in range(NCH):
            EV[("uptt", H - 1, qc)] = ("sV", off + qc + 1)

        @block.vector
        def _(v):
            for c, (c0, cw) in enumerate(CH):
                vself(v)
                W(v, ("a2", H - 1, SKT - 1), ("up", H - 3, NCH - 1), ("dinv", H - 3))
                Ip(v.tensor_copy(a2bs[(H - 1) % 2][0:8, c0:c0 + cw],
                                 arena[0:8, A2C + c0:A2C + c0 + cw]),
                   "sV", ("a2s", H - 1, c))
            emit_vtail(v, H - 1)

        @block.gpsimd
        def _(g):
            W(g, ("fcp", H - 1))
            I(g.dma_start(out=dgs.ap()[(H - 1) * 128:H * 128, :], in_=fsb[:, :]),
              "dF" if (H - 1) % 2 == 0 else "dH", ("fst", H - 1), n=16)

        cnt["sS"] -= H * SKT + 2 * (H - 1)

        @block.scalar
        def _(s):
            for h in range(H):
                for kt in range(SKT):
                    u = h * SKT + kt
                    war = None
                    if u >= NPB:
                        up_, uk = divmod(u - NPB, SKT)
                        war = ("a2", up_, uk)
                    W(s, ("sc", h, kt), war)
                    sl = u % NPB
                    inst = s.activation(pbuf[:, sl * T:(sl + 1) * T],
                                        arena[0:128, spair[(h, kt)]:spair[(h, kt)] + T],
                                        AF.Exp)
                    Ip(inst, "sS", ("exp", h, kt))
                    if kt == DKT and h > 0:
                        W(s, ("a2s", h - 1, NCH - 1),
                          ("bcd", h - 2, NCH - 1) if h >= 2 else None)
                        I(s.activation(dtmp[0:1, 0:T], a2bs[(h - 1) % 2][0:1, 0:T],
                                       AF.Ln), "sS")
                        sself(s)
                        Ip(s.activation(dinv_bf[0:1, 0:T], dtmp[0:1, 0:T], AF.Exp,
                                        scale=-1.0), "sS", ("dinv", h - 1))
            W(s, ("a2s", H - 1, NCH - 1), ("bcd", H - 2, NCH - 1))
            I(s.activation(dtmp[0:1, 0:T], a2bs[(H - 1) % 2][0:1, 0:T], AF.Ln), "sS")
            sself(s)
            Ip(s.activation(dinv_bf[0:1, 0:T], dtmp[0:1, 0:T], AF.Exp, scale=-1.0),
               "sS", ("dinv", H - 1))

        es_attn.close()
        es3 = ExitStack()
        wo_bf = bw_bf
        gsl = sbw(es3, "gsl", [128, 2 * T], mybir.dt.float16)
        ost = sbw(es3, "ost", [128, 2 * T], F32)
        g_f = sbw(es3, "g_f", [128, T], F32)

        tb = cnt["sT"]
        vb = cnt["sV"]
        sb_ = cnt["sS"]
        lb = cnt["dL0"]
        gb = cnt["dL1"]
        gB = cnt["dG"]
        ob = {0: cnt["dO0"], 1: cnt["dO1"]}
        for mo in range(KT):
            EV[("woT", mo)] = ("sT", tb + mo + 1)
            EV[("silu", mo)] = ("sS", sb_ + mo + 1)
            EV[("stt", mo)] = ("sV", vb + 2 * mo + 1)
            EV[("res", mo)] = ("sV", vb + 2 * mo + 2)
            EV[("xq", mo)] = ("dL0", lb + 16 * (mo + 1))
            EV[("gld", mo)] = (("dL1", gb + 16 * (mo // 2 + 1)) if mo % 2 == 0
                               else ("dG", gB + 16 * (mo // 2 + 1)))
            EV[("out", mo)] = ("dO%d" % (mo % 2), ob[mo % 2] + 16 * (mo // 2 + 1))

        @block.gpsimd
        def _(g):
            for mo in range(KT):
                if mo >= 2:
                    W(g, ("woT", mo - 2))
                else:
                    W(g, ("fT", H - 2 + mo))
                for i in range(ng):
                    k0, k1 = i * gsz, (i + 1) * gsz
                    I(g.dma_start(
                        out=wo_bf[:, (mo % 2) * D + k0 * 128:(mo % 2) * D + k1 * 128]
                            .rearrange("p (ki c) -> p ki c", ki=gsz),
                        in_=dwo.ap()[k0 * 128:k1 * 128, mo * 128:(mo + 1) * 128]
                            .rearrange("(ki p) c -> p ki c", p=128)),
                      "dP%d" % (mo % 2), n=16)
                reg(("wL", mo), "dP%d" % (mo % 2))

        for mo in range(KT):
            @block.tensor
            def _(t, mo=mo):
                PW = PW4[mo % 4]
                W(t, ("wL", mo), ("stt", mo - 4))
                if mo == 0:
                    W(t, ("uptt", H - 1, NCH - 1))
                    kis = [KT - 1]
                elif mo == 1:
                    W(t, ("a2s", H - 1, NCH - 1), ("uptt", H - 1, NCH - 1))
                    kis = range(KT)
                elif mo == 2:
                    W(t, ("uptt", H - 1, NCH - 1), *pair_free[0])
                    kis = range(KT)
                elif mo == 3:
                    W(t, *pair_free[1])
                    kis = range(KT)
                else:
                    kis = range(KT)
                mm = None
                for ki in kis:
                    for c0, cw in CH:
                        mm = t.matmul(arena[0:128, PW + c0:PW + c0 + cw],
                                      wo_bf[:, (mo % 2) * D + ki * 128:(mo % 2) * D + (ki + 1) * 128],
                                      aoT[:, ki * T + c0:ki * T + c0 + cw],
                                      start=(ki == 0 and mo != 0),
                                      stop=(ki == KT - 1),
                                      skip_group_check=True)
                Ip(mm, "sT", ("woT", mo))

        @block.scalar
        def _(s):
            for mo in range(KT):
                sself(s)
                W(s, ("gld", mo), ("res", mo - 1))
                if mo == 0:
                    W(s, ("uptt", H - 1, NCH - 1))
                Ip(s.activation(g_f[:, :], gsl[:, (mo % 2) * T:(mo % 2 + 1) * T],
                                AF.Silu), "sS", ("silu", mo))

        @block.vector
        def _(v):
            for mo in range(KT):
                PW = PW4[mo % 4]
                vself(v)
                W(v, ("woT", mo), ("silu", mo), ("out", mo - 2))
                Ip(v.scalar_tensor_tensor(ost[:, (mo % 2) * T:(mo % 2 + 1) * T],
                                          arena[0:128, PW:PW + T], bo_f[:, mo:mo + 1],
                                          g_f[:, :], ALU.add, ALU.mult), "sV", ("stt", mo))
                vself(v)
                W(v, ("xq", mo))
                Ip(v.tensor_tensor(ost[:, (mo % 2) * T:(mo % 2 + 1) * T],
                                   ost[:, (mo % 2) * T:(mo % 2 + 1) * T],
                                   g_f[:, :], ALU.add), "sV", ("res", mo))

        @block.sync
        def _(sy):
            def gld(mo):
                if mo >= KT:
                    return
                if mo >= 2:
                    W(sy, ("silu", mo - 2), ("fst", mo))
                else:
                    W(sy, ("fst", mo), ("dinv", H - 1), ("a2", H - 1, SKT - 1),
                      ("fst", H - 1), ("fst", H - 2), ("uptt", H - 1, NCH - 1))
                sgl = "dL1" if mo % 2 == 0 else "dG"
                inst = sy.dma_start(out=gsl[:, (mo % 2) * T:(mo % 2 + 1) * T],
                                    in_=dgs.ap()[mo * 128:(mo + 1) * 128, :])
                cnt[sgl] += 16
                inst.then_inc(sems[sgl], 16)
                assert EV[("gld", mo)] == (sgl, cnt[sgl]), (mo, EV[("gld", mo)], cnt[sgl])

            gld(0)
            gld(1)
            for mo in range(KT):
                W(sy, ("stt", mo))
                inst = sy.dma_start(out=g_f[:, :], in_=dx.ap()[mo * 128:(mo + 1) * 128, :])
                cnt["dL0"] += 16
                inst.then_inc(sems["dL0"], 16)
                assert EV[("xq", mo)] == ("dL0", cnt["dL0"])
                W(sy, ("res", mo))
                inst = sy.dma_start(out=dout.ap()[mo * 128:(mo + 1) * 128, :],
                                    in_=ost[:, (mo % 2) * T:(mo % 2 + 1) * T])
                so = "dO%d" % (mo % 2)
                cnt[so] += 16
                inst.then_inc(sems[so], 16)
                assert EV[("out", mo)] == (so, cnt[so])
                gld(mo + 2)

        @block.sync
        def _(sy):
            for s in ("dY", "dW", "dF", "dG", "dH", "dO0", "dO1", "dL0", "dL1", "dP0", "dP1",
                      "dX0", "dX1", "dX2", "dX3"):
                if cnt[s] > 0:
                    sy.wait_ge(sems[s], cnt[s])

        es3.close()
        es_ao.close()
        es_mid.close()

    return nc



def _host_layouts(cfg, inputs):
    D, T, H, DH = cfg["D"], cfg["T"], cfg["H"], cfg["DH"]
    QL, KL, E, R = cfg["QL"], cfg["KL"], cfg["E"], cfg["R"]
    KT = D // 128
    ER = E * R
    N1 = min(128, ER)
    N2 = ER - N1
    f32 = np.float32
    C = lambda a: np.ascontiguousarray(a, dtype=f32)

    w = {}
    w["wq"] = C(inputs["L0_attn_Wq_d"])
    w["wkv"] = C(inputs["L0_attn_Wkv_d"])
    w["wqk"] = C(inputs["L0_attn_W_qk"])
    w["wvu"] = C(inputs["L0_attn_Wv_u"])
    w["bqd"] = C(inputs["L0_attn_bq_d"].reshape(QL, 1))
    w["bkvd"] = C(inputs["L0_attn_bkv_d"].reshape(KL, 1))
    w["bqk"] = C(np.asarray(inputs["L0_attn_b_qk"]).reshape(H, KL).T)
    w["bvuf"] = C(np.asarray(inputs["L0_attn_bv_u"]).reshape(1, D))
    w["bo"] = C(np.asarray(inputs["L0_attn_bo"]).reshape(KT, 128).T)
    w["wo"] = C(inputs["L0_attn_Wo"])
    w["bwt"] = C(np.asarray(inputs["L0_ffn_base_W"]).T)
    w["gwt"] = C(np.asarray(inputs["L0_ffn_gate_W"]).T)
    w["lat"] = C(np.asarray(inputs["L0_ffn_lora_A"]).reshape(ER, D).T)
    w["lbt"] = C(np.asarray(inputs["L0_ffn_lora_B"]).transpose(0, 2, 1).reshape(ER, D))
    e_full = np.zeros((E, ER), dtype=f32)
    for e in range(E):
        e_full[e, e * R:(e + 1) * R] = 1.0
    w["e1"] = C(e_full[:, 0:N1])
    w["e2"] = C(e_full[:, N1:ER])
    w["idm"] = np.eye(128, dtype=f32)
    return w


def kernel(**inputs):
    cfg = FULL_CFG
    D, T = cfg["D"], cfg["T"]
    x = np.asarray(inputs["L0_attn_x"], dtype=np.float32)
    B, S, _ = x.shape
    n_cores = 8
    w = _host_layouts(cfg, inputs)

    nc = build(cfg)
    in_maps = []
    for c in range(n_cores):
        b, q = c // 2, c % 2
        own = np.ascontiguousarray(x[b, q * T:(q + 1) * T, :].T)
        oth = np.ascontiguousarray(x[b, (1 - q) * T:(2 - q) * T, :].T)
        m = dict(w)
        m["xT_own"] = own
        m["xT_oth"] = oth
        in_maps.append(m)

    res = run_bass_kernel_spmd(nc, in_maps, core_ids=list(range(n_cores)))
    out = np.empty((B, S, D), dtype=np.float32)
    for c in range(n_cores):
        b, q = c // 2, c % 2
        out[b, q * T:(q + 1) * T, :] = res.results[c]["outT"].T
    return out
